# version 01
# baseline (speedup 1.0000x reference)
import numpy as np
from contextlib import ExitStack

import concourse.bass as bass
import concourse.mybir as mybir
from concourse.bass_utils import run_bass_kernel_spmd

F32 = mybir.dt.float32
BF16 = mybir.dt.bfloat16
AF = mybir.ActivationFunctionType
ALU = mybir.AluOpType
AX = mybir.AxisListType

L, D, NH, DK, DV, CHK, NE, HF = 8192, 1024, 4, 256, 256, 32, 8, 4096
T = 512
NT = L // T
TC = T // 128
PHASE = 9
SCANSUB = 9


def _f(evts):
    return [e for e in evts if e is not None]


class Walk:
    def __init__(self, b, eng_obj, eng_name):
        self.b = b
        self.eng = eng_obj
        self.name = eng_name
        self.cnt = {k: 0 for k in b.semnames}
        self.last_wait = {k: -1 for k in b.semnames}

    def _wait(self, evt):
        if evt is None:
            return
        sem, val = evt
        if self.last_wait[sem] >= val:
            return
        self.last_wait[sem] = val
        self.eng.wait_ge(self.b.sems[sem], val)

    def op(self, engine, fn, after=(), inc=True):
        evt = None
        if inc:
            self.cnt[engine] += 1
            evt = (engine, self.cnt[engine])
        if engine == self.name:
            for e in after:
                self._wait(e)
            inst = fn(self.eng)
            if inc:
                inst.then_inc(self.b.sems[engine], 1)
        return evt

    def dma(self, engine, sem, out, in_, after=()):
        self.cnt[sem] += 16
        evt = (sem, self.cnt[sem])
        if engine == self.name:
            for e in after:
                self._wait(e)
            self.eng.dma_start(out=out, in_=in_).then_inc(self.b.sems[sem], 16)
        return evt

    def wait(self, engine, evts):
        if engine == self.name:
            for e in evts:
                self._wait(e)


class Builder:
    semnames = ("tensor", "dve", "act", "pool", "sync", "dinit", "xin",
                "outa", "outb", "up0", "up1", "dn0", "dn1", "wq0", "wq1", "dinig")

    def __init__(self, nc):
        self.nc = nc
        self.stack = ExitStack()
        s = self.stack
        sb = lambda name, shape, dt=F32: s.enter_context(nc.sbuf_tensor(name, shape, dt))
        dram = lambda name, shape: nc.dram_tensor(name, shape, F32, kind="ExternalInput")
        self.x_h = dram("x", [L, D])
        self.y_h = nc.dram_tensor("y", [L, D], F32, kind="ExternalOutput")
        self.Wq_h, self.Wk_h, self.Wv_h = dram("Wq", [D, D]), dram("Wk", [D, D]), dram("Wv", [D, D])
        self.cq_h, self.ck_h, self.cv_h = dram("conv_q", [D, 4]), dram("conv_k", [D, 4]), dram("conv_v", [D, 4])
        self.Wb_h, self.Wg_h, self.Wh_h = dram("Wb", [D, 8]), dram("Wg", [D, 4]), dram("Wh", [D, 4])
        self.bg_h, self.bh_h = dram("bg", [4]), dram("bh", [4])
        self.normw_h, self.Wo_h = dram("norm_w", [256]), dram("Wo", [D, D])
        self.rwt_h = dram("router_w", [D, 8])
        self.up_h, self.dn_h = dram("up_w", [NE, D, HF]), dram("down_w", [NE, HF, D])
        self.idf = sb("idf", [128, 128])
        self.onesf = sb("onesf", [128, 128])
        self.maskSneg = sb("maskSneg", [128, 32])
        self.maskIncl = sb("maskIncl", [128, 32])
        self.i32 = sb("i32", [128, 32])
        self.c2 = sb("c2", [128, 4])
        self.bc2 = sb("bc2", [2, 256])
        self.convt = sb("convt", [128, 96])
        self.wcat = sb("wcat", [128, 192], BF16)
        self.bgt = sb("bgt", [128, 8])
        self.normw = sb("normw", [128, 2])
        self.ceps = sb("ceps", [128, 2])
        self.x_t = sb("x_t", [128, 4096])
        self.wqb = sb("wqb", [128, 2 * 4096], BF16)
        self.upb = sb("upb", [128, 2 * 2048], BF16)
        self.dnb = sb("dnb", [128, 2 * 2048], BF16)
        self.ln_s = sb("ln_s", [128, 20])
        self.xn_scr = sb("xn_scr", [128, 1024])
        self.sq_scr = sb("sq_scr", [128, 1024])
        self.xn_fm = sb("xn_fm", [128, 4096], BF16)
        self.gh = sb("gh", [128, 32 * 512], BF16)
        self.pre = self.gh
        self.g_acc = sb("g_acc", [128, 4096])
        self.gat = sb("gat", [128, 64])
        self.beta_tm = sb("beta_tm", [128, 32])
        self.w3 = sb("w3", [128, 48])
        self.w3n = sb("w3n", [128, 32])
        self.rsc = sb("rsc", [128, 80])
        self.srt = sb("srt", [128, 32])
        self.rw_tm = sb("rw_tm", [128, 32])
        self.tails = sb("tails", [128, 72], BF16)
        self.qbf = sb("qbf", [128, 2048], BF16)
        self.kbf = sb("kbf", [128, 2048], BF16)
        self.sf32 = sb("sf32", [128, 2048])
        self.k_tm = sb("k_tm", [128, 2048], BF16)
        self.v_tm = sb("v_tm", [128, 2048], BF16)
        self.nmn = sb("nmn", [128, 2 * 1024])
        self.nmB = sb("nmB", [128, 7 * 1024], BF16)
        self.nmA = sb("nmA", [128, 1024], BF16)
        self.nmR = sb("nmR", [128, 1024], BF16)
        self.nmT = sb("nmT", [128, 2 * 1024], BF16)
        self.wm_sb = sb("wm_sb", [128, 1024], BF16)
        self.u0s = sb("u0s", [128, 1024], BF16)
        self.u_sb = sb("u_sb", [128, 1024], BF16)
        self.usum = sb("usum", [128, 512], BF16)
        self.a_sb = sb("a_sb", [128, 256], BF16)
        self.S_bf = sb("S_bf", [128, 2048], BF16)
        self.S32 = sb("S32", [128, 4096])
        self.o_fm = sb("o_fm", [128, 4096], BF16)
        self.rmsv = sb("rmsv", [128, 32])
        self.PSA = s.enter_context(nc.psum_tensor("PSA", [128, 2048], F32))
        self.PSB = s.enter_context(nc.psum_tensor("PSB", [128, 2048], F32))
        self.sems = {k: s.enter_context(nc.semaphore("sem_" + k)) for k in self.semnames}

    def bank(self, i, lo=0, n=512):
        t = self.PSA if i < 4 else self.PSB
        return t[:, (i % 4) * 512 + lo:(i % 4) * 512 + lo + n]

    def walk(self, eng_obj, eng_name):
        P = Walk(self, eng_obj, eng_name)
        self.program(P)

    def program(self, P):
        st = {"bank": [None] * 8, "out": [], "gh_rd": None,
              "stage_rd": [None, None], "scan_done": None, "s32_ev": [None, None],
              "wq_free": [None, None], "up_free": [None, None],
              "dn_free": [None, None]}
        st["init"] = self.init_consts(P)
        ev_x = self.load_x(P, 0, after=st["init"])
        for t in range(NT):
            ev_ln = self.phase_ln(P, t, ev_x, st)
            ev_g = ev_ln
            if PHASE >= 2:
                ev_g = self.phase_ffn(P, t, ev_ln, st)
            if PHASE >= 3:
                self.phase_attn(P, t, ev_ln, st)
            if PHASE >= 9:
                ev_cmb = self.phase_combine(P, t, ev_g, st)
            else:
                dst = bass.AP(self.y_h, t * T * D, [[D, 128], [1, D]])
                ev_cmb = P.dma("sync", "outa", dst, self.x_t[:, 0:1024],
                               after=_f([ev_g, st.get("s_mm"), st["stage_rd"][0]]))
                st["out"].append(ev_cmb)
                st["stage_rd"][0] = ev_cmb
            if t + 1 < NT:
                ev_x = self.load_x(P, t + 1, after=[ev_cmb])
        P.wait("sync", st["out"][-2:])

    def load_x(self, P, t, after=()):
        xv = bass.AP(self.x_h, t * T * D, [[D, 128], [128 * D, TC], [1, D]])
        return P.dma("sync", "xin", self.x_t[:, :].rearrange("p (tc d) -> p tc d", tc=TC),
                     xv, after=_f(after))

    def init_consts(self, P):
        b = self
        ev = []
        ch = None
        chg = None
        e1 = P.op("pool", lambda g: g.memset(b.onesf[:, :], 1.0))
        ev.append(P.op("pool", lambda g: g.affine_select(
            b.idf[:, :], b.onesf[:, :], [[-1, 128]], ALU.is_equal, 0.0,
            base=0, channel_multiplier=1), after=[e1]))
        for tsr, fill, cmp_, cm, stp in [(b.maskSneg, -1.0, ALU.is_gt, 1, -1),
                                         (b.maskIncl, 1.0, ALU.is_ge, -1, 1),
                                         (b.i32, 1.0, ALU.is_equal, 1, -1)]:
            e2 = P.op("pool", lambda g, t_=tsr, f=fill: g.memset(t_[:, :], f))
            e3 = P.op("pool", lambda g, t_=tsr, c=cmp_, cm_=cm, s_=stp: g.affine_select(
                t_[0:32, :], t_[0:32, :], [[s_, 32]], c, 0.0, base=0,
                channel_multiplier=cm_), after=[e2])
            chg = P.dma("pool", "dinig", tsr[32:64, :], tsr[0:32, :], after=_f([e3, chg]))
            chg = P.dma("pool", "dinig", tsr[64:128, :], tsr[0:64, :], after=[chg])
        e4 = P.op("pool", lambda g: g.memset(b.c2[:, :], 0.0))
        e5 = P.op("pool", lambda g: g.memset(b.c2[:, 0:1], 1.0), after=[e4])
        ev.append(P.op("pool", lambda g: g.memset(b.c2[:, 3:4], 1.0), after=[e5]))
        e6 = P.op("pool", lambda g: g.memset(b.bc2[:, :], 0.0))
        e7 = P.op("pool", lambda g: g.memset(b.bc2[0:1, 0:128], 1.0), after=[e6])
        chg = P.dma("pool", "dinig", b.bc2[1:2, 128:256], b.bc2[0:1, 0:128],
                    after=_f([e7, chg]))
        ev.append(P.op("pool", lambda g: g.memset(b.tails[:, :], 0.0)))
        ev.append(P.op("pool", lambda g: g.memset(b.ceps[:, 0:1], 1e-5)))
        ev.append(P.op("pool", lambda g: g.memset(b.ceps[:, 1:2], 1e-12)))
        for i, h in enumerate([b.cq_h, b.ck_h, b.cv_h]):
            dst = bass.AP(b.convt, i * 32, [[96, 128], [4, 8], [1, 4]])
            ch = P.dma("sync", "dinit", dst,
                       h.ap().rearrange("(c p) k -> p c k", p=128), after=[ch])
        for hsrc, o0, wdt in [(b.Wb_h, 0, 8), (b.Wg_h, 8, 4), (b.Wh_h, 12, 4), (b.rwt_h, 16, 8)]:
            dst = bass.AP(b.wcat, o0, [[192, 128], [24, 8], [1, wdt]])
            chg = P.dma("pool", "dinig", dst,
                        hsrc.ap().rearrange("(c p) k -> p c k", p=128), after=[chg])
        ch = P.dma("sync", "dinit", bass.AP(b.bgt, 0, [[8, 1], [1, 4]]),
                   b.bg_h.ap()[None, :], after=[ch])
        ch = P.dma("sync", "dinit", bass.AP(b.bgt, 4, [[8, 1], [1, 4]]),
                   b.bh_h.ap()[None, :], after=[ch])
        nrep = 1
        while nrep < 128:
            hi = min(2 * nrep, 128)
            ch = P.dma("sync", "dinit", b.bgt[nrep:hi, :], b.bgt[0:hi - nrep, :],
                       after=[ch])
            nrep = hi
        for j in range(2):
            ch = P.dma("sync", "dinit", b.normw[:, j:j + 1],
                       b.normw_h.ap()[j * 128:(j + 1) * 128][:, None], after=[ch])
        ev.append(ch)
        ev.append(chg)
        return _f(ev)

    def phase_ln(self, P, t, ev_x, st):
        b = self
        last_evac = None
        for tc in range(TC):
            xc = b.x_t[:, tc * 1024:(tc + 1) * 1024]
            cS, cQ, cM, cR, cT = tc, 4 + tc, 8 + tc, 12 + tc, 16 + tc
            first = _f([st["stage_rd"][0], st["stage_rd"][1]]) if tc == 0 else []
            e_sum = P.op("dve", lambda v, xc=xc, cS=cS: v.reduce_sum(
                b.ln_s[:, cS:cS + 1], xc, AX.X), after=[ev_x] + first)
            e_sq = P.op("act", lambda a, xc=xc, cQ=cQ: a.activation(
                b.sq_scr[:, :], xc, AF.Square, accum_out=b.ln_s[:, cQ:cQ + 1]),
                after=[ev_x] + first)
            e_mu = P.op("dve", lambda v, cS=cS, cM=cM: v.tensor_scalar(
                b.ln_s[:, cM:cM + 1], b.ln_s[:, cS:cS + 1], 1.0 / D, None, ALU.mult),
                after=[e_sum])
            e_m2 = P.op("dve", lambda v, cM=cM, cT=cT: v.tensor_tensor(
                b.ln_s[:, cT:cT + 1], b.ln_s[:, cM:cM + 1], b.ln_s[:, cM:cM + 1],
                ALU.mult), after=[e_mu])
            e_qd = P.op("dve", lambda v, cQ=cQ, cR=cR: v.tensor_scalar(
                b.ln_s[:, cR:cR + 1], b.ln_s[:, cQ:cQ + 1], 1.0 / D, None, ALU.mult),
                after=[e_sq])
            e_var = P.op("dve", lambda v, cR=cR, cT=cT: v.tensor_sub(
                b.ln_s[:, cR:cR + 1], b.ln_s[:, cR:cR + 1], b.ln_s[:, cT:cT + 1]),
                after=[e_qd, e_m2])
            e_sd = P.op("act", lambda a, cR=cR: a.activation(
                b.ln_s[:, cR:cR + 1], b.ln_s[:, cR:cR + 1], AF.Sqrt, bias=b.ceps[:, 0:1],
                scale=1.0), after=[e_var])
            e_rs = P.op("dve", lambda v, cR=cR: v.reciprocal(
                b.ln_s[:, cR:cR + 1], b.ln_s[:, cR:cR + 1]), after=[e_sd])
            e_xn = P.op("dve", lambda v, xc=xc, cM=cM, cR=cR: v.tensor_scalar(
                b.xn_scr[:, :], xc, b.ln_s[:, cM:cM + 1], b.ln_s[:, cR:cR + 1],
                ALU.subtract, ALU.mult),
                after=[e_rs] + (_f([st["stage_rd"][0], st.get("xn_scr_rd")]) if tc == 0 else []))
            for half in range(2):
                bk = 4 + half
                mm = None
                for j in range(4):
                    dd = half * 4 + j
                    mm = P.op("tensor", lambda pe, dd=dd, bk=bk, j=j: pe.transpose(
                        self.bank(bk)[:, j * 128:(j + 1) * 128],
                        b.xn_scr[:, dd * 128:(dd + 1) * 128], b.idf[:, :]),
                        after=_f([e_xn, st["bank"][bk]] + st["init"]) if j == 0 else (),
                        inc=(j == 3))
                dst = bass.AP(b.xn_fm, half * 2048 + tc * 128,
                              [[4096, 128], [512, 4], [1, 128]])
                e_ev = P.op("dve", lambda v, bk=bk, dst=dst: v.tensor_copy(
                    dst, self.bank(bk)[:, 0:512].rearrange("p (c n) -> p c n", c=4)),
                    after=[mm])
                st["bank"][bk] = e_ev
                last_evac = e_ev
            st["xn_scr_rd"] = mm
        return last_evac

    def phase_ffn(self, P, t, ev_ln, st):
        b = self
        mmg = None
        for s in range(TC):
            for k in range(8):
                aft = _f([ev_ln, st["bank"][6]]) if (s == 0 and k == 0) else ()
                mmg = P.op("tensor", lambda pe, s=s, k=k: pe.matmul(
                    self.bank(6, s * 24, 24),
                    b.xn_fm[:, k * 512 + s * 128:k * 512 + s * 128 + 128],
                    b.wcat[:, k * 24:(k + 1) * 24], start=(k == 0), stop=(k == 7)),
                    after=aft, inc=(k == 7 and s == 3))
        e_bet = P.op("act", lambda a: a.activation(
            b.beta_tm[:, :].rearrange("p (s c) -> p s c", s=4),
            bass.AP(self.PSB, 2 * 512, [[2048, 128], [24, 4], [1, 8]]), AF.Sigmoid),
            after=[mmg])
        e_gat = P.op("dve", lambda v: v.tensor_copy(
            b.gat[:, :].rearrange("p (s c) -> p s c", s=4),
            bass.AP(self.PSB, 2 * 512 + 8, [[2048, 128], [24, 4], [1, 16]])),
            after=[mmg])
        st["bank"][6] = e_gat
        st["beta_ev"] = e_bet
        gv = bass.AP(b.gat, 0, [[64, 128], [16, 4], [1, 4]])
        hv = bass.AP(b.gat, 4, [[64, 128], [16, 4], [1, 4]])
        bgap = bass.AP(b.bgt, 0, [[8, 128], [0, 4], [1, 4]])
        bhap = bass.AP(b.bgt, 4, [[8, 128], [0, 4], [1, 4]])
        e1 = P.op("dve", lambda v: v.tensor_tensor(gv, gv, bgap, ALU.add), after=[e_gat])
        e2 = P.op("dve", lambda v: v.tensor_tensor(hv, hv, bhap, ALU.add), after=[e1])
        m_ = b.rsc[:, 0:16]
        m_v = m_.rearrange("p (s c) -> p s c", s=4)
        e3 = P.op("dve", lambda v: v.tensor_tensor(m_v, gv, hv, ALU.max), after=[e2])
        e4 = P.op("dve", lambda v: v.tensor_scalar_max(m_, m_, 0.0), after=[e3])
        eg_, ehg = b.rsc[:, 16:32], b.rsc[:, 32:48]
        e5 = P.op("dve", lambda v: v.tensor_sub(
            eg_.rearrange("p (s c) -> p s c", s=4), gv, m_v), after=[e4])
        e6 = P.op("dve", lambda v: v.tensor_sub(
            ehg.rearrange("p (s c) -> p s c", s=4), hv, m_v), after=[e5])
        wl, wg, wr = b.w3[:, 0:16], b.w3[:, 16:32], b.w3[:, 32:48]
        e7 = P.op("act", lambda a: a.activation(wl, eg_, AF.Exp), after=[e5])
        e8 = P.op("act", lambda a: a.activation(wg, ehg, AF.Exp), after=[e6])
        e9 = P.op("act", lambda a: a.activation(wr, m_, AF.Exp, scale=-1.0), after=[e6])
        den = b.rsc[:, 48:64]
        e10 = P.op("dve", lambda v: v.tensor_add(den, wl, wg), after=[e7, e8])
        e11 = P.op("dve", lambda v: v.tensor_add(den, den, wr), after=[e10, e9])
        e12 = P.op("dve", lambda v: v.reciprocal(den, den), after=[e11])
        e13 = P.op("dve", lambda v: v.tensor_mul(wl, wl, den), after=[e12])
        e14 = P.op("dve", lambda v: v.tensor_mul(wg, wg, den), after=[e13])
        e15 = P.op("dve", lambda v: v.tensor_mul(wr, wr, den), after=[e14])
        e16 = P.op("dve", lambda v: v.tensor_scalar(
            b.w3n[:, 0:16], wl, -1.0, None, ALU.mult), after=[e13])
        e17 = P.op("dve", lambda v: v.tensor_scalar(
            b.w3n[:, 16:32], wg, -1.0, None, ALU.mult), after=[e14])
        st["w3_ev"] = [e15, e16, e17]
        rv = bass.AP(b.gat, 8, [[64, 128], [16, 4], [1, 8]])
        m8, dn8, ts4, rd2 = (b.rsc[:, 64:68], b.rsc[:, 68:72],
                             b.rsc[:, 72:76], b.rsc[:, 76:80])
        r1 = P.op("dve", lambda v: v.reduce_max(m8, rv, AX.X), after=[e_gat])
        subs = []
        for s in range(TC):
            subs.append(P.op("dve", lambda v, s=s: v.tensor_scalar(
                b.sq_scr[:, s * 8:(s + 1) * 8], b.gat[:, s * 16 + 8:s * 16 + 16],
                m8[:, s:s + 1], None, ALU.subtract), after=[r1]))
        r2 = P.op("act", lambda a: a.activation(
            b.rw_tm[:, :], b.sq_scr[:, 0:32], AF.Exp), after=subs)
        r3 = P.op("dve", lambda v: v.reduce_sum(
            dn8, b.rw_tm[:, :].rearrange("p (s c) -> p s c", s=4), AX.X), after=[r2])
        r4 = P.op("dve", lambda v: v.reciprocal(dn8, dn8), after=[r3])
        dnap = bass.AP(b.rsc, 68, [[80, 128], [1, 4], [0, 8]])
        r5 = P.op("dve", lambda v: v.tensor_tensor(
            b.rw_tm[:, :].rearrange("p (s c) -> p s c", s=4),
            b.rw_tm[:, :].rearrange("p (s c) -> p s c", s=4), dnap, ALU.mult),
            after=[r4])
        srts = []
        for s in range(TC):
            srts.append(P.op("dve", lambda v, s=s: v.max(
                b.srt[:, s * 8:(s + 1) * 8], b.rw_tm[:, s * 8:(s + 1) * 8]),
                after=[r5]))
        r6 = P.op("dve", lambda v: v.reduce_sum(
            ts4, bass.AP(b.srt, 0, [[32, 128], [8, 4], [1, 4]]), AX.X), after=srts)
        r7 = P.op("dve", lambda v: v.reciprocal(rd2, ts4), after=[r6])
        keeps = []
        for s in range(TC):
            keeps.append(P.op("dve", lambda v, s=s: v.tensor_scalar(
                b.sq_scr[:, s * 8:(s + 1) * 8], b.rw_tm[:, s * 8:(s + 1) * 8],
                b.srt[:, s * 8 + 3:s * 8 + 4], None, ALU.is_ge), after=[r7]))
        r8 = P.op("dve", lambda v: v.tensor_mul(
            b.rw_tm[:, :], b.rw_tm[:, :], b.sq_scr[:, 0:32]), after=keeps)
        rd2ap = bass.AP(b.rsc, 76, [[80, 128], [1, 4], [0, 8]])
        e_rw = P.op("dve", lambda v: v.tensor_tensor(
            b.rw_tm[:, :].rearrange("p (s c) -> p s c", s=4),
            b.rw_tm[:, :].rearrange("p (s c) -> p s c", s=4), rd2ap, ALU.mult),
            after=[r8])
        st["rw_ev"] = e_rw
        return self.experts(P, t, ev_ln, e_rw, st)

    def experts(self, P, t, ev_ln, e_rw, st):
        b = self
        upv = lambda par: b.upb[:, par * 2048:(par + 1) * 2048]
        dnv = lambda par: b.dnb[:, par * 2048:(par + 1) * 2048]
        e_up, e_dn = {}, {}

        def fetch_up(e, he):
            par = (e * 16 + he) % 2
            src = bass.AP(b.up_h, e * D * HF + he * 256,
                          [[HF, 128], [128 * HF, 8], [1, 256]])
            e_up[(e, he)] = P.dma("pool", "up%d" % par,
                                  upv(par).rearrange("p (c n) -> p c n", c=8),
                                  src, after=_f([st["up_free"][par]]))

        def fetch_dn(e, ke):
            par = (e * 16 + ke) % 2
            src = bass.AP(b.dn_h, e * HF * D + ke * 256 * D,
                          [[D, 128], [128 * D, 2], [1, 1024]])
            e_dn[(e, ke)] = P.dma("pool", "dn%d" % par,
                                  dnv(par).rearrange("p (c n) -> p c n", c=2),
                                  src, after=_f([st["dn_free"][par]]))

        fetch_up(0, 0)
        fetch_up(0, 1)
        e_gacc = None
        for e in range(NE):
            gelu_ev = {}
            for he in range(16):
                par = (e * 16 + he) % 2
                for kk in range(2):
                    k = he * 2 + kk
                    bk = 4 + (k % 2)
                    mm = None
                    for dd in range(8):
                        aft = _f([e_up[(e, he)], st["bank"][bk], ev_ln]) if dd == 0 else ()
                        mm = P.op("tensor", lambda pe, dd=dd, bk=bk, par=par, kk=kk: pe.matmul(
                            self.bank(bk),
                            upv(par)[:, dd * 256 + kk * 128:dd * 256 + kk * 128 + 128],
                            b.xn_fm[:, dd * 512:(dd + 1) * 512],
                            start=(dd == 0), stop=(dd == 7)),
                            after=aft, inc=(dd == 7))
                    if kk == 1:
                        st["up_free"][par] = mm
                    gaft = [mm] + ([st["gh_rd"], st.get("tail_ev")] if k == 0 else [])
                    gelu_ev[k] = P.op("act", lambda a, k=k, bk=bk: a.activation(
                        b.gh[:, k * 512:(k + 1) * 512], self.bank(bk),
                        AF.Gelu_apprx_tanh), after=_f(gaft))
                    st["bank"][bk] = gelu_ev[k]
                if he + 2 < 16:
                    fetch_up(e, he + 2)
                elif e + 1 < NE:
                    fetch_up(e + 1, he - 14)
                if he == 15:
                    fetch_dn(e, 0)
                    fetch_dn(e, 1)
            mm = None
            for ke in range(16):
                par = (e * 16 + ke) % 2
                for kk in range(2):
                    k = ke * 2 + kk
                    for tc in range(TC):
                        for half in range(2):
                            bki = tc * 2 + half
                            aft = []
                            if tc == 0 and half == 0:
                                aft = [e_dn[(e, ke)], gelu_ev[k]]
                            if k == 0:
                                aft.append(st["bank"][bki])
                            mm = P.op("tensor",
                                      lambda pe, k=k, tc=tc, half=half, par=par, kk=kk, bki=bki:
                                      pe.matmul(
                                          self.bank(bki),
                                          b.gh[:, k * 512 + tc * 128:k * 512 + tc * 128 + 128],
                                          dnv(par)[:, kk * 1024 + half * 512:kk * 1024 + half * 512 + 512],
                                          start=(k == 0), stop=(k == 31)),
                                      after=_f(aft), inc=(kk == 1 and tc == 3 and half == 1))
                    if kk == 1:
                        st["dn_free"][par] = mm
                if ke + 2 < 16:
                    fetch_dn(e, ke + 2)
            st["gh_rd"] = mm
            for tc in range(TC):
                for half in range(2):
                    bki = tc * 2 + half
                    dst = b.g_acc[:, tc * 1024 + half * 512:tc * 1024 + half * 512 + 512]
                    rwc = b.rw_tm[:, tc * 8 + e:tc * 8 + e + 1]
                    if e == 0:
                        ev2 = P.op("dve", lambda v, dst=dst, rwc=rwc, bki=bki: v.tensor_scalar(
                            dst, self.bank(bki), rwc, None, ALU.mult),
                            after=_f([mm, e_rw]))
                    else:
                        ev2 = P.op("dve", lambda v, dst=dst, rwc=rwc, bki=bki:
                                   v.scalar_tensor_tensor(
                                       dst, self.bank(bki), rwc, dst, ALU.mult, ALU.add),
                                   after=[mm])
                    st["bank"][bki] = ev2
                    e_gacc = ev2
        return e_gacc

    def phase_attn(self, P, t, ev_ln, st):
        for hp in range(2):
            self.proj_phase(P, t, hp, ev_ln, st)
            if PHASE >= 4:
                self.scan_phase(P, t, hp, st)
        if PHASE >= 5:
            self.wo_phase(P, t, st)

    def proj_phase(self, P, t, hp, ev_ln, st):
        b = self
        srcs = [b.Wq_h, b.Wk_h, b.Wv_h]
        wq_ev = {}

        def fetch_w(slot, h_src, col0):
            par = slot % 2
            dst = b.wqb[:, par * 4096:(par + 1) * 4096].rearrange("p (c n) -> p c n", c=8)
            src = bass.AP(h_src, col0, [[D, 128], [128 * D, 8], [1, 512]])
            wq_ev[slot] = P.dma("pool", "wq%d" % par, dst, src,
                                after=_f([st["wq_free"][par]]))

        base = hp * 3
        if hp == 0:
            fetch_w(0, srcs[0], 0)
        else:
            wq_ev[base] = st.pop("wq_next")
        for pj in range(3):
            slot = base + pj
            par = slot % 2
            if pj + 1 < 3:
                fetch_w(slot + 1, srcs[pj + 1], hp * 512)
            elif hp == 0:
                fetch_w(slot + 1, srcs[0], 512)
                st["wq_next"] = wq_ev[slot + 1]
            e_pre = {}
            for m in range(4):
                bk = 4 + (m % 2)
                mm = None
                for dd in range(8):
                    aft = _f([wq_ev[slot], st["bank"][bk], ev_ln]) if dd == 0 else ()
                    mm = P.op("tensor", lambda pe, m=m, dd=dd, bk=bk, par=par: pe.matmul(
                        self.bank(bk),
                        b.wqb[:, par * 4096 + dd * 512 + m * 128:par * 4096 + dd * 512 + m * 128 + 128],
                        b.xn_fm[:, dd * 512:(dd + 1) * 512],
                        start=(dd == 0), stop=(dd == 7)), after=aft, inc=(dd == 7))
                if m == 3:
                    st["wq_free"][par] = mm
                pcol = (pj * 4 + m) * 515
                aft2 = [mm]
                if pj == 0 and m == 0:
                    aft2 += _f([st["scan_done"], st.get("rw_ev"), st.get("gh_rd")])
                e_pre[m] = P.op("act", lambda a, pcol=pcol, bk=bk: a.activation(
                    b.pre[:, pcol + 3:pcol + 515], self.bank(bk), AF.Copy),
                    after=_f(aft2))
                st["bank"][bk] = e_pre[m]
            e_sil = {}
            for m in range(4):
                mg = hp * 4 + m
                pcol = (pj * 4 + m) * 515
                tcol = (pj * 8 + mg) * 3
                acc = b.sf32[:, 0:512]
                tmp = b.sf32[:, 512:1024]
                tapc = lambda j, pj=pj, mg=mg: b.convt[:, pj * 32 + mg * 4 + j:pj * 32 + mg * 4 + j + 1]
                e_ti = P.op("pool", lambda g, pcol=pcol, tcol=tcol: g.tensor_copy(
                    b.pre[:, pcol:pcol + 3], b.tails[:, tcol:tcol + 3]),
                    after=_f([e_pre[m], st.get("tail_ev")] + st["init"]))
                e_c = P.op("pool", lambda g, pcol=pcol, tapc=tapc, acc=acc: g.tensor_scalar(
                    acc, b.pre[:, pcol + 3:pcol + 515], tapc(3), None, ALU.mult),
                    after=_f([e_ti, st.get("acc_rd")]))
                for j in (2, 1, 0):
                    e_tj = P.op("pool", lambda g, pcol=pcol, j=j, tapc=tapc, tmp=tmp:
                                g.tensor_scalar(tmp, b.pre[:, pcol + j:pcol + j + 512],
                                                tapc(j), None, ALU.mult), after=[e_c])
                    e_c = P.op("pool", lambda g, acc=acc, tmp=tmp: g.tensor_tensor(
                        acc, acc, tmp, ALU.add), after=[e_tj])
                st["tail_ev"] = P.op("pool", lambda g, pcol=pcol, tcol=tcol: g.tensor_copy(
                    b.tails[:, tcol:tcol + 3], b.pre[:, pcol + 512:pcol + 515]),
                    after=[e_c])
                if pj < 2:
                    dstb = [b.qbf, b.kbf][pj]
                    e_sil[m] = P.op("act", lambda a, m=m, dstb=dstb, acc=acc: a.activation(
                        dstb[:, m * 512:(m + 1) * 512], acc, AF.Silu),
                        after=_f([e_c, st.get("qk_rd")]))
                    st["acc_rd"] = e_sil[m]
                else:
                    e_sv = P.op("act", lambda a, acc=acc: a.activation(
                        b.sf32[:, 1024:1536], acc, AF.Silu),
                        after=_f([e_c, st.get("sl2_rd")]))
                    st["acc_rd"] = e_sv
                    self.kv_transpose(P, st, b.sf32[:, 1024:1536], True, m // 2, m % 2, e_sv)
            if pj < 2:
                self.l2_phase(P, hp, pj, e_sil, st)

    def l2_phase(self, P, hp, pj, e_sil, st):
        b = self
        dstb = [b.qbf, b.kbf][pj]
        rs = self.PSB[0:2, 3 * 512:4 * 512]
        mm = None
        for m in range(4):
            e_q = P.op("act", lambda a, m=m, dstb=dstb: a.activation(
                b.sq_scr[:, 0:512], dstb[:, m * 512:(m + 1) * 512], AF.Square),
                after=_f([e_sil[m]] + ([mm] if mm else []) +
                         ([st.get("rw_ev")] if m == 0 else [])))
            aft = _f([e_q] + ([st["bank"][7]] if m == 0 else []))
            mm = P.op("tensor", lambda pe, m=m: pe.matmul(
                rs, b.c2[:, (m // 2) * 2:(m // 2) * 2 + 2], b.sq_scr[:, 0:512],
                start=(m == 0), stop=(m == 3)), after=aft, inc=True)
        rn = b.sf32[0:2, 1536:2048]
        e_rt = P.op("act", lambda a, rn=rn: a.activation(
            rn, rs, AF.Sqrt, bias=b.ceps[0:2, 1:2], scale=1.0),
            after=_f([mm, st.get("sl3_rd")]))
        e_rc = P.op("dve", lambda v, rn=rn: v.reciprocal(rn, rn), after=[e_rt])
        st["bank"][7] = e_rc
        mmb_last = None
        for h in range(2):
            bcast = self.bank(6)
            mmb = P.op("tensor", lambda pe, h=h, rn=rn: pe.matmul(
                bcast, b.bc2[:, h * 128:(h + 1) * 128], rn, start=True, stop=True),
                after=_f([e_rc, st["bank"][6]]), inc=True)
            mmb_last = mmb
            last = None
            for m in (2 * h, 2 * h + 1):
                e_n = P.op("dve", lambda v, m=m, dstb=dstb, bcast=bcast: v.tensor_tensor(
                    dstb[:, m * 512:(m + 1) * 512], dstb[:, m * 512:(m + 1) * 512],
                    bcast, ALU.mult), after=[mmb])
                last = e_n
                if pj == 1:
                    e_fc = P.op("dve", lambda v, m=m, dstb=dstb: v.tensor_copy(
                        b.sf32[:, 1024:1536], dstb[:, m * 512:(m + 1) * 512]),
                        after=_f([e_n, st.get("sl2_rd")]))
                    self.kv_transpose(P, st, b.sf32[:, 1024:1536], False, h, m % 2, e_fc)
            st["bank"][6] = last
        st["sl3_rd"] = mmb_last
        st["qk_rd"] = None
        if pj == 0:
            st["qbf_ev"] = last
        else:
            st["kbf_ev"] = last

    def kv_transpose(self, P, st, src, is_v, h, vh, after):
        b = self
        bk = 7
        mm = None
        for ts_ in range(4):
            mm = P.op("tensor", lambda pe, ts_=ts_, bk=bk, src=src: pe.transpose(
                self.bank(bk)[:, ts_ * 128:(ts_ + 1) * 128],
                src[:, ts_ * 128:(ts_ + 1) * 128], b.idf[:, :]),
                after=_f([after, st["bank"][bk]]) if ts_ == 0 else (), inc=(ts_ == 3))
        st["sl2_rd"] = mm
        dstt = b.v_tm if is_v else b.k_tm
        dst = bass.AP(dstt, h * 256 + vh * 128, [[2048, 128], [512, 4], [1, 128]])
        e_ev = P.op("dve", lambda v, dst=dst, bk=bk: v.tensor_copy(
            dst, self.bank(bk)[:, 0:512].rearrange("p (c n) -> p c n", c=4)),
            after=[mm])
        st["bank"][bk] = e_ev
        st["tm_ev"] = e_ev

    def _ap32(self, tsr, freesz, base, g, col, n=32):
        return bass.AP(tsr, 32 * g * freesz + base + col, [[freesz, 32], [1, n]])

    def scan_phase(self, P, t, hp, st):
        b = self
        if t > 0:
            seed = None
            for r in range(4):
                aft = _f([st["bank"][0], st["bank"][1], st["bank"][2], st["bank"][3],
                          st["s32_ev"][hp]]) if r == 0 else ()
                seed = P.op("tensor", lambda pe, r=r: pe.matmul(
                    self.PSA[:, r * 512:(r + 1) * 512], b.idf[:, :],
                    b.S32[:, hp * 2048 + r * 512:hp * 2048 + (r + 1) * 512],
                    start=True, stop=False, skip_group_check=True),
                    after=aft, inc=(r == 3))
            e_supd = seed
        else:
            e_supd = None
        kq = _f([st.get("kbf_ev"), st.get("qbf_ev"), st.get("tm_ev")])
        e_z = P.op("dve", lambda v: v.memset(self.PSB[:, 0:1024], 0.0),
                   after=_f([st["bank"][4], st["bank"][5]]))
        st["bank"][4] = e_z
        st["bank"][5] = e_z
        mmA = None
        for h in range(2):
            for s in range(TC):
                for g in range(4):
                    c = s * 4 + g
                    for kc in range(2):
                        aft = _f([st["bank"][4], st["bank"][5]] + kq) \
                            if (h == 0 and s == 0 and g == 0 and kc == 0) else ()
                        col = (h * 4 + s) * 128 + g * 32
                        mmA = P.op("tensor", lambda pe, h=h, c=c, g=g, kc=kc, col=col: pe.matmul(
                            self.PSB[32 * g:32 * g + 32, col:col + 32],
                            b.kbf[:, (2 * h + kc) * 512 + c * 32:(2 * h + kc) * 512 + c * 32 + 32],
                            b.kbf[:, (2 * h + kc) * 512 + c * 32:(2 * h + kc) * 512 + c * 32 + 32],
                            start=(kc == 0), stop=(kc == 1), tile_position=(0, 32 * g)),
                            after=aft, inc=(kc == 1))
        mmS = None
        for h in range(2):
            for s in range(TC):
                for g in range(4):
                    c = s * 4 + g
                    for kc in range(2):
                        aft = _f([st["bank"][6]]) if (h == 0 and s == 0 and g == 0 and kc == 0) else ()
                        col = 2 * 512 + (h * 4 + s) * 32
                        mmS = P.op("tensor", lambda pe, h=h, c=c, g=g, kc=kc, col=col: pe.matmul(
                            self.PSB[32 * g:32 * g + 32, col:col + 32],
                            b.kbf[:, (2 * h + kc) * 512 + c * 32:(2 * h + kc) * 512 + c * 32 + 32],
                            b.qbf[:, (2 * h + kc) * 512 + c * 32:(2 * h + kc) * 512 + c * 32 + 32],
                            start=(kc == 0), stop=(kc == 1), tile_position=(0, 32 * g)),
                            after=aft, inc=(kc == 1))
        if SCANSUB < 2:
            st["scan_done"] = mmS
            st["nmT_rd"] = mmS
            st["nmn_rd"] = mmS
            st["s32_ev"][hp] = mmS
            return
        e_a = P.op("dve", lambda v: v.tensor_tensor(
            b.a_sb[:, :].rearrange("p (q j) -> p q j", q=8),
            bass.AP(self.PSB, 2 * 512, [[2048, 128], [32, 8], [1, 32]]),
            bass.AP(b.maskIncl, 0, [[32, 128], [0, 8], [1, 32]]), ALU.mult),
            after=[mmS])
        st["bank"][6] = e_a
        e_As = {}
        e_msk = None
        for scan in range(2):
            e_msk = P.op("dve", lambda v, scan=scan: v.tensor_tensor(
                bass.AP(b.nmn, scan * 1024, [[2048, 128], [128, 8], [32, 4], [1, 32]]),
                bass.AP(self.PSB, 0, [[2048, 128], [128, 8], [32, 4], [1, 32]]),
                bass.AP(b.maskSneg, 0, [[32, 128], [0, 8], [0, 4], [1, 32]]), ALU.mult),
                after=_f([mmA] + ([st.get("nmn_rd")] if scan == 0 else [])))
            bap = bass.AP(b.beta_tm, scan * 4 + 2 * hp,
                          [[32, 128], [1, 2], [8, 4], [0, 128]])
            e_As[scan] = P.op("dve", lambda v, scan=scan, bap=bap: v.tensor_tensor(
                bass.AP(b.nmn, scan * 1024, [[2048, 128], [512, 2], [128, 4], [1, 128]]),
                bass.AP(b.nmn, scan * 1024, [[2048, 128], [512, 2], [128, 4], [1, 128]]),
                bap, ALU.mult),
                after=_f([e_msk, st.get("beta_ev")]))
        st["bank"][4] = e_As[1]
        st["bank"][5] = e_As[1]
        if SCANSUB < 3:
            st["scan_done"] = e_As[1]
            st["nmT_rd"] = e_As[1]
            st["nmn_rd"] = e_As[1]
            st["s32_ev"][hp] = e_As[1]
            return
        for scan in range(2):
            e_abf = P.op("dve", lambda v, scan=scan: v.tensor_copy(
                b.nmB[:, 0:1024], b.nmn[:, scan * 1024:(scan + 1) * 1024]),
                after=_f([e_As[scan], st.get("nmB_rd")]))
            mm = None
            for blk in range(8):
                bk = 4 + blk // 4
                mm = P.op("tensor", lambda pe, blk=blk, bk=bk, scan=scan: pe.transpose(
                    self.bank(bk)[:, (blk % 4) * 128:(blk % 4) * 128 + 128],
                    b.nmn[:, scan * 1024 + blk * 128:scan * 1024 + (blk + 1) * 128],
                    b.idf[:, :]),
                    after=_f([e_As[scan], st["bank"][4], st["bank"][5]]) if blk == 0 else (),
                    inc=(blk == 7))
            e_Ap = P.op("dve", lambda v: v.tensor_copy(
                b.nmA[:, :].rearrange("p (u n) -> p u n", u=2),
                bass.AP(self.PSB, 0, [[2048, 128], [512, 2], [1, 512]])),
                after=_f([mm, st.get("nmA_rd")]))
            st["bank"][4] = e_Ap
            st["bank"][5] = e_Ap
            if SCANSUB < 4:
                st["scan_done"] = e_Ap
                st["nmT_rd"] = e_Ap
                st["nmn_rd"] = e_Ap
                st["s32_ev"][hp] = e_Ap
                st["ttil_%d" % scan] = e_Ap
                continue
            def bap32(slot, g, col, n=32):
                return self._ap32(b.nmB, 7168, slot * 1024, g, col, n)

            def aap32(g, col, n=32):
                return self._ap32(b.nmA, 1024, 0, g, col, n)

            chain = [(1, "A'", 0), (5, 0, "A'"), (2, 5, 1), (6, 1, 5),
                     (3, 6, 2), (5, 2, 6), (4, 5, 3)]
            prev_ev = [e_abf, e_Ap]
            for (dsts, lt, rh) in chain:
                mm = None
                for blk in range(8):
                    def sl_of(x, blk=0):
                        t_ = b.nmA if x == "A'" else None
                        return None
                    lt_ap = (b.nmA if lt == "A'" else
                             b.nmB[:, lt * 1024:(lt + 1) * 1024])
                    rh_ap = (b.nmA if rh == "A'" else
                             b.nmB[:, rh * 1024:(rh + 1) * 1024])
                    mm = P.op("tensor",
                              lambda pe, blk=blk, lt_ap=lt_ap, rh_ap=rh_ap: pe.matmul(
                                  self.PSB[:, blk * 128:(blk + 1) * 128],
                                  lt_ap[:, blk * 128:(blk + 1) * 128],
                                  rh_ap[:, blk * 128:(blk + 1) * 128],
                                  start=True, stop=True),
                              after=_f(prev_ev + [st["bank"][4], st["bank"][5]])
                              if blk == 0 else (),
                              inc=(blk == 7))
                e_ev = P.op("dve", lambda v, dsts=dsts: v.tensor_copy(
                    b.nmB[:, dsts * 1024:(dsts + 1) * 1024].rearrange("p (u n) -> p u n", u=2),
                    bass.AP(self.PSB, 0, [[2048, 128], [512, 2], [1, 512]])),
                    after=[mm])
                st["bank"][4] = e_ev
                st["bank"][5] = e_ev
                prev_ev = [e_ev]
            if SCANSUB < 5:
                st["scan_done"] = prev_ev[-1]
                st["nmT_rd"] = prev_ev[-1]
                st["nmn_rd"] = prev_ev[-1]
                st["s32_ev"][hp] = prev_ev[-1]
                st["ttil_%d" % scan] = prev_ev[-1]
                continue
            e_r = P.op("dve", lambda v: v.tensor_add(
                bass.AP(b.nmR, 0, [[1024, 128], [128, 8], [1, 128]]),
                bass.AP(b.nmA, 0, [[1024, 128], [128, 8], [1, 128]]),
                bass.AP(b.idf, 0, [[128, 128], [0, 8], [1, 128]])),
                after=_f([e_Ap, st.get("nmR_rd")]))
            first_r = True
            for bslot in (1, 2, 3, 4):
                mm = None
                for blk in range(8):
                    aft = ()
                    if blk == 0:
                        aft = _f([e_r] + ([st["bank"][6], st["bank"][7]]
                                          if first_r else []))
                    mm = P.op("tensor",
                              lambda pe, blk=blk, bslot=bslot: pe.matmul(
                                  self.PSB[:, 1024 + blk * 128:1024 + (blk + 1) * 128],
                                  b.nmB[:, bslot * 1024 + blk * 128:bslot * 1024 + (blk + 1) * 128],
                                  b.nmR[:, blk * 128:(blk + 1) * 128],
                                  start=True, stop=True),
                              after=aft, inc=(blk == 7))
                first_r = False
                e_r = P.op("dve", lambda v: v.tensor_add(
                    b.nmR[:, :], b.nmR[:, :], self.PSB[:, 1024:2048]), after=[mm])
            st["bank"][6] = e_r
            st["bank"][7] = e_r
            bap = bass.AP(b.beta_tm, scan * 4 + 2 * hp,
                          [[32, 128], [1, 2], [8, 4], [0, 128]])
            e_tt = P.op("dve", lambda v, scan=scan, bap=bap: v.tensor_tensor(
                bass.AP(b.nmT, scan * 1024, [[2048, 128], [512, 2], [128, 4], [1, 128]]),
                bass.AP(b.nmR, 0, [[1024, 128], [128, 8], [32, 4], [1, 128]])
                if False else
                bass.AP(b.nmR, 0, [[1024, 128], [512, 2], [128, 4], [1, 128]]),
                bap, ALU.mult),
                after=_f([e_r, st.get("nmT_rd")]))
            st["nmR_rd"] = e_tt
            st["nmA_rd"] = e_r
            st["nmB_rd"] = e_r
            st["ttil_%d" % scan] = e_tt
        if SCANSUB < 6:
            ev = st["ttil_1"]
            st["scan_done"] = ev
            st["nmT_rd"] = ev
            st["nmn_rd"] = ev
            st["s32_ev"][hp] = ev
            return
        e_prev_chunk = None
        for s in range(TC):
            ev_u0, ev_wm = {}, {}
            for scan in range(2):
                bk = 6 + scan
                mm = None
                for h in range(2):
                    mm = P.op("tensor", lambda pe, h=h, scan=scan, bk=bk, s=s: pe.matmul(
                        self.bank(bk)[:, h * 256:h * 256 + 256],
                        b.nmT[:, scan * 1024 + (h * 4 + s) * 128:
                              scan * 1024 + (h * 4 + s) * 128 + 128],
                        b.v_tm[:, (s * 2 + h) * 256:(s * 2 + h) * 256 + 256],
                        start=True, stop=True),
                        after=_f([st["ttil_%d" % scan], st["bank"][bk]])
                        if h == 0 else (), inc=(h == 1))
                wap = bass.AP(b.w3, scan * 16 + s * 4 + 2 * hp,
                              [[48, 128], [1, 2], [0, 256]])
                ev_u0[scan] = P.op("dve", lambda v, scan=scan, bk=bk, wap=wap: v.tensor_tensor(
                    b.u0s[:, scan * 512:(scan + 1) * 512].rearrange("p (h n) -> p h n", h=2),
                    self.bank(bk).rearrange("p (h n) -> p h n", h=2), wap, ALU.mult),
                    after=_f([mm] + st.get("w3_ev", [])))
                st["bank"][bk] = ev_u0[scan]
                mm2 = None
                for h in range(2):
                    for kc in range(2):
                        mm2 = P.op("tensor",
                                   lambda pe, scan=scan, bk=bk, s=s, h=h, kc=kc: pe.matmul(
                                       self.bank(bk)[:, (h * 2 + kc) * 128:(h * 2 + kc) * 128 + 128],
                                       b.k_tm[:, (s * 2 + h) * 256 + kc * 128:
                                              (s * 2 + h) * 256 + kc * 128 + 128],
                                       b.nmT[:, scan * 1024 + (h * 4 + s) * 128:
                                             scan * 1024 + (h * 4 + s) * 128 + 128],
                                       start=True, stop=True),
                                   after=_f([ev_u0[scan]]) if (h == 0 and kc == 0) else (),
                                   inc=(h == 1 and kc == 1))
                ev_wm[scan] = P.op("dve", lambda v, scan=scan, bk=bk: v.tensor_copy(
                    b.wm_sb[:, scan * 512:(scan + 1) * 512], self.bank(bk)), after=[mm2])
                st["bank"][bk] = ev_wm[scan]
            if SCANSUB < 7:
                ev = ev_wm[1]
                st["scan_done"] = ev
                st["nmT_rd"] = ev
                st["nmn_rd"] = ev
                st["s32_ev"][hp] = ev
                continue
            for g in range(4):
                c = s * 4 + g
                if not (t == 0 and c == 0):
                    e_snap = P.op("act", lambda a: a.activation(
                        b.S_bf[:, :], self.PSA[:, :], AF.Copy),
                        after=_f([e_supd, e_prev_chunk]))
                else:
                    e_snap = None
                e_wmS = None
                if e_snap is not None:
                    mm = None
                    for h in range(2):
                        for scan in range(2):
                            for kc in range(2):
                                col4 = (h * 2 + scan) * 256
                                mm = P.op("tensor",
                                          lambda pe, h=h, scan=scan, kc=kc, g=g, col4=col4, s=s:
                                          pe.matmul(
                                              self.PSB[32 * g:32 * g + 32, col4:col4 + 256],
                                              b.wm_sb[:, scan * 512 + (h * 2 + kc) * 128 + g * 32:
                                                      scan * 512 + (h * 2 + kc) * 128 + g * 32 + 32],
                                              b.S_bf[:, (h * 2 + scan) * 512 + kc * 256:
                                                     (h * 2 + scan) * 512 + kc * 256 + 256],
                                              start=(kc == 0), stop=(kc == 1),
                                              tile_position=(0, 32 * g)),
                                          after=_f([e_snap, ev_wm[0], ev_wm[1],
                                                    st["bank"][4], st["bank"][5]])
                                          if (h == 0 and scan == 0 and kc == 0) else (),
                                          inc=(h == 1 and scan == 1 and kc == 1))
                    e_wmS = mm
                e_us = []
                for h in range(2):
                    for scan in range(2):
                        dst = self._ap32(b.u_sb, 1024, (h * 2 + scan) * 256, g, 0, 256)
                        src0 = self._ap32(b.u0s, 1024, scan * 512 + h * 256, g, 0, 256)
                        if e_wmS is None:
                            e_us.append(P.op("dve", lambda v, dst=dst, src0=src0: v.tensor_copy(
                                dst, src0), after=_f([ev_u0[scan]])))
                        else:
                            wnc = b.w3n[32 * g:32 * g + 32,
                                        scan * 16 + s * 4 + 2 * hp + h:
                                        scan * 16 + s * 4 + 2 * hp + h + 1]
                            psrc = self.PSB[32 * g:32 * g + 32,
                                            (h * 2 + scan) * 256:(h * 2 + scan) * 256 + 256]
                            e_us.append(P.op("dve",
                                             lambda v, dst=dst, src0=src0, wnc=wnc, psrc=psrc:
                                             v.scalar_tensor_tensor(dst, psrc, wnc, src0,
                                                                    ALU.mult, ALU.add),
                                             after=[e_wmS]))
                st["bank"][4] = e_us[-1]
                st["bank"][5] = e_us[-1]
                e_usum = P.op("dve", lambda v, g=g: v.tensor_add(
                    bass.AP(b.usum, 32 * g * 512, [[512, 32], [256, 2], [1, 256]]),
                    bass.AP(b.u_sb, 32 * g * 1024, [[1024, 32], [512, 2], [1, 256]]),
                    bass.AP(b.u_sb, 32 * g * 1024 + 256, [[1024, 32], [512, 2], [1, 256]])),
                    after=e_us)
                mmo = None
                for h in range(2):
                    if e_snap is not None:
                        for scan in range(2):
                            for kc in range(2):
                                aft = _f([e_snap] + ([st["bank"][6]] if (h == 0 and scan == 0 and kc == 0 and g == 0) else []))
                                mmo = P.op("tensor",
                                           lambda pe, h=h, scan=scan, kc=kc, g=g, c=c:
                                           pe.matmul(
                                               self.bank(6)[32 * g:32 * g + 32, h * 256:h * 256 + 256],
                                               b.qbf[:, (2 * h + kc) * 512 + c * 32:
                                                     (2 * h + kc) * 512 + c * 32 + 32],
                                               b.S_bf[:, (h * 2 + scan) * 512 + kc * 256:
                                                      (h * 2 + scan) * 512 + kc * 256 + 256],
                                               start=(scan == 0 and kc == 0), stop=False,
                                               tile_position=(0, 32 * g)),
                                           after=aft if (scan == 0 and kc == 0) else (),
                                           inc=False)
                        astart = False
                    else:
                        astart = True
                    aft = _f([e_usum] + ([st["bank"][6]] if (astart and h == 0 and g == 0) else []))
                    mmo = P.op("tensor", lambda pe, h=h, g=g, s=s, astart=astart: pe.matmul(
                        self.bank(6)[32 * g:32 * g + 32, h * 256:h * 256 + 256],
                        self._ap32(b.a_sb, 256, (h * 4 + s) * 32, g, 0),
                        bass.AP(b.usum, 32 * g * 512 + h * 256, [[512, 32], [1, 256]]),
                        start=astart, stop=True, tile_position=(32 * g, 32 * g)),
                        after=aft, inc=(h == 1))
                st["o_mm_%d" % g] = mmo
                mm = None
                for h in range(2):
                    for scan in range(2):
                        for kc in range(2):
                            sstart = (t == 0 and c == 0 and kc == 0)
                            mm = P.op("tensor",
                                      lambda pe, h=h, scan=scan, kc=kc, g=g, s=s, sstart=sstart:
                                      pe.matmul(
                                          self.PSA[:, (h * 2 + scan) * 512 + kc * 256:
                                                   (h * 2 + scan) * 512 + kc * 256 + 256],
                                          self._ap32(b.k_tm, 2048, (s * 2 + h) * 256 + kc * 128, g, 0, 128),
                                          self._ap32(b.u_sb, 1024, (h * 2 + scan) * 256, g, 0, 256),
                                          start=sstart, stop=False,
                                          tile_position=(32 * g, 0),
                                          skip_group_check=True),
                                      after=_f([e_usum] + ([e_snap] if e_snap else []))
                                      if (h == 0 and scan == 0 and kc == 0) else (),
                                      inc=(h == 1 and scan == 1 and kc == 1))
                e_supd = mm
                e_prev_chunk = mm
            if SCANSUB < 8:
                ev = e_prev_chunk
                st["scan_done"] = ev
                st["nmT_rd"] = ev
                st["nmn_rd"] = ev
                st["s32_ev"][hp] = ev
                st["bank"][6] = st.get("o_mm_3")
                continue
            o_ev = _f([st.get("o_mm_%d" % g) for g in range(4)])
            wrap = bass.AP(b.w3, 32 + s * 4 + 2 * hp, [[48, 128], [1, 2], [0, 256]])
            e_t1 = P.op("dve", lambda v, s=s, wrap=wrap: v.tensor_tensor(
                b.sf32[:, 1024:1536].rearrange("p (h n) -> p h n", h=2),
                b.v_tm[:, s * 512:(s + 1) * 512].rearrange("p (h n) -> p h n", h=2),
                wrap, ALU.mult),
                after=_f([st.get("sl2_rd")] + st.get("w3_ev", [])))
            e_t2 = P.op("dve", lambda v: v.tensor_add(
                b.sf32[:, 1024:1536], b.sf32[:, 1024:1536], self.bank(6)),
                after=[e_t1] + o_ev)
            st["bank"][6] = e_t2
            st["sl2_rd"] = e_t2
            e_ms = []
            prev = e_t2
            for h in range(2):
                prev = P.op("act", lambda a, h=h, s=s, hp=hp: a.activation(
                    b.sq_scr[:, 0:256], b.sf32[:, 1024 + h * 256:1024 + h * 256 + 256],
                    AF.Square, accum_out=b.rmsv[:, hp * 16 + s * 2 + h:hp * 16 + s * 2 + h + 1]),
                    after=[prev])
                e_ms.append(prev)
            e_rt = P.op("act", lambda a, s=s, hp=hp: a.activation(
                b.rmsv[:, hp * 16 + 8 + s * 2:hp * 16 + 8 + s * 2 + 2],
                b.rmsv[:, hp * 16 + s * 2:hp * 16 + s * 2 + 2],
                AF.Sqrt, bias=b.ceps[:, 0:1], scale=1.0 / 256), after=e_ms)
            e_rc = P.op("dve", lambda v, s=s, hp=hp: v.reciprocal(
                b.rmsv[:, hp * 16 + 8 + s * 2:hp * 16 + 8 + s * 2 + 2],
                b.rmsv[:, hp * 16 + 8 + s * 2:hp * 16 + 8 + s * 2 + 2]), after=[e_rt])
            rstdap = bass.AP(b.rmsv, hp * 16 + 8 + s * 2, [[32, 128], [1, 2], [0, 256]])
            e_orm = P.op("dve", lambda v, rstdap=rstdap: v.tensor_tensor(
                b.sf32[:, 1536:2048].rearrange("p (h n) -> p h n", h=2),
                b.sf32[:, 1024:1536].rearrange("p (h n) -> p h n", h=2),
                rstdap, ALU.mult), after=_f([e_rc, st.get("sl3_rd")])),
            e_orm = e_orm[0]
            mm = None
            for i in range(4):
                mm = P.op("tensor", lambda pe, i=i: pe.transpose(
                    self.bank(7)[:, i * 128:(i + 1) * 128],
                    b.sf32[:, 1536 + i * 128:1536 + (i + 1) * 128], b.idf[:, :]),
                    after=_f([e_orm, st["bank"][7]]) if i == 0 else (), inc=(i == 3))
            st["sl3_rd"] = mm
            dst = bass.AP(b.o_fm, hp * 2048 + s * 128,
                          [[4096, 128], [1024, 2], [512, 2], [1, 128]])
            e_ofm = P.op("dve", lambda v, dst=dst: v.tensor_copy(
                dst, bass.AP(self.PSB, 3 * 512, [[2048, 128], [256, 2], [128, 2], [1, 128]])),
                after=[mm])
            st["bank"][7] = e_ofm
            st["ofm_ev"] = e_ofm
        e_s32 = P.op("act", lambda a, hp=hp: a.activation(
            b.S32[:, hp * 2048:(hp + 1) * 2048], self.PSA[:, :], AF.Copy),
            after=_f([e_supd]))
        st["s32_ev"][hp] = e_s32
        for r in range(4):
            st["bank"][r] = e_s32
        st["scan_done"] = e_s32
        st["nmT_rd"] = e_s32
        st["nmn_rd"] = e_s32

    def wo_phase(self, P, t, st):
        b = self
        wo_ev = {}
        for half in range(2):
            par = half
            src = bass.AP(b.Wo_h, half * 512, [[D, 128], [128 * D, 8], [1, 512]])
            e_d = P.dma("pool", "wq%d" % par,
                        b.wqb[:, par * 4096:(par + 1) * 4096]
                        .rearrange("p (c n) -> p c n", c=8), src,
                        after=_f([st["wq_free"][par]]))
            last = e_d
            for m in range(8):
                last = P.op("dve", lambda v, m=m, par=par: v.tensor_scalar(
                    b.wqb[:, par * 4096 + m * 512:par * 4096 + (m + 1) * 512],
                    b.wqb[:, par * 4096 + m * 512:par * 4096 + (m + 1) * 512],
                    b.normw[:, m % 2:m % 2 + 1], None, ALU.mult),
                    after=[e_d] if m == 0 else ())
            wo_ev[half] = last
        mm = None
        for m in range(8):
            for tc in range(TC):
                for half in range(2):
                    bki = tc * 2 + half
                    aft = []
                    if m == 0:
                        aft = _f([wo_ev[half], st["bank"][bki], st.get("ofm_ev")])
                    mm = P.op("tensor", lambda pe, m=m, tc=tc, half=half, bki=bki: pe.matmul(
                        self.bank(bki), b.o_fm[:, m * 512 + tc * 128:m * 512 + tc * 128 + 128],
                        b.wqb[:, half * 4096 + m * 512:half * 4096 + (m + 1) * 512],
                        start=(m == 0), stop=(m == 7)),
                        after=aft, inc=(m == 7 and tc == 3 and half == 1))
        st["wq_free"][0] = mm
        st["wq_free"][1] = mm
        st["s_mm"] = mm

    def phase_combine(self, P, t, ev_g, st):
        b = self
        last = None
        for tc in range(TC):
            stage = b.xn_scr if tc % 2 == 0 else b.sq_scr
            e_gs = P.op("act", lambda a, tc=tc: a.activation(
                b.sf32[:, 0:1024], b.g_acc[:, tc * 1024:(tc + 1) * 1024], AF.Silu),
                after=_f([ev_g, st.get("acc_rd"), st.get("gs_rd")]))
            e_o = None
            for half in range(2):
                bki = tc * 2 + half
                sl = stage[:, half * 512:(half + 1) * 512]
                e_m = P.op("dve", lambda v, sl=sl, bki=bki, half=half: v.tensor_tensor(
                    sl, self.bank(bki), b.sf32[:, half * 512:(half + 1) * 512], ALU.mult),
                    after=_f([e_gs, st["s_mm"], st["stage_rd"][tc % 2],
                              st.get("xn_scr_rd") if tc % 2 == 0 else None]))
                e_o = P.op("dve", lambda v, sl=sl, tc=tc, half=half: v.tensor_tensor(
                    sl, sl, b.x_t[:, tc * 1024 + half * 512:tc * 1024 + half * 512 + 512],
                    ALU.add), after=[e_m])
                st["bank"][bki] = e_o
            st["gs_rd"] = e_o
            dst = bass.AP(b.y_h, (t * T + tc * 128) * D, [[D, 128], [1, D]])
            e_dma = P.dma("sync", "outa" if tc % 2 == 0 else "outb",
                          dst, stage[:, 0:1024], after=[e_o])
            st["out"].append(e_dma)
            st["stage_rd"][tc % 2] = e_dma
            last = e_o
        return last


def build_nc():
    nc = bass.Bass("TRN2", target_bir_lowering=False)
    b = Builder(nc)
    with b.stack, nc.Block() as block:
        block.sync(lambda eng: b.walk(eng, "sync"))
        block.scalar(lambda eng: b.walk(eng, "act"))
        block.vector(lambda eng: b.walk(eng, "dve"))
        block.tensor(lambda eng: b.walk(eng, "tensor"))
        block.gpsimd(lambda eng: b.walk(eng, "pool"))
    return nc, b


def kernel(**inputs):
    x = np.ascontiguousarray(inputs["L0_attn_hidden_states"])
    nm = {"L0_attn_Wq": "Wq", "L0_attn_Wk": "Wk", "L0_attn_Wv": "Wv",
          "L0_attn_conv_q": "conv_q", "L0_attn_conv_k": "conv_k",
          "L0_attn_conv_v": "conv_v", "L0_attn_Wb": "Wb", "L0_attn_Wg": "Wg",
          "L0_attn_bg": "bg", "L0_attn_Wh": "Wh", "L0_attn_bh": "bh",
          "L0_attn_norm_w": "norm_w", "L0_attn_Wo": "Wo",
          "L0_ffn_router_w": "router_w", "L0_ffn_up_w": "up_w",
          "L0_ffn_down_w": "down_w"}
    weights = {v: np.ascontiguousarray(inputs[k]) for k, v in nm.items()}
    nc, b = build_nc()
    in_maps = []
    for core in range(8):
        m = {"x": x[core]}
        m.update(weights)
        in_maps.append(m)
    res = run_bass_kernel_spmd(nc, in_maps, core_ids=list(range(8)))
    return np.stack([r["y"] for r in res.results], axis=0)


# version 02
# speedup vs baseline: 1.0664x; 1.0664x over previous
import numpy as np
from contextlib import ExitStack

import concourse.bass as bass
import concourse.mybir as mybir
from concourse.bass_utils import run_bass_kernel_spmd

F32 = mybir.dt.float32
BF16 = mybir.dt.bfloat16
AF = mybir.ActivationFunctionType
ALU = mybir.AluOpType
AX = mybir.AxisListType

L, D, NH, DK, DV, CHK, NE, HF = 8192, 1024, 4, 256, 256, 32, 8, 4096
T = 512
NT = L // T
TC = T // 128
PHASE = 9
SCANSUB = 9


def _f(evts):
    return [e for e in evts if e is not None]


class Walk:
    def __init__(self, b, eng_obj, eng_name):
        self.b = b
        self.eng = eng_obj
        self.name = eng_name
        self.cnt = {k: 0 for k in b.semnames}
        self.last_wait = {k: -1 for k in b.semnames}

    def _wait(self, evt):
        if evt is None:
            return
        sem, val = evt
        if self.last_wait[sem] >= val:
            return
        self.last_wait[sem] = val
        self.eng.wait_ge(self.b.sems[sem], val)

    def op(self, engine, fn, after=(), inc=True):
        evt = None
        if inc:
            self.cnt[engine] += 1
            evt = (engine, self.cnt[engine])
        if engine == self.name:
            for e in after:
                self._wait(e)
            inst = fn(self.eng)
            if inc:
                inst.then_inc(self.b.sems[engine], 1)
        return evt

    def dma(self, engine, sem, out, in_, after=()):
        self.cnt[sem] += 16
        evt = (sem, self.cnt[sem])
        if engine == self.name:
            for e in after:
                self._wait(e)
            self.eng.dma_start(out=out, in_=in_).then_inc(self.b.sems[sem], 16)
        return evt

    def wait(self, engine, evts):
        if engine == self.name:
            for e in evts:
                self._wait(e)


class Builder:
    semnames = ("tensor", "dve", "act", "pool", "sync", "dinit", "xin",
                "outa", "outb", "up0", "up1", "dn0", "dn1", "wq0", "wq1", "dinig",
                "uw0", "uw1", "dw0", "dw1", "ww0", "ww1")

    def __init__(self, nc):
        self.nc = nc
        self.stack = ExitStack()
        s = self.stack
        sb = lambda name, shape, dt=F32: s.enter_context(nc.sbuf_tensor(name, shape, dt))
        dram = lambda name, shape: nc.dram_tensor(name, shape, F32, kind="ExternalInput")
        self.x_h = dram("x", [L, D])
        self.y_h = nc.dram_tensor("y", [L, D], F32, kind="ExternalOutput")
        self.Wq_h, self.Wk_h, self.Wv_h = dram("Wq", [D, D]), dram("Wk", [D, D]), dram("Wv", [D, D])
        self.cq_h, self.ck_h, self.cv_h = dram("conv_q", [D, 4]), dram("conv_k", [D, 4]), dram("conv_v", [D, 4])
        self.Wb_h, self.Wg_h, self.Wh_h = dram("Wb", [D, 8]), dram("Wg", [D, 4]), dram("Wh", [D, 4])
        self.bg_h, self.bh_h = dram("bg", [4]), dram("bh", [4])
        self.normw_h, self.Wo_h = dram("norm_w", [256]), dram("Wo", [D, D])
        self.rwt_h = dram("router_w", [D, 8])
        self.up_h, self.dn_h = dram("up_w", [NE, D, HF]), dram("down_w", [NE, HF, D])
        self.up_s = nc.dram_tensor("up_s", [NE, D, HF], BF16)
        self.dn_s = nc.dram_tensor("dn_s", [NE, HF, D], BF16)
        self.wq_s = nc.dram_tensor("wq_s", [4, D, D], BF16)
        self.idf = sb("idf", [128, 128])
        self.onesf = sb("onesf", [128, 128])
        self.maskSneg = sb("maskSneg", [128, 32])
        self.maskIncl = sb("maskIncl", [128, 32])
        self.i32 = sb("i32", [128, 32])
        self.c2 = sb("c2", [128, 4])
        self.bc2 = sb("bc2", [2, 256])
        self.convt = sb("convt", [128, 96])
        self.wcat = sb("wcat", [128, 192], BF16)
        self.bgt = sb("bgt", [128, 8])
        self.normw = sb("normw", [128, 2])
        self.ceps = sb("ceps", [128, 2])
        self.x_t = sb("x_t", [128, 4096])
        self.wqb = sb("wqb", [128, 2 * 4096], BF16)
        self.upb = sb("upb", [128, 2 * 2048], BF16)
        self.dnb = sb("dnb", [128, 2 * 2048], BF16)
        self.ln_s = sb("ln_s", [128, 20])
        self.xn_scr = sb("xn_scr", [128, 1024])
        self.sq_scr = sb("sq_scr", [128, 1024])
        self.xn_fm = sb("xn_fm", [128, 4096], BF16)
        self.gh = sb("gh", [128, 32 * 512], BF16)
        self.pre = self.gh
        self.g_acc = sb("g_acc", [128, 4096])
        self.gat = sb("gat", [128, 64])
        self.beta_tm = sb("beta_tm", [128, 32])
        self.w3 = sb("w3", [128, 48])
        self.w3n = sb("w3n", [128, 32])
        self.rsc = sb("rsc", [128, 80])
        self.srt = sb("srt", [128, 32])
        self.rw_tm = sb("rw_tm", [128, 32])
        self.tails = sb("tails", [128, 72], BF16)
        self.qbf = sb("qbf", [128, 2048], BF16)
        self.kbf = sb("kbf", [128, 2048], BF16)
        self.sf32 = sb("sf32", [128, 2048])
        self.k_tm = sb("k_tm", [128, 2048], BF16)
        self.v_tm = sb("v_tm", [128, 2048], BF16)
        self.nmn = sb("nmn", [128, 2 * 1024])
        self.nmB = sb("nmB", [128, 7 * 1024], BF16)
        self.nmA = sb("nmA", [128, 1024], BF16)
        self.nmR = sb("nmR", [128, 1024], BF16)
        self.nmT = sb("nmT", [128, 2 * 1024], BF16)
        self.wm_sb = sb("wm_sb", [128, 1024], BF16)
        self.u0s = sb("u0s", [128, 1024], BF16)
        self.u_sb = sb("u_sb", [128, 1024], BF16)
        self.usum = sb("usum", [128, 512], BF16)
        self.a_sb = sb("a_sb", [128, 256], BF16)
        self.S_bf = sb("S_bf", [128, 2048], BF16)
        self.S32 = sb("S32", [128, 4096])
        self.o_fm = sb("o_fm", [128, 4096], BF16)
        self.rmsv = sb("rmsv", [128, 32])
        self.PSA = s.enter_context(nc.psum_tensor("PSA", [128, 2048], F32))
        self.PSB = s.enter_context(nc.psum_tensor("PSB", [128, 2048], F32))
        self.sems = {k: s.enter_context(nc.semaphore("sem_" + k)) for k in self.semnames}

    def bank(self, i, lo=0, n=512):
        t = self.PSA if i < 4 else self.PSB
        return t[:, (i % 4) * 512 + lo:(i % 4) * 512 + lo + n]

    def walk(self, eng_obj, eng_name):
        P = Walk(self, eng_obj, eng_name)
        self.program(P)

    def program(self, P):
        st = {"bank": [None] * 8, "out": [], "gh_rd": None,
              "stage_rd": [None, None], "scan_done": None, "s32_ev": [None, None],
              "wq_free": [None, None], "up_free": [None, None],
              "dn_free": [None, None], "upw": [None, None], "dnw": [None, None],
              "wqw": [None, None]}
        st["init"] = self.init_consts(P)
        ev_x = self.load_x(P, 0, after=st["init"])
        for t in range(NT):
            ev_ln = self.phase_ln(P, t, ev_x, st)
            ev_g = ev_ln
            if PHASE >= 2:
                ev_g = self.phase_ffn(P, t, ev_ln, st)
            if PHASE >= 3:
                self.phase_attn(P, t, ev_ln, st)
            if PHASE >= 9:
                ev_cmb = self.phase_combine(P, t, ev_g, st)
            else:
                dst = bass.AP(self.y_h, t * T * D, [[D, 128], [1, D]])
                ev_cmb = P.dma("sync", "outa", dst, self.x_t[:, 0:1024],
                               after=_f([ev_g, st.get("s_mm"), st["stage_rd"][0]]))
                st["out"].append(ev_cmb)
                st["stage_rd"][0] = ev_cmb
            if t + 1 < NT:
                ev_x = self.load_x(P, t + 1, after=[ev_cmb])
        P.wait("sync", st["out"][-2:])

    def load_x(self, P, t, after=()):
        xv = bass.AP(self.x_h, t * T * D, [[D, 128], [128 * D, TC], [1, D]])
        return P.dma("sync", "xin", self.x_t[:, :].rearrange("p (tc d) -> p tc d", tc=TC),
                     xv, after=_f(after))

    def init_consts(self, P):
        b = self
        ev = []
        ch = None
        chg = None
        e1 = P.op("pool", lambda g: g.memset(b.onesf[:, :], 1.0))
        ev.append(P.op("pool", lambda g: g.affine_select(
            b.idf[:, :], b.onesf[:, :], [[-1, 128]], ALU.is_equal, 0.0,
            base=0, channel_multiplier=1), after=[e1]))
        for tsr, fill, cmp_, cm, stp in [(b.maskSneg, -1.0, ALU.is_gt, 1, -1),
                                         (b.maskIncl, 1.0, ALU.is_ge, -1, 1),
                                         (b.i32, 1.0, ALU.is_equal, 1, -1)]:
            e2 = P.op("pool", lambda g, t_=tsr, f=fill: g.memset(t_[:, :], f))
            e3 = P.op("pool", lambda g, t_=tsr, c=cmp_, cm_=cm, s_=stp: g.affine_select(
                t_[0:32, :], t_[0:32, :], [[s_, 32]], c, 0.0, base=0,
                channel_multiplier=cm_), after=[e2])
            chg = P.dma("pool", "dinig", tsr[32:64, :], tsr[0:32, :], after=_f([e3, chg]))
            chg = P.dma("pool", "dinig", tsr[64:128, :], tsr[0:64, :], after=[chg])
        e4 = P.op("pool", lambda g: g.memset(b.c2[:, :], 0.0))
        e5 = P.op("pool", lambda g: g.memset(b.c2[:, 0:1], 1.0), after=[e4])
        ev.append(P.op("pool", lambda g: g.memset(b.c2[:, 3:4], 1.0), after=[e5]))
        e6 = P.op("pool", lambda g: g.memset(b.bc2[:, :], 0.0))
        e7 = P.op("pool", lambda g: g.memset(b.bc2[0:1, 0:128], 1.0), after=[e6])
        chg = P.dma("pool", "dinig", b.bc2[1:2, 128:256], b.bc2[0:1, 0:128],
                    after=_f([e7, chg]))
        ev.append(P.op("pool", lambda g: g.memset(b.tails[:, :], 0.0)))
        ev.append(P.op("pool", lambda g: g.memset(b.ceps[:, 0:1], 1e-5)))
        ev.append(P.op("pool", lambda g: g.memset(b.ceps[:, 1:2], 1e-12)))
        for i, h in enumerate([b.cq_h, b.ck_h, b.cv_h]):
            dst = bass.AP(b.convt, i * 32, [[96, 128], [4, 8], [1, 4]])
            ch = P.dma("sync", "dinit", dst,
                       h.ap().rearrange("(c p) k -> p c k", p=128), after=[ch])
        for hsrc, o0, wdt in [(b.Wb_h, 0, 8), (b.Wg_h, 8, 4), (b.Wh_h, 12, 4), (b.rwt_h, 16, 8)]:
            dst = bass.AP(b.wcat, o0, [[192, 128], [24, 8], [1, wdt]])
            chg = P.dma("pool", "dinig", dst,
                        hsrc.ap().rearrange("(c p) k -> p c k", p=128), after=[chg])
        ch = P.dma("sync", "dinit", bass.AP(b.bgt, 0, [[8, 1], [1, 4]]),
                   b.bg_h.ap()[None, :], after=[ch])
        ch = P.dma("sync", "dinit", bass.AP(b.bgt, 4, [[8, 1], [1, 4]]),
                   b.bh_h.ap()[None, :], after=[ch])
        nrep = 1
        while nrep < 128:
            hi = min(2 * nrep, 128)
            ch = P.dma("sync", "dinit", b.bgt[nrep:hi, :], b.bgt[0:hi - nrep, :],
                       after=[ch])
            nrep = hi
        for j in range(2):
            ch = P.dma("sync", "dinit", b.normw[:, j:j + 1],
                       b.normw_h.ap()[j * 128:(j + 1) * 128][:, None], after=[ch])
        ev.append(ch)
        ev.append(chg)
        return _f(ev)

    def phase_ln(self, P, t, ev_x, st):
        b = self
        last_evac = None
        for tc in range(TC):
            xc = b.x_t[:, tc * 1024:(tc + 1) * 1024]
            cS, cQ, cM, cR, cT = tc, 4 + tc, 8 + tc, 12 + tc, 16 + tc
            first = _f([st["stage_rd"][0], st["stage_rd"][1]]) if tc == 0 else []
            e_sum = P.op("dve", lambda v, xc=xc, cS=cS: v.reduce_sum(
                b.ln_s[:, cS:cS + 1], xc, AX.X), after=[ev_x] + first)
            e_sq = P.op("act", lambda a, xc=xc, cQ=cQ: a.activation(
                b.sq_scr[:, :], xc, AF.Square, accum_out=b.ln_s[:, cQ:cQ + 1]),
                after=[ev_x] + first)
            e_mu = P.op("dve", lambda v, cS=cS, cM=cM: v.tensor_scalar(
                b.ln_s[:, cM:cM + 1], b.ln_s[:, cS:cS + 1], 1.0 / D, None, ALU.mult),
                after=[e_sum])
            e_m2 = P.op("dve", lambda v, cM=cM, cT=cT: v.tensor_tensor(
                b.ln_s[:, cT:cT + 1], b.ln_s[:, cM:cM + 1], b.ln_s[:, cM:cM + 1],
                ALU.mult), after=[e_mu])
            e_qd = P.op("dve", lambda v, cQ=cQ, cR=cR: v.tensor_scalar(
                b.ln_s[:, cR:cR + 1], b.ln_s[:, cQ:cQ + 1], 1.0 / D, None, ALU.mult),
                after=[e_sq])
            e_var = P.op("dve", lambda v, cR=cR, cT=cT: v.tensor_sub(
                b.ln_s[:, cR:cR + 1], b.ln_s[:, cR:cR + 1], b.ln_s[:, cT:cT + 1]),
                after=[e_qd, e_m2])
            e_sd = P.op("act", lambda a, cR=cR: a.activation(
                b.ln_s[:, cR:cR + 1], b.ln_s[:, cR:cR + 1], AF.Sqrt, bias=b.ceps[:, 0:1],
                scale=1.0), after=[e_var])
            e_rs = P.op("dve", lambda v, cR=cR: v.reciprocal(
                b.ln_s[:, cR:cR + 1], b.ln_s[:, cR:cR + 1]), after=[e_sd])
            e_xn = P.op("dve", lambda v, xc=xc, cM=cM, cR=cR: v.tensor_scalar(
                b.xn_scr[:, :], xc, b.ln_s[:, cM:cM + 1], b.ln_s[:, cR:cR + 1],
                ALU.subtract, ALU.mult),
                after=[e_rs] + (_f([st["stage_rd"][0], st.get("xn_scr_rd")]) if tc == 0 else []))
            for half in range(2):
                bk = 4 + half
                mm = None
                for j in range(4):
                    dd = half * 4 + j
                    mm = P.op("tensor", lambda pe, dd=dd, bk=bk, j=j: pe.transpose(
                        self.bank(bk)[:, j * 128:(j + 1) * 128],
                        b.xn_scr[:, dd * 128:(dd + 1) * 128], b.idf[:, :]),
                        after=_f([e_xn, st["bank"][bk]] + st["init"]) if j == 0 else (),
                        inc=(j == 3))
                dst = bass.AP(b.xn_fm, half * 2048 + tc * 128,
                              [[4096, 128], [512, 4], [1, 128]])
                e_ev = P.op("dve", lambda v, bk=bk, dst=dst: v.tensor_copy(
                    dst, self.bank(bk)[:, 0:512].rearrange("p (c n) -> p c n", c=4)),
                    after=[mm])
                st["bank"][bk] = e_ev
                last_evac = e_ev
            st["xn_scr_rd"] = mm
        return last_evac

    def phase_ffn(self, P, t, ev_ln, st):
        b = self
        mmg = None
        for s in range(TC):
            for k in range(8):
                aft = _f([ev_ln, st["bank"][6]]) if (s == 0 and k == 0) else ()
                mmg = P.op("tensor", lambda pe, s=s, k=k: pe.matmul(
                    self.bank(6, s * 24, 24),
                    b.xn_fm[:, k * 512 + s * 128:k * 512 + s * 128 + 128],
                    b.wcat[:, k * 24:(k + 1) * 24], start=(k == 0), stop=(k == 7)),
                    after=aft, inc=(k == 7 and s == 3))
        e_bet = P.op("act", lambda a: a.activation(
            b.beta_tm[:, :].rearrange("p (s c) -> p s c", s=4),
            bass.AP(self.PSB, 2 * 512, [[2048, 128], [24, 4], [1, 8]]), AF.Sigmoid),
            after=[mmg])
        e_gat = P.op("dve", lambda v: v.tensor_copy(
            b.gat[:, :].rearrange("p (s c) -> p s c", s=4),
            bass.AP(self.PSB, 2 * 512 + 8, [[2048, 128], [24, 4], [1, 16]])),
            after=[mmg])
        st["bank"][6] = e_gat
        st["beta_ev"] = e_bet
        gv = bass.AP(b.gat, 0, [[64, 128], [16, 4], [1, 4]])
        hv = bass.AP(b.gat, 4, [[64, 128], [16, 4], [1, 4]])
        bgap = bass.AP(b.bgt, 0, [[8, 128], [0, 4], [1, 4]])
        bhap = bass.AP(b.bgt, 4, [[8, 128], [0, 4], [1, 4]])
        e1 = P.op("dve", lambda v: v.tensor_tensor(gv, gv, bgap, ALU.add), after=[e_gat])
        e2 = P.op("dve", lambda v: v.tensor_tensor(hv, hv, bhap, ALU.add), after=[e1])
        m_ = b.rsc[:, 0:16]
        m_v = m_.rearrange("p (s c) -> p s c", s=4)
        e3 = P.op("dve", lambda v: v.tensor_tensor(m_v, gv, hv, ALU.max), after=[e2])
        e4 = P.op("dve", lambda v: v.tensor_scalar_max(m_, m_, 0.0), after=[e3])
        eg_, ehg = b.rsc[:, 16:32], b.rsc[:, 32:48]
        e5 = P.op("dve", lambda v: v.tensor_sub(
            eg_.rearrange("p (s c) -> p s c", s=4), gv, m_v), after=[e4])
        e6 = P.op("dve", lambda v: v.tensor_sub(
            ehg.rearrange("p (s c) -> p s c", s=4), hv, m_v), after=[e5])
        wl, wg, wr = b.w3[:, 0:16], b.w3[:, 16:32], b.w3[:, 32:48]
        e7 = P.op("act", lambda a: a.activation(wl, eg_, AF.Exp), after=[e5])
        e8 = P.op("act", lambda a: a.activation(wg, ehg, AF.Exp), after=[e6])
        e9 = P.op("act", lambda a: a.activation(wr, m_, AF.Exp, scale=-1.0), after=[e6])
        den = b.rsc[:, 48:64]
        e10 = P.op("dve", lambda v: v.tensor_add(den, wl, wg), after=[e7, e8])
        e11 = P.op("dve", lambda v: v.tensor_add(den, den, wr), after=[e10, e9])
        e12 = P.op("dve", lambda v: v.reciprocal(den, den), after=[e11])
        e13 = P.op("dve", lambda v: v.tensor_mul(wl, wl, den), after=[e12])
        e14 = P.op("dve", lambda v: v.tensor_mul(wg, wg, den), after=[e13])
        e15 = P.op("dve", lambda v: v.tensor_mul(wr, wr, den), after=[e14])
        e16 = P.op("dve", lambda v: v.tensor_scalar(
            b.w3n[:, 0:16], wl, -1.0, None, ALU.mult), after=[e13])
        e17 = P.op("dve", lambda v: v.tensor_scalar(
            b.w3n[:, 16:32], wg, -1.0, None, ALU.mult), after=[e14])
        st["w3_ev"] = [e15, e16, e17]
        rv = bass.AP(b.gat, 8, [[64, 128], [16, 4], [1, 8]])
        m8, dn8, ts4, rd2 = (b.rsc[:, 64:68], b.rsc[:, 68:72],
                             b.rsc[:, 72:76], b.rsc[:, 76:80])
        r1 = P.op("dve", lambda v: v.reduce_max(m8, rv, AX.X), after=[e_gat])
        subs = []
        for s in range(TC):
            subs.append(P.op("dve", lambda v, s=s: v.tensor_scalar(
                b.sq_scr[:, s * 8:(s + 1) * 8], b.gat[:, s * 16 + 8:s * 16 + 16],
                m8[:, s:s + 1], None, ALU.subtract), after=[r1]))
        r2 = P.op("act", lambda a: a.activation(
            b.rw_tm[:, :], b.sq_scr[:, 0:32], AF.Exp), after=subs)
        r3 = P.op("dve", lambda v: v.reduce_sum(
            dn8, b.rw_tm[:, :].rearrange("p (s c) -> p s c", s=4), AX.X), after=[r2])
        r4 = P.op("dve", lambda v: v.reciprocal(dn8, dn8), after=[r3])
        dnap = bass.AP(b.rsc, 68, [[80, 128], [1, 4], [0, 8]])
        r5 = P.op("dve", lambda v: v.tensor_tensor(
            b.rw_tm[:, :].rearrange("p (s c) -> p s c", s=4),
            b.rw_tm[:, :].rearrange("p (s c) -> p s c", s=4), dnap, ALU.mult),
            after=[r4])
        srts = []
        for s in range(TC):
            srts.append(P.op("dve", lambda v, s=s: v.max(
                b.srt[:, s * 8:(s + 1) * 8], b.rw_tm[:, s * 8:(s + 1) * 8]),
                after=[r5]))
        r6 = P.op("dve", lambda v: v.reduce_sum(
            ts4, bass.AP(b.srt, 0, [[32, 128], [8, 4], [1, 4]]), AX.X), after=srts)
        r7 = P.op("dve", lambda v: v.reciprocal(rd2, ts4), after=[r6])
        keeps = []
        for s in range(TC):
            keeps.append(P.op("dve", lambda v, s=s: v.tensor_scalar(
                b.sq_scr[:, s * 8:(s + 1) * 8], b.rw_tm[:, s * 8:(s + 1) * 8],
                b.srt[:, s * 8 + 3:s * 8 + 4], None, ALU.is_ge), after=[r7]))
        r8 = P.op("dve", lambda v: v.tensor_mul(
            b.rw_tm[:, :], b.rw_tm[:, :], b.sq_scr[:, 0:32]), after=keeps)
        rd2ap = bass.AP(b.rsc, 76, [[80, 128], [1, 4], [0, 8]])
        e_rw = P.op("dve", lambda v: v.tensor_tensor(
            b.rw_tm[:, :].rearrange("p (s c) -> p s c", s=4),
            b.rw_tm[:, :].rearrange("p (s c) -> p s c", s=4), rd2ap, ALU.mult),
            after=[r8])
        st["rw_ev"] = e_rw
        return self.experts(P, t, ev_ln, e_rw, st)

    def experts(self, P, t, ev_ln, e_rw, st):
        b = self
        upv = lambda par: b.upb[:, par * 2048:(par + 1) * 2048]
        dnv = lambda par: b.dnb[:, par * 2048:(par + 1) * 2048]
        e_up, e_dn = {}, {}

        def fetch_up(e, he):
            par = (e * 16 + he) % 2
            apd = [[HF, 128], [128 * HF, 8], [1, 256]]
            off = e * D * HF + he * 256
            srcap = bass.AP(b.up_h if t == 0 else b.up_s, off, apd)
            e_up[(e, he)] = P.dma("pool", "up%d" % par,
                                  upv(par).rearrange("p (c n) -> p c n", c=8),
                                  srcap,
                                  after=_f([st["up_free"][par],
                                            st["upw"][par]]))
            if t == 0:
                st["upw"][par] = P.dma(
                    "sync", "uw%d" % par, bass.AP(b.up_s, off, apd),
                    upv(par).rearrange("p (c n) -> p c n", c=8),
                    after=[e_up[(e, he)]])

        def fetch_dn(e, ke):
            par = (e * 16 + ke) % 2
            apd = [[D, 128], [128 * D, 2], [1, 1024]]
            off = e * HF * D + ke * 256 * D
            srcap = bass.AP(b.dn_h if t == 0 else b.dn_s, off, apd)
            e_dn[(e, ke)] = P.dma("pool", "dn%d" % par,
                                  dnv(par).rearrange("p (c n) -> p c n", c=2),
                                  srcap,
                                  after=_f([st["dn_free"][par],
                                            st["dnw"][par]]))
            if t == 0:
                st["dnw"][par] = P.dma(
                    "sync", "dw%d" % par, bass.AP(b.dn_s, off, apd),
                    dnv(par).rearrange("p (c n) -> p c n", c=2),
                    after=[e_dn[(e, ke)]])

        fetch_up(0, 0)
        fetch_up(0, 1)
        e_gacc = None
        for e in range(NE):
            gelu_ev = {}
            for he in range(16):
                par = (e * 16 + he) % 2
                for kk in range(2):
                    k = he * 2 + kk
                    bk = 4 + (k % 2)
                    mm = None
                    for dd in range(8):
                        aft = _f([e_up[(e, he)], st["bank"][bk], ev_ln]) if dd == 0 else ()
                        mm = P.op("tensor", lambda pe, dd=dd, bk=bk, par=par, kk=kk: pe.matmul(
                            self.bank(bk),
                            upv(par)[:, dd * 256 + kk * 128:dd * 256 + kk * 128 + 128],
                            b.xn_fm[:, dd * 512:(dd + 1) * 512],
                            start=(dd == 0), stop=(dd == 7)),
                            after=aft, inc=(dd == 7))
                    if kk == 1:
                        st["up_free"][par] = mm
                    gaft = [mm] + ([st["gh_rd"], st.get("tail_ev")] if k == 0 else [])
                    gelu_ev[k] = P.op("act", lambda a, k=k, bk=bk: a.activation(
                        b.gh[:, k * 512:(k + 1) * 512], self.bank(bk),
                        AF.Gelu_apprx_tanh), after=_f(gaft))
                    st["bank"][bk] = gelu_ev[k]
                if he + 2 < 16:
                    fetch_up(e, he + 2)
                elif e + 1 < NE:
                    fetch_up(e + 1, he - 14)
                if he == 15:
                    fetch_dn(e, 0)
                    fetch_dn(e, 1)
            mm = None
            for ke in range(16):
                par = (e * 16 + ke) % 2
                for kk in range(2):
                    k = ke * 2 + kk
                    for tc in range(TC):
                        for half in range(2):
                            bki = tc * 2 + half
                            aft = []
                            if tc == 0 and half == 0:
                                aft = [e_dn[(e, ke)], gelu_ev[k]]
                            if k == 0:
                                aft.append(st["bank"][bki])
                            mm = P.op("tensor",
                                      lambda pe, k=k, tc=tc, half=half, par=par, kk=kk, bki=bki:
                                      pe.matmul(
                                          self.bank(bki),
                                          b.gh[:, k * 512 + tc * 128:k * 512 + tc * 128 + 128],
                                          dnv(par)[:, kk * 1024 + half * 512:kk * 1024 + half * 512 + 512],
                                          start=(k == 0), stop=(k == 31)),
                                      after=_f(aft), inc=(kk == 1 and tc == 3 and half == 1))
                    if kk == 1:
                        st["dn_free"][par] = mm
                if ke + 2 < 16:
                    fetch_dn(e, ke + 2)
            st["gh_rd"] = mm
            for tc in range(TC):
                for half in range(2):
                    bki = tc * 2 + half
                    dst = b.g_acc[:, tc * 1024 + half * 512:tc * 1024 + half * 512 + 512]
                    rwc = b.rw_tm[:, tc * 8 + e:tc * 8 + e + 1]
                    if e == 0:
                        ev2 = P.op("dve", lambda v, dst=dst, rwc=rwc, bki=bki: v.tensor_scalar(
                            dst, self.bank(bki), rwc, None, ALU.mult),
                            after=_f([mm, e_rw]))
                    else:
                        ev2 = P.op("dve", lambda v, dst=dst, rwc=rwc, bki=bki:
                                   v.scalar_tensor_tensor(
                                       dst, self.bank(bki), rwc, dst, ALU.mult, ALU.add),
                                   after=[mm])
                    st["bank"][bki] = ev2
                    e_gacc = ev2
        return e_gacc

    def phase_attn(self, P, t, ev_ln, st):
        for hp in range(2):
            self.proj_phase(P, t, hp, ev_ln, st)
            if PHASE >= 4:
                self.scan_phase(P, t, hp, st)
        if PHASE >= 5:
            self.wo_phase(P, t, st)

    def proj_phase(self, P, t, hp, ev_ln, st):
        b = self
        srcs = [b.Wq_h, b.Wk_h, b.Wv_h]
        wq_ev = {}

        def fetch_w(slot, h_src, col0, mat):
            par = slot % 2
            dst = b.wqb[:, par * 4096:(par + 1) * 4096].rearrange("p (c n) -> p c n", c=8)
            apd = [[D, 128], [128 * D, 8], [1, 512]]
            if t == 0:
                srcap = bass.AP(h_src, col0, apd)
            else:
                srcap = bass.AP(b.wq_s, mat * D * D + col0, apd)
            wq_ev[slot] = P.dma("pool", "wq%d" % par, dst, srcap,
                                after=_f([st["wq_free"][par],
                                          st["wqw"][par]]))
            if t == 0:
                st["wqw"][par] = P.dma(
                    "sync", "ww%d" % par, bass.AP(b.wq_s, mat * D * D + col0, apd),
                    dst, after=[wq_ev[slot]])

        base = hp * 3
        if hp == 0:
            fetch_w(0, srcs[0], 0, 0)
        else:
            wq_ev[base] = st.pop("wq_next")
        for pj in range(3):
            slot = base + pj
            par = slot % 2
            if pj + 1 < 3:
                fetch_w(slot + 1, srcs[pj + 1], hp * 512, pj + 1)
            elif hp == 0:
                fetch_w(slot + 1, srcs[0], 512, 0)
                st["wq_next"] = wq_ev[slot + 1]
            e_pre = {}
            for m in range(4):
                bk = 4 + (m % 2)
                mm = None
                for dd in range(8):
                    aft = _f([wq_ev[slot], st["bank"][bk], ev_ln]) if dd == 0 else ()
                    mm = P.op("tensor", lambda pe, m=m, dd=dd, bk=bk, par=par: pe.matmul(
                        self.bank(bk),
                        b.wqb[:, par * 4096 + dd * 512 + m * 128:par * 4096 + dd * 512 + m * 128 + 128],
                        b.xn_fm[:, dd * 512:(dd + 1) * 512],
                        start=(dd == 0), stop=(dd == 7)), after=aft, inc=(dd == 7))
                if m == 3:
                    st["wq_free"][par] = mm
                pcol = (pj * 4 + m) * 515
                aft2 = [mm]
                if pj == 0 and m == 0:
                    aft2 += _f([st["scan_done"], st.get("rw_ev"), st.get("gh_rd")])
                e_pre[m] = P.op("act", lambda a, pcol=pcol, bk=bk: a.activation(
                    b.pre[:, pcol + 3:pcol + 515], self.bank(bk), AF.Copy),
                    after=_f(aft2))
                st["bank"][bk] = e_pre[m]
            e_sil = {}
            for m in range(4):
                mg = hp * 4 + m
                pcol = (pj * 4 + m) * 515
                tcol = (pj * 8 + mg) * 3
                acc = b.sf32[:, 0:512]
                tmp = b.sf32[:, 512:1024]
                tapc = lambda j, pj=pj, mg=mg: b.convt[:, pj * 32 + mg * 4 + j:pj * 32 + mg * 4 + j + 1]
                e_ti = P.op("pool", lambda g, pcol=pcol, tcol=tcol: g.tensor_copy(
                    b.pre[:, pcol:pcol + 3], b.tails[:, tcol:tcol + 3]),
                    after=_f([e_pre[m], st.get("tail_ev")] + st["init"]))
                e_c = P.op("pool", lambda g, pcol=pcol, tapc=tapc, acc=acc: g.tensor_scalar(
                    acc, b.pre[:, pcol + 3:pcol + 515], tapc(3), None, ALU.mult),
                    after=_f([e_ti, st.get("acc_rd")]))
                for j in (2, 1, 0):
                    e_tj = P.op("pool", lambda g, pcol=pcol, j=j, tapc=tapc, tmp=tmp:
                                g.tensor_scalar(tmp, b.pre[:, pcol + j:pcol + j + 512],
                                                tapc(j), None, ALU.mult), after=[e_c])
                    e_c = P.op("pool", lambda g, acc=acc, tmp=tmp: g.tensor_tensor(
                        acc, acc, tmp, ALU.add), after=[e_tj])
                st["tail_ev"] = P.op("pool", lambda g, pcol=pcol, tcol=tcol: g.tensor_copy(
                    b.tails[:, tcol:tcol + 3], b.pre[:, pcol + 512:pcol + 515]),
                    after=[e_c])
                if pj < 2:
                    dstb = [b.qbf, b.kbf][pj]
                    e_sil[m] = P.op("act", lambda a, m=m, dstb=dstb, acc=acc: a.activation(
                        dstb[:, m * 512:(m + 1) * 512], acc, AF.Silu),
                        after=_f([e_c, st.get("qk_rd")]))
                    st["acc_rd"] = e_sil[m]
                else:
                    e_sv = P.op("act", lambda a, acc=acc: a.activation(
                        b.sf32[:, 1024:1536], acc, AF.Silu),
                        after=_f([e_c, st.get("sl2_rd")]))
                    st["acc_rd"] = e_sv
                    self.kv_transpose(P, st, b.sf32[:, 1024:1536], True, m // 2, m % 2, e_sv)
            if pj < 2:
                self.l2_phase(P, hp, pj, e_sil, st)

    def l2_phase(self, P, hp, pj, e_sil, st):
        b = self
        dstb = [b.qbf, b.kbf][pj]
        rs = self.PSB[0:2, 3 * 512:4 * 512]
        mm = None
        for m in range(4):
            e_q = P.op("act", lambda a, m=m, dstb=dstb: a.activation(
                b.sq_scr[:, 0:512], dstb[:, m * 512:(m + 1) * 512], AF.Square),
                after=_f([e_sil[m]] + ([mm] if mm else []) +
                         ([st.get("rw_ev")] if m == 0 else [])))
            aft = _f([e_q] + ([st["bank"][7]] if m == 0 else []))
            mm = P.op("tensor", lambda pe, m=m: pe.matmul(
                rs, b.c2[:, (m // 2) * 2:(m // 2) * 2 + 2], b.sq_scr[:, 0:512],
                start=(m == 0), stop=(m == 3)), after=aft, inc=True)
        rn = b.sf32[0:2, 1536:2048]
        e_rt = P.op("act", lambda a, rn=rn: a.activation(
            rn, rs, AF.Sqrt, bias=b.ceps[0:2, 1:2], scale=1.0),
            after=_f([mm, st.get("sl3_rd")]))
        e_rc = P.op("dve", lambda v, rn=rn: v.reciprocal(rn, rn), after=[e_rt])
        st["bank"][7] = e_rc
        mmb_last = None
        for h in range(2):
            bcast = self.bank(6)
            mmb = P.op("tensor", lambda pe, h=h, rn=rn: pe.matmul(
                bcast, b.bc2[:, h * 128:(h + 1) * 128], rn, start=True, stop=True),
                after=_f([e_rc, st["bank"][6]]), inc=True)
            mmb_last = mmb
            last = None
            for m in (2 * h, 2 * h + 1):
                e_n = P.op("dve", lambda v, m=m, dstb=dstb, bcast=bcast: v.tensor_tensor(
                    dstb[:, m * 512:(m + 1) * 512], dstb[:, m * 512:(m + 1) * 512],
                    bcast, ALU.mult), after=[mmb])
                last = e_n
                if pj == 1:
                    e_fc = P.op("dve", lambda v, m=m, dstb=dstb: v.tensor_copy(
                        b.sf32[:, 1024:1536], dstb[:, m * 512:(m + 1) * 512]),
                        after=_f([e_n, st.get("sl2_rd")]))
                    self.kv_transpose(P, st, b.sf32[:, 1024:1536], False, h, m % 2, e_fc)
            st["bank"][6] = last
        st["sl3_rd"] = mmb_last
        st["qk_rd"] = None
        if pj == 0:
            st["qbf_ev"] = last
        else:
            st["kbf_ev"] = last

    def kv_transpose(self, P, st, src, is_v, h, vh, after):
        b = self
        bk = 7
        mm = None
        for ts_ in range(4):
            mm = P.op("tensor", lambda pe, ts_=ts_, bk=bk, src=src: pe.transpose(
                self.bank(bk)[:, ts_ * 128:(ts_ + 1) * 128],
                src[:, ts_ * 128:(ts_ + 1) * 128], b.idf[:, :]),
                after=_f([after, st["bank"][bk]]) if ts_ == 0 else (), inc=(ts_ == 3))
        st["sl2_rd"] = mm
        dstt = b.v_tm if is_v else b.k_tm
        dst = bass.AP(dstt, h * 256 + vh * 128, [[2048, 128], [512, 4], [1, 128]])
        e_ev = P.op("dve", lambda v, dst=dst, bk=bk: v.tensor_copy(
            dst, self.bank(bk)[:, 0:512].rearrange("p (c n) -> p c n", c=4)),
            after=[mm])
        st["bank"][bk] = e_ev
        st["tm_ev"] = e_ev

    def _ap32(self, tsr, freesz, base, g, col, n=32):
        return bass.AP(tsr, 32 * g * freesz + base + col, [[freesz, 32], [1, n]])

    def scan_phase(self, P, t, hp, st):
        b = self
        if t > 0:
            seed = None
            for r in range(4):
                aft = _f([st["bank"][0], st["bank"][1], st["bank"][2], st["bank"][3],
                          st["s32_ev"][hp]]) if r == 0 else ()
                seed = P.op("tensor", lambda pe, r=r: pe.matmul(
                    self.PSA[:, r * 512:(r + 1) * 512], b.idf[:, :],
                    b.S32[:, hp * 2048 + r * 512:hp * 2048 + (r + 1) * 512],
                    start=True, stop=False, skip_group_check=True),
                    after=aft, inc=(r == 3))
            e_supd = seed
        else:
            e_supd = None
        kq = _f([st.get("kbf_ev"), st.get("qbf_ev"), st.get("tm_ev")])
        e_z = P.op("dve", lambda v: v.memset(self.PSB[:, 0:1024], 0.0),
                   after=_f([st["bank"][4], st["bank"][5]]))
        st["bank"][4] = e_z
        st["bank"][5] = e_z
        mmA = None
        for h in range(2):
            for s in range(TC):
                for g in range(4):
                    c = s * 4 + g
                    for kc in range(2):
                        aft = _f([st["bank"][4], st["bank"][5]] + kq) \
                            if (h == 0 and s == 0 and g == 0 and kc == 0) else ()
                        col = (h * 4 + s) * 128 + g * 32
                        mmA = P.op("tensor", lambda pe, h=h, c=c, g=g, kc=kc, col=col: pe.matmul(
                            self.PSB[32 * g:32 * g + 32, col:col + 32],
                            b.kbf[:, (2 * h + kc) * 512 + c * 32:(2 * h + kc) * 512 + c * 32 + 32],
                            b.kbf[:, (2 * h + kc) * 512 + c * 32:(2 * h + kc) * 512 + c * 32 + 32],
                            start=(kc == 0), stop=(kc == 1), tile_position=(0, 32 * g)),
                            after=aft, inc=(kc == 1))
        mmS = None
        for h in range(2):
            for s in range(TC):
                for g in range(4):
                    c = s * 4 + g
                    for kc in range(2):
                        aft = _f([st["bank"][6]]) if (h == 0 and s == 0 and g == 0 and kc == 0) else ()
                        col = 2 * 512 + (h * 4 + s) * 32
                        mmS = P.op("tensor", lambda pe, h=h, c=c, g=g, kc=kc, col=col: pe.matmul(
                            self.PSB[32 * g:32 * g + 32, col:col + 32],
                            b.kbf[:, (2 * h + kc) * 512 + c * 32:(2 * h + kc) * 512 + c * 32 + 32],
                            b.qbf[:, (2 * h + kc) * 512 + c * 32:(2 * h + kc) * 512 + c * 32 + 32],
                            start=(kc == 0), stop=(kc == 1), tile_position=(0, 32 * g)),
                            after=aft, inc=(kc == 1))
        if SCANSUB < 2:
            st["scan_done"] = mmS
            st["nmT_rd"] = mmS
            st["nmn_rd"] = mmS
            st["s32_ev"][hp] = mmS
            return
        e_a = P.op("dve", lambda v: v.tensor_tensor(
            b.a_sb[:, :].rearrange("p (q j) -> p q j", q=8),
            bass.AP(self.PSB, 2 * 512, [[2048, 128], [32, 8], [1, 32]]),
            bass.AP(b.maskIncl, 0, [[32, 128], [0, 8], [1, 32]]), ALU.mult),
            after=[mmS])
        st["bank"][6] = e_a
        e_As = {}
        e_msk = None
        for scan in range(2):
            e_msk = P.op("dve", lambda v, scan=scan: v.tensor_tensor(
                bass.AP(b.nmn, scan * 1024, [[2048, 128], [128, 8], [32, 4], [1, 32]]),
                bass.AP(self.PSB, 0, [[2048, 128], [128, 8], [32, 4], [1, 32]]),
                bass.AP(b.maskSneg, 0, [[32, 128], [0, 8], [0, 4], [1, 32]]), ALU.mult),
                after=_f([mmA] + ([st.get("nmn_rd")] if scan == 0 else [])))
            bap = bass.AP(b.beta_tm, scan * 4 + 2 * hp,
                          [[32, 128], [1, 2], [8, 4], [0, 128]])
            e_As[scan] = P.op("dve", lambda v, scan=scan, bap=bap: v.tensor_tensor(
                bass.AP(b.nmn, scan * 1024, [[2048, 128], [512, 2], [128, 4], [1, 128]]),
                bass.AP(b.nmn, scan * 1024, [[2048, 128], [512, 2], [128, 4], [1, 128]]),
                bap, ALU.mult),
                after=_f([e_msk, st.get("beta_ev")]))
        st["bank"][4] = e_As[1]
        st["bank"][5] = e_As[1]
        if SCANSUB < 3:
            st["scan_done"] = e_As[1]
            st["nmT_rd"] = e_As[1]
            st["nmn_rd"] = e_As[1]
            st["s32_ev"][hp] = e_As[1]
            return
        for scan in range(2):
            e_abf = P.op("dve", lambda v, scan=scan: v.tensor_copy(
                b.nmB[:, 0:1024], b.nmn[:, scan * 1024:(scan + 1) * 1024]),
                after=_f([e_As[scan], st.get("nmB_rd")]))
            mm = None
            for blk in range(8):
                bk = 4 + blk // 4
                mm = P.op("tensor", lambda pe, blk=blk, bk=bk, scan=scan: pe.transpose(
                    self.bank(bk)[:, (blk % 4) * 128:(blk % 4) * 128 + 128],
                    b.nmn[:, scan * 1024 + blk * 128:scan * 1024 + (blk + 1) * 128],
                    b.idf[:, :]),
                    after=_f([e_As[scan], st["bank"][4], st["bank"][5]]) if blk == 0 else (),
                    inc=(blk == 7))
            e_Ap = P.op("dve", lambda v: v.tensor_copy(
                b.nmA[:, :].rearrange("p (u n) -> p u n", u=2),
                bass.AP(self.PSB, 0, [[2048, 128], [512, 2], [1, 512]])),
                after=_f([mm, st.get("nmA_rd")]))
            st["bank"][4] = e_Ap
            st["bank"][5] = e_Ap
            if SCANSUB < 4:
                st["scan_done"] = e_Ap
                st["nmT_rd"] = e_Ap
                st["nmn_rd"] = e_Ap
                st["s32_ev"][hp] = e_Ap
                st["ttil_%d" % scan] = e_Ap
                continue
            def bap32(slot, g, col, n=32):
                return self._ap32(b.nmB, 7168, slot * 1024, g, col, n)

            def aap32(g, col, n=32):
                return self._ap32(b.nmA, 1024, 0, g, col, n)

            chain = [(1, "A'", 0), (5, 0, "A'"), (2, 5, 1), (6, 1, 5),
                     (3, 6, 2), (5, 2, 6), (4, 5, 3)]
            prev_ev = [e_abf, e_Ap]
            for (dsts, lt, rh) in chain:
                mm = None
                for blk in range(8):
                    def sl_of(x, blk=0):
                        t_ = b.nmA if x == "A'" else None
                        return None
                    lt_ap = (b.nmA if lt == "A'" else
                             b.nmB[:, lt * 1024:(lt + 1) * 1024])
                    rh_ap = (b.nmA if rh == "A'" else
                             b.nmB[:, rh * 1024:(rh + 1) * 1024])
                    mm = P.op("tensor",
                              lambda pe, blk=blk, lt_ap=lt_ap, rh_ap=rh_ap: pe.matmul(
                                  self.PSB[:, blk * 128:(blk + 1) * 128],
                                  lt_ap[:, blk * 128:(blk + 1) * 128],
                                  rh_ap[:, blk * 128:(blk + 1) * 128],
                                  start=True, stop=True),
                              after=_f(prev_ev + [st["bank"][4], st["bank"][5]])
                              if blk == 0 else (),
                              inc=(blk == 7))
                e_ev = P.op("dve", lambda v, dsts=dsts: v.tensor_copy(
                    b.nmB[:, dsts * 1024:(dsts + 1) * 1024].rearrange("p (u n) -> p u n", u=2),
                    bass.AP(self.PSB, 0, [[2048, 128], [512, 2], [1, 512]])),
                    after=[mm])
                st["bank"][4] = e_ev
                st["bank"][5] = e_ev
                prev_ev = [e_ev]
            if SCANSUB < 5:
                st["scan_done"] = prev_ev[-1]
                st["nmT_rd"] = prev_ev[-1]
                st["nmn_rd"] = prev_ev[-1]
                st["s32_ev"][hp] = prev_ev[-1]
                st["ttil_%d" % scan] = prev_ev[-1]
                continue
            e_r = P.op("dve", lambda v: v.tensor_add(
                bass.AP(b.nmR, 0, [[1024, 128], [128, 8], [1, 128]]),
                bass.AP(b.nmA, 0, [[1024, 128], [128, 8], [1, 128]]),
                bass.AP(b.idf, 0, [[128, 128], [0, 8], [1, 128]])),
                after=_f([e_Ap, st.get("nmR_rd")]))
            first_r = True
            for bslot in (1, 2, 3, 4):
                mm = None
                for blk in range(8):
                    aft = ()
                    if blk == 0:
                        aft = _f([e_r] + ([st["bank"][6], st["bank"][7]]
                                          if first_r else []))
                    mm = P.op("tensor",
                              lambda pe, blk=blk, bslot=bslot: pe.matmul(
                                  self.PSB[:, 1024 + blk * 128:1024 + (blk + 1) * 128],
                                  b.nmB[:, bslot * 1024 + blk * 128:bslot * 1024 + (blk + 1) * 128],
                                  b.nmR[:, blk * 128:(blk + 1) * 128],
                                  start=True, stop=True),
                              after=aft, inc=(blk == 7))
                first_r = False
                e_r = P.op("dve", lambda v: v.tensor_add(
                    b.nmR[:, :], b.nmR[:, :], self.PSB[:, 1024:2048]), after=[mm])
            st["bank"][6] = e_r
            st["bank"][7] = e_r
            bap = bass.AP(b.beta_tm, scan * 4 + 2 * hp,
                          [[32, 128], [1, 2], [8, 4], [0, 128]])
            e_tt = P.op("dve", lambda v, scan=scan, bap=bap: v.tensor_tensor(
                bass.AP(b.nmT, scan * 1024, [[2048, 128], [512, 2], [128, 4], [1, 128]]),
                bass.AP(b.nmR, 0, [[1024, 128], [128, 8], [32, 4], [1, 128]])
                if False else
                bass.AP(b.nmR, 0, [[1024, 128], [512, 2], [128, 4], [1, 128]]),
                bap, ALU.mult),
                after=_f([e_r, st.get("nmT_rd")]))
            st["nmR_rd"] = e_tt
            st["nmA_rd"] = e_r
            st["nmB_rd"] = e_r
            st["ttil_%d" % scan] = e_tt
        if SCANSUB < 6:
            ev = st["ttil_1"]
            st["scan_done"] = ev
            st["nmT_rd"] = ev
            st["nmn_rd"] = ev
            st["s32_ev"][hp] = ev
            return
        e_prev_chunk = None
        for s in range(TC):
            ev_u0, ev_wm = {}, {}
            for scan in range(2):
                bk = 6 + scan
                mm = None
                for h in range(2):
                    mm = P.op("tensor", lambda pe, h=h, scan=scan, bk=bk, s=s: pe.matmul(
                        self.bank(bk)[:, h * 256:h * 256 + 256],
                        b.nmT[:, scan * 1024 + (h * 4 + s) * 128:
                              scan * 1024 + (h * 4 + s) * 128 + 128],
                        b.v_tm[:, (s * 2 + h) * 256:(s * 2 + h) * 256 + 256],
                        start=True, stop=True),
                        after=_f([st["ttil_%d" % scan], st["bank"][bk]])
                        if h == 0 else (), inc=(h == 1))
                wap = bass.AP(b.w3, scan * 16 + s * 4 + 2 * hp,
                              [[48, 128], [1, 2], [0, 256]])
                ev_u0[scan] = P.op("dve", lambda v, scan=scan, bk=bk, wap=wap: v.tensor_tensor(
                    b.u0s[:, scan * 512:(scan + 1) * 512].rearrange("p (h n) -> p h n", h=2),
                    self.bank(bk).rearrange("p (h n) -> p h n", h=2), wap, ALU.mult),
                    after=_f([mm] + st.get("w3_ev", [])))
                st["bank"][bk] = ev_u0[scan]
                mm2 = None
                for h in range(2):
                    for kc in range(2):
                        mm2 = P.op("tensor",
                                   lambda pe, scan=scan, bk=bk, s=s, h=h, kc=kc: pe.matmul(
                                       self.bank(bk)[:, (h * 2 + kc) * 128:(h * 2 + kc) * 128 + 128],
                                       b.k_tm[:, (s * 2 + h) * 256 + kc * 128:
                                              (s * 2 + h) * 256 + kc * 128 + 128],
                                       b.nmT[:, scan * 1024 + (h * 4 + s) * 128:
                                             scan * 1024 + (h * 4 + s) * 128 + 128],
                                       start=True, stop=True),
                                   after=_f([ev_u0[scan]]) if (h == 0 and kc == 0) else (),
                                   inc=(h == 1 and kc == 1))
                ev_wm[scan] = P.op("dve", lambda v, scan=scan, bk=bk: v.tensor_copy(
                    b.wm_sb[:, scan * 512:(scan + 1) * 512], self.bank(bk)), after=[mm2])
                st["bank"][bk] = ev_wm[scan]
            if SCANSUB < 7:
                ev = ev_wm[1]
                st["scan_done"] = ev
                st["nmT_rd"] = ev
                st["nmn_rd"] = ev
                st["s32_ev"][hp] = ev
                continue
            for g in range(4):
                c = s * 4 + g
                if not (t == 0 and c == 0):
                    e_snap = P.op("act", lambda a: a.activation(
                        b.S_bf[:, :], self.PSA[:, :], AF.Copy),
                        after=_f([e_supd, e_prev_chunk, st.get("o_mm_prev")]))
                else:
                    e_snap = None
                e_wmS = None
                if e_snap is not None:
                    mm = None
                    for h in range(2):
                        for scan in range(2):
                            for kc in range(2):
                                col4 = (h * 2 + scan) * 256
                                mm = P.op("tensor",
                                          lambda pe, h=h, scan=scan, kc=kc, g=g, col4=col4, s=s:
                                          pe.matmul(
                                              self.PSB[32 * g:32 * g + 32, col4:col4 + 256],
                                              b.wm_sb[:, scan * 512 + (h * 2 + kc) * 128 + g * 32:
                                                      scan * 512 + (h * 2 + kc) * 128 + g * 32 + 32],
                                              b.S_bf[:, (h * 2 + scan) * 512 + kc * 256:
                                                     (h * 2 + scan) * 512 + kc * 256 + 256],
                                              start=(kc == 0), stop=(kc == 1),
                                              tile_position=(0, 32 * g)),
                                          after=_f([e_snap, ev_wm[0], ev_wm[1],
                                                    st["bank"][4], st["bank"][5]])
                                          if (h == 0 and scan == 0 and kc == 0) else (),
                                          inc=(h == 1 and scan == 1 and kc == 1))
                    e_wmS = mm
                e_us = []
                for h in range(2):
                    for scan in range(2):
                        dst = self._ap32(b.u_sb, 1024, (h * 2 + scan) * 256, g, 0, 256)
                        src0 = self._ap32(b.u0s, 1024, scan * 512 + h * 256, g, 0, 256)
                        if e_wmS is None:
                            e_us.append(P.op("dve", lambda v, dst=dst, src0=src0: v.tensor_copy(
                                dst, src0), after=_f([ev_u0[scan]])))
                        else:
                            wnc = b.w3n[32 * g:32 * g + 32,
                                        scan * 16 + s * 4 + 2 * hp + h:
                                        scan * 16 + s * 4 + 2 * hp + h + 1]
                            psrc = self.PSB[32 * g:32 * g + 32,
                                            (h * 2 + scan) * 256:(h * 2 + scan) * 256 + 256]
                            e_us.append(P.op("dve",
                                             lambda v, dst=dst, src0=src0, wnc=wnc, psrc=psrc:
                                             v.scalar_tensor_tensor(dst, psrc, wnc, src0,
                                                                    ALU.mult, ALU.add),
                                             after=[e_wmS]))
                st["bank"][4] = e_us[-1]
                st["bank"][5] = e_us[-1]
                e_usum = P.op("dve", lambda v, g=g: v.tensor_add(
                    bass.AP(b.usum, 32 * g * 512, [[512, 32], [256, 2], [1, 256]]),
                    bass.AP(b.u_sb, 32 * g * 1024, [[1024, 32], [512, 2], [1, 256]]),
                    bass.AP(b.u_sb, 32 * g * 1024 + 256, [[1024, 32], [512, 2], [1, 256]])),
                    after=e_us)
                mm = None
                for h in range(2):
                    for scan in range(2):
                        for kc in range(2):
                            sstart = (t == 0 and c == 0 and kc == 0)
                            mm = P.op("tensor",
                                      lambda pe, h=h, scan=scan, kc=kc, g=g, s=s, sstart=sstart:
                                      pe.matmul(
                                          self.PSA[:, (h * 2 + scan) * 512 + kc * 256:
                                                   (h * 2 + scan) * 512 + kc * 256 + 256],
                                          self._ap32(b.k_tm, 2048, (s * 2 + h) * 256 + kc * 128, g, 0, 128),
                                          self._ap32(b.u_sb, 1024, (h * 2 + scan) * 256, g, 0, 256),
                                          start=sstart, stop=False,
                                          tile_position=(32 * g, 0),
                                          skip_group_check=True),
                                      after=_f([e_usum] + ([e_snap] if e_snap else []))
                                      if (h == 0 and scan == 0 and kc == 0) else (),
                                      inc=(h == 1 and scan == 1 and kc == 1))
                e_supd = mm
                e_prev_chunk = mm
                mmo = None
                for h in range(2):
                    if e_snap is not None:
                        for scan in range(2):
                            for kc in range(2):
                                aft = _f([e_snap] + ([st["bank"][6]] if (h == 0 and scan == 0 and kc == 0 and g == 0) else []))
                                mmo = P.op("tensor",
                                           lambda pe, h=h, scan=scan, kc=kc, g=g, c=c:
                                           pe.matmul(
                                               self.bank(6)[32 * g:32 * g + 32, h * 256:h * 256 + 256],
                                               b.qbf[:, (2 * h + kc) * 512 + c * 32:
                                                     (2 * h + kc) * 512 + c * 32 + 32],
                                               b.S_bf[:, (h * 2 + scan) * 512 + kc * 256:
                                                      (h * 2 + scan) * 512 + kc * 256 + 256],
                                               start=(scan == 0 and kc == 0), stop=False,
                                               tile_position=(0, 32 * g)),
                                           after=aft if (scan == 0 and kc == 0) else (),
                                           inc=False)
                        astart = False
                    else:
                        astart = True
                    aft = _f([e_usum] + ([st["bank"][6]] if (astart and h == 0 and g == 0) else []))
                    mmo = P.op("tensor", lambda pe, h=h, g=g, s=s, astart=astart: pe.matmul(
                        self.bank(6)[32 * g:32 * g + 32, h * 256:h * 256 + 256],
                        self._ap32(b.a_sb, 256, (h * 4 + s) * 32, g, 0),
                        bass.AP(b.usum, 32 * g * 512 + h * 256, [[512, 32], [1, 256]]),
                        start=astart, stop=True, tile_position=(32 * g, 32 * g)),
                        after=aft, inc=(h == 1))
                st["o_mm_%d" % g] = mmo
                st["o_mm_prev"] = mmo
            if SCANSUB < 8:
                ev = e_prev_chunk
                st["scan_done"] = ev
                st["nmT_rd"] = ev
                st["nmn_rd"] = ev
                st["s32_ev"][hp] = ev
                st["bank"][6] = st.get("o_mm_3")
                continue
            o_ev = _f([st.get("o_mm_%d" % g) for g in range(4)])
            wrap = bass.AP(b.w3, 32 + s * 4 + 2 * hp, [[48, 128], [1, 2], [0, 256]])
            e_t1 = P.op("dve", lambda v, s=s, wrap=wrap: v.tensor_tensor(
                b.sf32[:, 1024:1536].rearrange("p (h n) -> p h n", h=2),
                b.v_tm[:, s * 512:(s + 1) * 512].rearrange("p (h n) -> p h n", h=2),
                wrap, ALU.mult),
                after=_f([st.get("sl2_rd")] + st.get("w3_ev", [])))
            e_t2 = P.op("dve", lambda v: v.tensor_add(
                b.sf32[:, 1024:1536], b.sf32[:, 1024:1536], self.bank(6)),
                after=[e_t1] + o_ev)
            st["bank"][6] = e_t2
            st["sl2_rd"] = e_t2
            e_ms = []
            prev = e_t2
            for h in range(2):
                prev = P.op("act", lambda a, h=h, s=s, hp=hp: a.activation(
                    b.sq_scr[:, 0:256], b.sf32[:, 1024 + h * 256:1024 + h * 256 + 256],
                    AF.Square, accum_out=b.rmsv[:, hp * 16 + s * 2 + h:hp * 16 + s * 2 + h + 1]),
                    after=[prev])
                e_ms.append(prev)
            e_rt = P.op("act", lambda a, s=s, hp=hp: a.activation(
                b.rmsv[:, hp * 16 + 8 + s * 2:hp * 16 + 8 + s * 2 + 2],
                b.rmsv[:, hp * 16 + s * 2:hp * 16 + s * 2 + 2],
                AF.Sqrt, bias=b.ceps[:, 0:1], scale=1.0 / 256), after=e_ms)
            e_rc = P.op("dve", lambda v, s=s, hp=hp: v.reciprocal(
                b.rmsv[:, hp * 16 + 8 + s * 2:hp * 16 + 8 + s * 2 + 2],
                b.rmsv[:, hp * 16 + 8 + s * 2:hp * 16 + 8 + s * 2 + 2]), after=[e_rt])
            rstdap = bass.AP(b.rmsv, hp * 16 + 8 + s * 2, [[32, 128], [1, 2], [0, 256]])
            e_orm = P.op("dve", lambda v, rstdap=rstdap: v.tensor_tensor(
                b.sf32[:, 1536:2048].rearrange("p (h n) -> p h n", h=2),
                b.sf32[:, 1024:1536].rearrange("p (h n) -> p h n", h=2),
                rstdap, ALU.mult), after=_f([e_rc, st.get("sl3_rd")])),
            e_orm = e_orm[0]
            mm = None
            for i in range(4):
                mm = P.op("tensor", lambda pe, i=i: pe.transpose(
                    self.bank(7)[:, i * 128:(i + 1) * 128],
                    b.sf32[:, 1536 + i * 128:1536 + (i + 1) * 128], b.idf[:, :]),
                    after=_f([e_orm, st["bank"][7]]) if i == 0 else (), inc=(i == 3))
            st["sl3_rd"] = mm
            dst = bass.AP(b.o_fm, hp * 2048 + s * 128,
                          [[4096, 128], [1024, 2], [512, 2], [1, 128]])
            e_ofm = P.op("dve", lambda v, dst=dst: v.tensor_copy(
                dst, bass.AP(self.PSB, 3 * 512, [[2048, 128], [256, 2], [128, 2], [1, 128]])),
                after=[mm])
            st["bank"][7] = e_ofm
            st["ofm_ev"] = e_ofm
        e_s32 = P.op("act", lambda a, hp=hp: a.activation(
            b.S32[:, hp * 2048:(hp + 1) * 2048], self.PSA[:, :], AF.Copy),
            after=_f([e_supd]))
        st["s32_ev"][hp] = e_s32
        for r in range(4):
            st["bank"][r] = e_s32
        st["scan_done"] = e_s32
        st["nmT_rd"] = e_s32
        st["nmn_rd"] = e_s32

    def wo_phase(self, P, t, st):
        b = self
        wo_ev = {}
        for half in range(2):
            par = half
            apd = [[D, 128], [128 * D, 8], [1, 512]]
            if t == 0:
                srcap = bass.AP(b.Wo_h, half * 512, apd)
            else:
                srcap = bass.AP(b.wq_s, 3 * D * D + half * 512, apd)
            dstv = b.wqb[:, par * 4096:(par + 1) * 4096].rearrange("p (c n) -> p c n", c=8)
            e_d = P.dma("pool", "wq%d" % par, dstv, srcap,
                        after=_f([st["wq_free"][par],
                                  st["wqw"][par]]))
            if t == 0:
                st["wqw"][par] = P.dma(
                    "sync", "ww%d" % par, bass.AP(b.wq_s, 3 * D * D + half * 512, apd),
                    dstv, after=[e_d])
            last = e_d
            for m in range(8):
                last = P.op("dve", lambda v, m=m, par=par: v.tensor_scalar(
                    b.wqb[:, par * 4096 + m * 512:par * 4096 + (m + 1) * 512],
                    b.wqb[:, par * 4096 + m * 512:par * 4096 + (m + 1) * 512],
                    b.normw[:, m % 2:m % 2 + 1], None, ALU.mult),
                    after=_f([e_d, st["wqw"][par]]) if m == 0 else ())
            wo_ev[half] = last
        mm = None
        for m in range(8):
            for tc in range(TC):
                for half in range(2):
                    bki = tc * 2 + half
                    aft = []
                    if m == 0:
                        aft = _f([wo_ev[half], st["bank"][bki], st.get("ofm_ev")])
                    mm = P.op("tensor", lambda pe, m=m, tc=tc, half=half, bki=bki: pe.matmul(
                        self.bank(bki), b.o_fm[:, m * 512 + tc * 128:m * 512 + tc * 128 + 128],
                        b.wqb[:, half * 4096 + m * 512:half * 4096 + (m + 1) * 512],
                        start=(m == 0), stop=(m == 7)),
                        after=aft, inc=(m == 7 and tc == 3 and half == 1))
        st["wq_free"][0] = mm
        st["wq_free"][1] = mm
        st["s_mm"] = mm

    def phase_combine(self, P, t, ev_g, st):
        b = self
        last = None
        for tc in range(TC):
            stage = b.xn_scr if tc % 2 == 0 else b.sq_scr
            e_gs = P.op("act", lambda a, tc=tc: a.activation(
                b.sf32[:, 0:1024], b.g_acc[:, tc * 1024:(tc + 1) * 1024], AF.Silu),
                after=_f([ev_g, st.get("acc_rd"), st.get("gs_rd")]))
            e_o = None
            for half in range(2):
                bki = tc * 2 + half
                sl = stage[:, half * 512:(half + 1) * 512]
                e_m = P.op("dve", lambda v, sl=sl, bki=bki, half=half: v.tensor_tensor(
                    sl, self.bank(bki), b.sf32[:, half * 512:(half + 1) * 512], ALU.mult),
                    after=_f([e_gs, st["s_mm"], st["stage_rd"][tc % 2],
                              st.get("xn_scr_rd") if tc % 2 == 0 else None]))
                e_o = P.op("dve", lambda v, sl=sl, tc=tc, half=half: v.tensor_tensor(
                    sl, sl, b.x_t[:, tc * 1024 + half * 512:tc * 1024 + half * 512 + 512],
                    ALU.add), after=[e_m])
                st["bank"][bki] = e_o
            st["gs_rd"] = e_o
            dst = bass.AP(b.y_h, (t * T + tc * 128) * D, [[D, 128], [1, D]])
            e_dma = P.dma("sync", "outa" if tc % 2 == 0 else "outb",
                          dst, stage[:, 0:1024], after=[e_o])
            st["out"].append(e_dma)
            st["stage_rd"][tc % 2] = e_dma
            last = e_o
        return last


def build_nc():
    nc = bass.Bass("TRN2", target_bir_lowering=False)
    b = Builder(nc)
    with b.stack, nc.Block() as block:
        block.sync(lambda eng: b.walk(eng, "sync"))
        block.scalar(lambda eng: b.walk(eng, "act"))
        block.vector(lambda eng: b.walk(eng, "dve"))
        block.tensor(lambda eng: b.walk(eng, "tensor"))
        block.gpsimd(lambda eng: b.walk(eng, "pool"))
    return nc, b


def kernel(**inputs):
    x = np.ascontiguousarray(inputs["L0_attn_hidden_states"])
    nm = {"L0_attn_Wq": "Wq", "L0_attn_Wk": "Wk", "L0_attn_Wv": "Wv",
          "L0_attn_conv_q": "conv_q", "L0_attn_conv_k": "conv_k",
          "L0_attn_conv_v": "conv_v", "L0_attn_Wb": "Wb", "L0_attn_Wg": "Wg",
          "L0_attn_bg": "bg", "L0_attn_Wh": "Wh", "L0_attn_bh": "bh",
          "L0_attn_norm_w": "norm_w", "L0_attn_Wo": "Wo",
          "L0_ffn_router_w": "router_w", "L0_ffn_up_w": "up_w",
          "L0_ffn_down_w": "down_w"}
    weights = {v: np.ascontiguousarray(inputs[k]) for k, v in nm.items()}
    nc, b = build_nc()
    in_maps = []
    for core in range(8):
        m = {"x": x[core]}
        m.update(weights)
        in_maps.append(m)
    res = run_bass_kernel_spmd(nc, in_maps, core_ids=list(range(8)))
    return np.stack([r["y"] for r in res.results], axis=0)


# version 03
# speedup vs baseline: 1.4319x; 1.3428x over previous
import numpy as np
from contextlib import ExitStack

import concourse.bass as bass
import concourse.mybir as mybir
from concourse.bass_utils import run_bass_kernel_spmd

F32 = mybir.dt.float32
BF16 = mybir.dt.bfloat16
AF = mybir.ActivationFunctionType
ALU = mybir.AluOpType
AX = mybir.AxisListType

L, D, NH, DK, DV, CHK, NE, HF = 8192, 1024, 4, 256, 256, 32, 8, 4096
T = 512
NT = L // T
TC = T // 128
PHASE = 9
SCANSUB = 9


def _f(evts):
    return [e for e in evts if e is not None]


class Walk:
    def __init__(self, b, eng_obj, eng_name):
        self.b = b
        self.eng = eng_obj
        self.name = eng_name
        self.cnt = {k: 0 for k in b.semnames}
        self.last_wait = {k: -1 for k in b.semnames}

    def _wait(self, evt):
        if evt is None:
            return
        sem, val = evt
        if self.last_wait[sem] >= val:
            return
        self.last_wait[sem] = val
        self.eng.wait_ge(self.b.sems[sem], val)

    def op(self, engine, fn, after=(), inc=True):
        evt = None
        if inc:
            self.cnt[engine] += 1
            evt = (engine, self.cnt[engine])
        if engine == self.name:
            for e in after:
                self._wait(e)
            inst = fn(self.eng)
            if inc:
                inst.then_inc(self.b.sems[engine], 1)
        return evt

    def dma(self, engine, sem, out, in_, after=()):
        self.cnt[sem] += 16
        evt = (sem, self.cnt[sem])
        if engine == self.name:
            for e in after:
                self._wait(e)
            self.eng.dma_start(out=out, in_=in_).then_inc(self.b.sems[sem], 16)
        return evt

    def wait(self, engine, evts):
        if engine == self.name:
            for e in evts:
                self._wait(e)


class Builder:
    semnames = ("tensor", "dve", "act", "pool", "sync", "dinit", "xin",
                "outa", "outb", "up0", "up1", "dn0", "dn1", "wq0", "wq1", "dinig",
                "uw0", "uw1", "dw0", "dw1", "ww0", "ww1")

    def __init__(self, nc):
        self.nc = nc
        self.stack = ExitStack()
        s = self.stack
        sb = lambda name, shape, dt=F32: s.enter_context(nc.sbuf_tensor(name, shape, dt))
        dram = lambda name, shape: nc.dram_tensor(name, shape, F32, kind="ExternalInput")
        self.x_h = dram("x", [L, D])
        self.y_h = nc.dram_tensor("y", [L, D], F32, kind="ExternalOutput")
        self.Wq_h, self.Wk_h, self.Wv_h = dram("Wq", [D, D]), dram("Wk", [D, D]), dram("Wv", [D, D])
        self.cq_h, self.ck_h, self.cv_h = dram("conv_q", [D, 4]), dram("conv_k", [D, 4]), dram("conv_v", [D, 4])
        self.Wb_h, self.Wg_h, self.Wh_h = dram("Wb", [D, 8]), dram("Wg", [D, 4]), dram("Wh", [D, 4])
        self.bg_h, self.bh_h = dram("bg", [4]), dram("bh", [4])
        self.normw_h, self.Wo_h = dram("norm_w", [256]), dram("Wo", [D, D])
        self.rwt_h = dram("router_w", [D, 8])
        self.up_h, self.dn_h = dram("up_w", [NE, D, HF]), dram("down_w", [NE, HF, D])
        self.up_s = nc.dram_tensor("up_s", [NE, D, HF], BF16)
        self.dn_s = nc.dram_tensor("dn_s", [NE, HF, D], BF16)
        self.wq_s = nc.dram_tensor("wq_s", [4, D, D], BF16)
        self.idf = sb("idf", [128, 128])
        self.onesf = sb("onesf", [128, 128])
        self.maskSneg = sb("maskSneg", [128, 32])
        self.maskIncl = sb("maskIncl", [128, 32])
        self.i32 = sb("i32", [128, 32])
        self.c2 = sb("c2", [128, 4])
        self.bc2 = sb("bc2", [2, 256])
        self.convt = sb("convt", [128, 96])
        self.wcat = sb("wcat", [128, 192], BF16)
        self.bgt = sb("bgt", [128, 8])
        self.normw = sb("normw", [128, 2])
        self.ceps = sb("ceps", [128, 2])
        self.x_t = sb("x_t", [128, 4096])
        self.wqb = sb("wqb", [128, 2 * 4096], BF16)
        self.upb = sb("upb", [128, 2 * 2048], BF16)
        self.dnb = sb("dnb", [128, 2 * 2048], BF16)
        self.ln_s = sb("ln_s", [128, 20])
        self.xn_scr = sb("xn_scr", [128, 1024])
        self.sq_scr = sb("sq_scr", [128, 1024])
        self.xn_fm = sb("xn_fm", [128, 4096], BF16)
        self.gh = sb("gh", [128, 32 * 512], BF16)
        self.pre = self.gh
        self.g_acc = sb("g_acc", [128, 4096])
        self.gat = sb("gat", [128, 64])
        self.beta_tm = sb("beta_tm", [128, 32])
        self.w3 = sb("w3", [128, 48])
        self.w3n = sb("w3n", [128, 32])
        self.rsc = sb("rsc", [128, 80])
        self.srt = sb("srt", [128, 32])
        self.rw_tm = sb("rw_tm", [128, 32])
        self.tails = sb("tails", [128, 72], BF16)
        self.qbf = sb("qbf", [128, 2048], BF16)
        self.kbf = sb("kbf", [128, 2048], BF16)
        self.sf32 = sb("sf32", [128, 2048])
        self.k_tm = sb("k_tm", [128, 2048], BF16)
        self.v_tm = sb("v_tm", [128, 2048], BF16)
        self.nmn = sb("nmn", [128, 2 * 1024])
        self.nmB = sb("nmB", [128, 7 * 1024], BF16)
        self.nmA = sb("nmA", [128, 1024], BF16)
        self.nmR = sb("nmR", [128, 1024], BF16)
        self.nmT = sb("nmT", [128, 2 * 1024], BF16)
        self.wm_sb = sb("wm_sb", [128, 1024], BF16)
        self.u0s = sb("u0s", [128, 1024], BF16)
        self.u_sb = sb("u_sb", [128, 1024], BF16)
        self.usum = sb("usum", [128, 512], BF16)
        self.a_sb = sb("a_sb", [128, 256], BF16)
        self.S_bf = sb("S_bf", [128, 2048], BF16)
        self.S32 = sb("S32", [128, 4096])
        self.o_fm = sb("o_fm", [128, 4096], BF16)
        self.rmsv = sb("rmsv", [128, 32])
        self.PSA = s.enter_context(nc.psum_tensor("PSA", [128, 2048], F32))
        self.PSB = s.enter_context(nc.psum_tensor("PSB", [128, 2048], F32))
        self.sems = {k: s.enter_context(nc.semaphore("sem_" + k)) for k in self.semnames}

    def bank(self, i, lo=0, n=512):
        t = self.PSA if i < 4 else self.PSB
        return t[:, (i % 4) * 512 + lo:(i % 4) * 512 + lo + n]

    def walk(self, eng_obj, eng_name):
        P = Walk(self, eng_obj, eng_name)
        self.program(P)

    def program(self, P):
        st = {"bank": [None] * 8, "out": [], "gh_rd": None,
              "stage_rd": [None, None], "scan_done": None, "s32_ev": [None, None],
              "wq_free": [None, None], "up_free": [None, None],
              "dn_free": [None, None], "upw": [None, None], "dnw": [None, None],
              "wqw": [None, None]}
        st["init"] = self.init_consts(P)
        ev_x = self.load_x(P, 0, after=st["init"])
        for t in range(NT):
            ev_ln = self.phase_ln(P, t, ev_x, st)
            ev_g = ev_ln
            if PHASE >= 2:
                ev_g = self.phase_ffn(P, t, ev_ln, st)
            if PHASE >= 3:
                self.phase_attn(P, t, ev_ln, st)
            if PHASE >= 9:
                ev_cmb = self.phase_combine(P, t, ev_g, st)
            else:
                dst = bass.AP(self.y_h, t * T * D, [[D, 128], [1, D]])
                ev_cmb = P.dma("sync", "outa", dst, self.x_t[:, 0:1024],
                               after=_f([ev_g, st.get("s_mm"), st["stage_rd"][0]]))
                st["out"].append(ev_cmb)
                st["stage_rd"][0] = ev_cmb
            if t + 1 < NT:
                ev_x = self.load_x(P, t + 1, after=[ev_cmb])
        P.wait("sync", st["out"][-2:])

    def load_x(self, P, t, after=()):
        xv = bass.AP(self.x_h, t * T * D, [[D, 128], [128 * D, TC], [1, D]])
        return P.dma("sync", "xin", self.x_t[:, :].rearrange("p (tc d) -> p tc d", tc=TC),
                     xv, after=_f(after))

    def init_consts(self, P):
        b = self
        ev = []
        ch = None
        chg = None
        e1 = P.op("pool", lambda g: g.memset(b.onesf[:, :], 1.0))
        ev.append(P.op("pool", lambda g: g.affine_select(
            b.idf[:, :], b.onesf[:, :], [[-1, 128]], ALU.is_equal, 0.0,
            base=0, channel_multiplier=1), after=[e1]))
        for tsr, fill, cmp_, cm, stp in [(b.maskSneg, -1.0, ALU.is_gt, 1, -1),
                                         (b.maskIncl, 1.0, ALU.is_ge, -1, 1),
                                         (b.i32, 1.0, ALU.is_equal, 1, -1)]:
            e2 = P.op("pool", lambda g, t_=tsr, f=fill: g.memset(t_[:, :], f))
            e3 = P.op("pool", lambda g, t_=tsr, c=cmp_, cm_=cm, s_=stp: g.affine_select(
                t_[0:32, :], t_[0:32, :], [[s_, 32]], c, 0.0, base=0,
                channel_multiplier=cm_), after=[e2])
            chg = P.dma("pool", "dinig", tsr[32:64, :], tsr[0:32, :], after=_f([e3, chg]))
            chg = P.dma("pool", "dinig", tsr[64:128, :], tsr[0:64, :], after=[chg])
        e4 = P.op("pool", lambda g: g.memset(b.c2[:, :], 0.0))
        e5 = P.op("pool", lambda g: g.memset(b.c2[:, 0:1], 1.0), after=[e4])
        ev.append(P.op("pool", lambda g: g.memset(b.c2[:, 3:4], 1.0), after=[e5]))
        e6 = P.op("pool", lambda g: g.memset(b.bc2[:, :], 0.0))
        e7 = P.op("pool", lambda g: g.memset(b.bc2[0:1, 0:128], 1.0), after=[e6])
        chg = P.dma("pool", "dinig", b.bc2[1:2, 128:256], b.bc2[0:1, 0:128],
                    after=_f([e7, chg]))
        ev.append(P.op("pool", lambda g: g.memset(b.tails[:, :], 0.0)))
        ev.append(P.op("pool", lambda g: g.memset(b.ceps[:, 0:1], 1e-5)))
        ev.append(P.op("pool", lambda g: g.memset(b.ceps[:, 1:2], 1e-12)))
        for i, h in enumerate([b.cq_h, b.ck_h, b.cv_h]):
            dst = bass.AP(b.convt, i * 32, [[96, 128], [4, 8], [1, 4]])
            ch = P.dma("sync", "dinit", dst,
                       h.ap().rearrange("(c p) k -> p c k", p=128), after=[ch])
        for hsrc, o0, wdt in [(b.Wb_h, 0, 8), (b.Wg_h, 8, 4), (b.Wh_h, 12, 4), (b.rwt_h, 16, 8)]:
            dst = bass.AP(b.wcat, o0, [[192, 128], [24, 8], [1, wdt]])
            chg = P.dma("pool", "dinig", dst,
                        hsrc.ap().rearrange("(c p) k -> p c k", p=128), after=[chg])
        ch = P.dma("sync", "dinit", bass.AP(b.bgt, 0, [[8, 1], [1, 4]]),
                   b.bg_h.ap()[None, :], after=[ch])
        ch = P.dma("sync", "dinit", bass.AP(b.bgt, 4, [[8, 1], [1, 4]]),
                   b.bh_h.ap()[None, :], after=[ch])
        nrep = 1
        while nrep < 128:
            hi = min(2 * nrep, 128)
            ch = P.dma("sync", "dinit", b.bgt[nrep:hi, :], b.bgt[0:hi - nrep, :],
                       after=[ch])
            nrep = hi
        for j in range(2):
            ch = P.dma("sync", "dinit", b.normw[:, j:j + 1],
                       b.normw_h.ap()[j * 128:(j + 1) * 128][:, None], after=[ch])
        ev.append(ch)
        ev.append(chg)
        return _f(ev)

    def phase_ln(self, P, t, ev_x, st):
        b = self
        last_evac = None
        for tc in range(TC):
            xc = b.x_t[:, tc * 1024:(tc + 1) * 1024]
            cS, cQ, cM, cR, cT = tc, 4 + tc, 8 + tc, 12 + tc, 16 + tc
            first = _f([st["stage_rd"][0], st["stage_rd"][1]]) if tc == 0 else []
            e_sum = P.op("dve", lambda v, xc=xc, cS=cS: v.reduce_sum(
                b.ln_s[:, cS:cS + 1], xc, AX.X), after=[ev_x] + first)
            e_sq = P.op("act", lambda a, xc=xc, cQ=cQ: a.activation(
                b.sq_scr[:, :], xc, AF.Square, accum_out=b.ln_s[:, cQ:cQ + 1]),
                after=[ev_x] + first)
            e_mu = P.op("dve", lambda v, cS=cS, cM=cM: v.tensor_scalar(
                b.ln_s[:, cM:cM + 1], b.ln_s[:, cS:cS + 1], 1.0 / D, None, ALU.mult),
                after=[e_sum])
            e_m2 = P.op("dve", lambda v, cM=cM, cT=cT: v.tensor_tensor(
                b.ln_s[:, cT:cT + 1], b.ln_s[:, cM:cM + 1], b.ln_s[:, cM:cM + 1],
                ALU.mult), after=[e_mu])
            e_qd = P.op("dve", lambda v, cQ=cQ, cR=cR: v.tensor_scalar(
                b.ln_s[:, cR:cR + 1], b.ln_s[:, cQ:cQ + 1], 1.0 / D, None, ALU.mult),
                after=[e_sq])
            e_var = P.op("dve", lambda v, cR=cR, cT=cT: v.tensor_sub(
                b.ln_s[:, cR:cR + 1], b.ln_s[:, cR:cR + 1], b.ln_s[:, cT:cT + 1]),
                after=[e_qd, e_m2])
            e_sd = P.op("act", lambda a, cR=cR: a.activation(
                b.ln_s[:, cR:cR + 1], b.ln_s[:, cR:cR + 1], AF.Sqrt, bias=b.ceps[:, 0:1],
                scale=1.0), after=[e_var])
            e_rs = P.op("dve", lambda v, cR=cR: v.reciprocal(
                b.ln_s[:, cR:cR + 1], b.ln_s[:, cR:cR + 1]), after=[e_sd])
            e_xn = P.op("dve", lambda v, xc=xc, cM=cM, cR=cR: v.tensor_scalar(
                b.xn_scr[:, :], xc, b.ln_s[:, cM:cM + 1], b.ln_s[:, cR:cR + 1],
                ALU.subtract, ALU.mult),
                after=[e_rs] + (_f([st["stage_rd"][0], st.get("xn_scr_rd")]) if tc == 0 else []))
            for half in range(2):
                bk = 4 + half
                mm = None
                for j in range(4):
                    dd = half * 4 + j
                    mm = P.op("tensor", lambda pe, dd=dd, bk=bk, j=j: pe.transpose(
                        self.bank(bk)[:, j * 128:(j + 1) * 128],
                        b.xn_scr[:, dd * 128:(dd + 1) * 128], b.idf[:, :]),
                        after=_f([e_xn, st["bank"][bk]] + st["init"]) if j == 0 else (),
                        inc=(j == 3))
                dst = bass.AP(b.xn_fm, half * 2048 + tc * 128,
                              [[4096, 128], [512, 4], [1, 128]])
                e_ev = P.op("dve", lambda v, bk=bk, dst=dst: v.tensor_copy(
                    dst, self.bank(bk)[:, 0:512].rearrange("p (c n) -> p c n", c=4)),
                    after=[mm])
                st["bank"][bk] = e_ev
                last_evac = e_ev
            st["xn_scr_rd"] = mm
        return last_evac

    def phase_ffn(self, P, t, ev_ln, st):
        b = self
        mmg = None
        for s in range(TC):
            for k in range(8):
                aft = _f([ev_ln, st["bank"][6]]) if (s == 0 and k == 0) else ()
                mmg = P.op("tensor", lambda pe, s=s, k=k: pe.matmul(
                    self.bank(6, s * 24, 24),
                    b.xn_fm[:, k * 512 + s * 128:k * 512 + s * 128 + 128],
                    b.wcat[:, k * 24:(k + 1) * 24], start=(k == 0), stop=(k == 7)),
                    after=aft, inc=(k == 7 and s == 3))
        e_bet = P.op("act", lambda a: a.activation(
            b.beta_tm[:, :].rearrange("p (s c) -> p s c", s=4),
            bass.AP(self.PSB, 2 * 512, [[2048, 128], [24, 4], [1, 8]]), AF.Sigmoid),
            after=[mmg])
        e_gat = P.op("dve", lambda v: v.tensor_copy(
            b.gat[:, :].rearrange("p (s c) -> p s c", s=4),
            bass.AP(self.PSB, 2 * 512 + 8, [[2048, 128], [24, 4], [1, 16]])),
            after=[mmg])
        st["bank"][6] = e_gat
        st["beta_ev"] = e_bet
        gv = bass.AP(b.gat, 0, [[64, 128], [16, 4], [1, 4]])
        hv = bass.AP(b.gat, 4, [[64, 128], [16, 4], [1, 4]])
        bgap = bass.AP(b.bgt, 0, [[8, 128], [0, 4], [1, 4]])
        bhap = bass.AP(b.bgt, 4, [[8, 128], [0, 4], [1, 4]])
        e1 = P.op("dve", lambda v: v.tensor_tensor(gv, gv, bgap, ALU.add), after=[e_gat])
        e2 = P.op("dve", lambda v: v.tensor_tensor(hv, hv, bhap, ALU.add), after=[e1])
        m_ = b.rsc[:, 0:16]
        m_v = m_.rearrange("p (s c) -> p s c", s=4)
        e3 = P.op("dve", lambda v: v.tensor_tensor(m_v, gv, hv, ALU.max), after=[e2])
        e4 = P.op("dve", lambda v: v.tensor_scalar_max(m_, m_, 0.0), after=[e3])
        eg_, ehg = b.rsc[:, 16:32], b.rsc[:, 32:48]
        e5 = P.op("dve", lambda v: v.tensor_sub(
            eg_.rearrange("p (s c) -> p s c", s=4), gv, m_v), after=[e4])
        e6 = P.op("dve", lambda v: v.tensor_sub(
            ehg.rearrange("p (s c) -> p s c", s=4), hv, m_v), after=[e5])
        wl, wg, wr = b.w3[:, 0:16], b.w3[:, 16:32], b.w3[:, 32:48]
        e7 = P.op("act", lambda a: a.activation(wl, eg_, AF.Exp), after=[e5])
        e8 = P.op("act", lambda a: a.activation(wg, ehg, AF.Exp), after=[e6])
        e9 = P.op("act", lambda a: a.activation(wr, m_, AF.Exp, scale=-1.0), after=[e6])
        den = b.rsc[:, 48:64]
        e10 = P.op("dve", lambda v: v.tensor_add(den, wl, wg), after=[e7, e8])
        e11 = P.op("dve", lambda v: v.tensor_add(den, den, wr), after=[e10, e9])
        e12 = P.op("dve", lambda v: v.reciprocal(den, den), after=[e11])
        e13 = P.op("dve", lambda v: v.tensor_mul(wl, wl, den), after=[e12])
        e14 = P.op("dve", lambda v: v.tensor_mul(wg, wg, den), after=[e13])
        e15 = P.op("dve", lambda v: v.tensor_mul(wr, wr, den), after=[e14])
        e16 = P.op("dve", lambda v: v.tensor_scalar(
            b.w3n[:, 0:16], wl, -1.0, None, ALU.mult), after=[e13])
        e17 = P.op("dve", lambda v: v.tensor_scalar(
            b.w3n[:, 16:32], wg, -1.0, None, ALU.mult), after=[e14])
        st["w3_ev"] = [e15, e16, e17]
        rv = bass.AP(b.gat, 8, [[64, 128], [16, 4], [1, 8]])
        m8, dn8, ts4, rd2 = (b.rsc[:, 64:68], b.rsc[:, 68:72],
                             b.rsc[:, 72:76], b.rsc[:, 76:80])
        r1 = P.op("dve", lambda v: v.reduce_max(m8, rv, AX.X), after=[e_gat])
        subs = []
        for s in range(TC):
            subs.append(P.op("dve", lambda v, s=s: v.tensor_scalar(
                b.sq_scr[:, s * 8:(s + 1) * 8], b.gat[:, s * 16 + 8:s * 16 + 16],
                m8[:, s:s + 1], None, ALU.subtract), after=[r1]))
        r2 = P.op("act", lambda a: a.activation(
            b.rw_tm[:, :], b.sq_scr[:, 0:32], AF.Exp), after=subs)
        r3 = P.op("dve", lambda v: v.reduce_sum(
            dn8, b.rw_tm[:, :].rearrange("p (s c) -> p s c", s=4), AX.X), after=[r2])
        r4 = P.op("dve", lambda v: v.reciprocal(dn8, dn8), after=[r3])
        dnap = bass.AP(b.rsc, 68, [[80, 128], [1, 4], [0, 8]])
        r5 = P.op("dve", lambda v: v.tensor_tensor(
            b.rw_tm[:, :].rearrange("p (s c) -> p s c", s=4),
            b.rw_tm[:, :].rearrange("p (s c) -> p s c", s=4), dnap, ALU.mult),
            after=[r4])
        srts = []
        for s in range(TC):
            srts.append(P.op("dve", lambda v, s=s: v.max(
                b.srt[:, s * 8:(s + 1) * 8], b.rw_tm[:, s * 8:(s + 1) * 8]),
                after=[r5]))
        r6 = P.op("dve", lambda v: v.reduce_sum(
            ts4, bass.AP(b.srt, 0, [[32, 128], [8, 4], [1, 4]]), AX.X), after=srts)
        r7 = P.op("dve", lambda v: v.reciprocal(rd2, ts4), after=[r6])
        keeps = []
        for s in range(TC):
            keeps.append(P.op("dve", lambda v, s=s: v.tensor_scalar(
                b.sq_scr[:, s * 8:(s + 1) * 8], b.rw_tm[:, s * 8:(s + 1) * 8],
                b.srt[:, s * 8 + 3:s * 8 + 4], None, ALU.is_ge), after=[r7]))
        r8 = P.op("dve", lambda v: v.tensor_mul(
            b.rw_tm[:, :], b.rw_tm[:, :], b.sq_scr[:, 0:32]), after=keeps)
        rd2ap = bass.AP(b.rsc, 76, [[80, 128], [1, 4], [0, 8]])
        e_rw = P.op("dve", lambda v: v.tensor_tensor(
            b.rw_tm[:, :].rearrange("p (s c) -> p s c", s=4),
            b.rw_tm[:, :].rearrange("p (s c) -> p s c", s=4), rd2ap, ALU.mult),
            after=[r8])
        st["rw_ev"] = e_rw
        return self.experts(P, t, ev_ln, e_rw, st)

    def experts(self, P, t, ev_ln, e_rw, st):
        b = self
        upv = lambda par: b.upb[:, par * 2048:(par + 1) * 2048]
        dnv = lambda par: b.dnb[:, par * 2048:(par + 1) * 2048]
        e_up, e_dn = {}, {}

        def fetch_up(e, he):
            par = (e * 16 + he) % 2
            apd = [[HF, 128], [128 * HF, 8], [1, 256]]
            off = e * D * HF + he * 256
            srcap = bass.AP(b.up_h if t == 0 else b.up_s, off, apd)
            e_up[(e, he)] = P.dma("pool", "up%d" % par,
                                  upv(par).rearrange("p (c n) -> p c n", c=8),
                                  srcap,
                                  after=_f([st["up_free"][par],
                                            st["upw"][par]]))
            if t == 0:
                st["upw"][par] = P.dma(
                    "sync", "uw%d" % par, bass.AP(b.up_s, off, apd),
                    upv(par).rearrange("p (c n) -> p c n", c=8),
                    after=[e_up[(e, he)]])

        def fetch_dn(e, ke):
            par = (e * 16 + ke) % 2
            apd = [[D, 128], [128 * D, 2], [1, 1024]]
            off = e * HF * D + ke * 256 * D
            srcap = bass.AP(b.dn_h if t == 0 else b.dn_s, off, apd)
            e_dn[(e, ke)] = P.dma("pool", "dn%d" % par,
                                  dnv(par).rearrange("p (c n) -> p c n", c=2),
                                  srcap,
                                  after=_f([st["dn_free"][par],
                                            st["dnw"][par]]))
            if t == 0:
                st["dnw"][par] = P.dma(
                    "sync", "dw%d" % par, bass.AP(b.dn_s, off, apd),
                    dnv(par).rearrange("p (c n) -> p c n", c=2),
                    after=[e_dn[(e, ke)]])

        fetch_up(0, 0)
        fetch_up(0, 1)
        e_gacc = None
        for e in range(NE):
            gelu_ev = {}
            for he in range(16):
                par = (e * 16 + he) % 2
                for kk in range(2):
                    k = he * 2 + kk
                    bk = 4 + (k % 2)
                    mm = None
                    for dd in range(8):
                        aft = _f([e_up[(e, he)], st["bank"][bk], ev_ln]) if dd == 0 else ()
                        mm = P.op("tensor", lambda pe, dd=dd, bk=bk, par=par, kk=kk: pe.matmul(
                            self.bank(bk),
                            upv(par)[:, dd * 256 + kk * 128:dd * 256 + kk * 128 + 128],
                            b.xn_fm[:, dd * 512:(dd + 1) * 512],
                            start=(dd == 0), stop=(dd == 7)),
                            after=aft, inc=(dd == 7))
                    if kk == 1:
                        st["up_free"][par] = mm
                    gaft = [mm] + ([st["gh_rd"], st.get("tail_ev")] if k == 0 else [])
                    gelu_ev[k] = P.op("act", lambda a, k=k, bk=bk: a.activation(
                        b.gh[:, k * 512:(k + 1) * 512], self.bank(bk),
                        AF.Gelu_apprx_tanh), after=_f(gaft))
                    st["bank"][bk] = gelu_ev[k]
                if he + 2 < 16:
                    fetch_up(e, he + 2)
                elif e + 1 < NE:
                    fetch_up(e + 1, he - 14)
                if he == 15:
                    fetch_dn(e, 0)
                    fetch_dn(e, 1)
            mm = None
            for ke in range(16):
                par = (e * 16 + ke) % 2
                for kk in range(2):
                    k = ke * 2 + kk
                    for tc in range(TC):
                        for half in range(2):
                            bki = tc * 2 + half
                            aft = []
                            if tc == 0 and half == 0:
                                aft = [e_dn[(e, ke)], gelu_ev[k]]
                            if k == 0:
                                aft.append(st["bank"][bki])
                            mm = P.op("tensor",
                                      lambda pe, k=k, tc=tc, half=half, par=par, kk=kk, bki=bki:
                                      pe.matmul(
                                          self.bank(bki),
                                          b.gh[:, k * 512 + tc * 128:k * 512 + tc * 128 + 128],
                                          dnv(par)[:, kk * 1024 + half * 512:kk * 1024 + half * 512 + 512],
                                          start=(k == 0), stop=(k == 31)),
                                      after=_f(aft), inc=(kk == 1 and tc == 3 and half == 1))
                    if kk == 1:
                        st["dn_free"][par] = mm
                if ke + 2 < 16:
                    fetch_dn(e, ke + 2)
            st["gh_rd"] = mm
            for tc in range(TC):
                for half in range(2):
                    bki = tc * 2 + half
                    dst = b.g_acc[:, tc * 1024 + half * 512:tc * 1024 + half * 512 + 512]
                    rwc = b.rw_tm[:, tc * 8 + e:tc * 8 + e + 1]
                    if e == 0:
                        ev2 = P.op("dve", lambda v, dst=dst, rwc=rwc, bki=bki: v.tensor_scalar(
                            dst, self.bank(bki), rwc, None, ALU.mult),
                            after=_f([mm, e_rw]))
                    else:
                        ev2 = P.op("dve", lambda v, dst=dst, rwc=rwc, bki=bki:
                                   v.scalar_tensor_tensor(
                                       dst, self.bank(bki), rwc, dst, ALU.mult, ALU.add),
                                   after=[mm])
                    st["bank"][bki] = ev2
                    e_gacc = ev2
        return e_gacc

    def phase_attn(self, P, t, ev_ln, st):
        for hp in range(2):
            self.proj_phase(P, t, hp, ev_ln, st)
            if PHASE >= 4:
                self.scan_phase(P, t, hp, st)
        if PHASE >= 5:
            self.wo_phase(P, t, st)

    def proj_phase(self, P, t, hp, ev_ln, st):
        b = self
        srcs = [b.Wq_h, b.Wk_h, b.Wv_h]
        wq_ev = {}

        def fetch_w(slot, h_src, col0, mat):
            par = slot % 2
            dst = b.wqb[:, par * 4096:(par + 1) * 4096].rearrange("p (c n) -> p c n", c=8)
            apd = [[D, 128], [128 * D, 8], [1, 512]]
            if t == 0:
                srcap = bass.AP(h_src, col0, apd)
            else:
                srcap = bass.AP(b.wq_s, mat * D * D + col0, apd)
            wq_ev[slot] = P.dma("pool", "wq%d" % par, dst, srcap,
                                after=_f([st["wq_free"][par],
                                          st["wqw"][par]]))
            if t == 0:
                st["wqw"][par] = P.dma(
                    "sync", "ww%d" % par, bass.AP(b.wq_s, mat * D * D + col0, apd),
                    dst, after=[wq_ev[slot]])

        base = hp * 3
        if hp == 0:
            fetch_w(0, srcs[0], 0, 0)
        else:
            wq_ev[base] = st.pop("wq_next")
        for pj in range(3):
            slot = base + pj
            par = slot % 2
            if pj + 1 < 3:
                fetch_w(slot + 1, srcs[pj + 1], hp * 512, pj + 1)
            elif hp == 0:
                fetch_w(slot + 1, srcs[0], 512, 0)
                st["wq_next"] = wq_ev[slot + 1]
            e_pre = {}
            for m in range(4):
                bk = 4 + (m % 2)
                mm = None
                for dd in range(8):
                    aft = _f([wq_ev[slot], st["bank"][bk], ev_ln]) if dd == 0 else ()
                    mm = P.op("tensor", lambda pe, m=m, dd=dd, bk=bk, par=par: pe.matmul(
                        self.bank(bk),
                        b.wqb[:, par * 4096 + dd * 512 + m * 128:par * 4096 + dd * 512 + m * 128 + 128],
                        b.xn_fm[:, dd * 512:(dd + 1) * 512],
                        start=(dd == 0), stop=(dd == 7)), after=aft, inc=(dd == 7))
                if m == 3:
                    st["wq_free"][par] = mm
                pcol = (pj * 4 + m) * 515
                aft2 = [mm]
                if pj == 0 and m == 0:
                    aft2 += _f([st["scan_done"], st.get("rw_ev"), st.get("gh_rd")])
                e_pre[m] = P.op("act", lambda a, pcol=pcol, bk=bk: a.activation(
                    b.pre[:, pcol + 3:pcol + 515], self.bank(bk), AF.Copy),
                    after=_f(aft2))
                st["bank"][bk] = e_pre[m]
            e_sil = {}
            for m in range(4):
                mg = hp * 4 + m
                pcol = (pj * 4 + m) * 515
                tcol = (pj * 8 + mg) * 3
                acc = b.sf32[:, 0:512]
                tmp = b.sf32[:, 512:1024]
                tapc = lambda j, pj=pj, mg=mg: b.convt[:, pj * 32 + mg * 4 + j:pj * 32 + mg * 4 + j + 1]
                e_ti = P.op("pool", lambda g, pcol=pcol, tcol=tcol: g.tensor_copy(
                    b.pre[:, pcol:pcol + 3], b.tails[:, tcol:tcol + 3]),
                    after=_f([e_pre[m], st.get("tail_ev")] + st["init"]))
                e_c = P.op("dve", lambda v, pcol=pcol, tapc=tapc, acc=acc: v.tensor_scalar(
                    acc, b.pre[:, pcol + 3:pcol + 515], tapc(3), None, ALU.mult),
                    after=_f([e_ti, st.get("acc_rd")]))
                for j in (2, 1, 0):
                    e_c = P.op("dve", lambda v, pcol=pcol, j=j, tapc=tapc, acc=acc:
                               v.scalar_tensor_tensor(
                                   acc, b.pre[:, pcol + j:pcol + j + 512], tapc(j),
                                   acc, ALU.mult, ALU.add), after=[e_c])
                st["tail_ev"] = P.op("pool", lambda g, pcol=pcol, tcol=tcol: g.tensor_copy(
                    b.tails[:, tcol:tcol + 3], b.pre[:, pcol + 512:pcol + 515]),
                    after=[e_c])
                if pj < 2:
                    dstb = [b.qbf, b.kbf][pj]
                    e_sil[m] = P.op("act", lambda a, m=m, dstb=dstb, acc=acc: a.activation(
                        dstb[:, m * 512:(m + 1) * 512], acc, AF.Silu),
                        after=_f([e_c, st.get("qk_rd")]))
                    st["acc_rd"] = e_sil[m]
                else:
                    e_sv = P.op("act", lambda a, acc=acc: a.activation(
                        b.sf32[:, 1024:1536], acc, AF.Silu),
                        after=_f([e_c, st.get("sl2_rd")]))
                    st["acc_rd"] = e_sv
                    self.kv_transpose(P, st, b.sf32[:, 1024:1536], True, m // 2, m % 2, e_sv)
            if pj < 2:
                self.l2_phase(P, hp, pj, e_sil, st)

    def l2_phase(self, P, hp, pj, e_sil, st):
        b = self
        dstb = [b.qbf, b.kbf][pj]
        rs = self.PSB[0:2, 3 * 512:4 * 512]
        mm = None
        for m in range(4):
            e_q = P.op("act", lambda a, m=m, dstb=dstb: a.activation(
                b.sq_scr[:, 0:512], dstb[:, m * 512:(m + 1) * 512], AF.Square),
                after=_f([e_sil[m]] + ([mm] if mm else []) +
                         ([st.get("rw_ev")] if m == 0 else [])))
            aft = _f([e_q] + ([st["bank"][7]] if m == 0 else []))
            mm = P.op("tensor", lambda pe, m=m: pe.matmul(
                rs, b.c2[:, (m // 2) * 2:(m // 2) * 2 + 2], b.sq_scr[:, 0:512],
                start=(m == 0), stop=(m == 3)), after=aft, inc=True)
        rn = b.sf32[0:2, 1536:2048]
        e_rt = P.op("act", lambda a, rn=rn: a.activation(
            rn, rs, AF.Sqrt, bias=b.ceps[0:2, 1:2], scale=1.0),
            after=_f([mm, st.get("sl3_rd")]))
        e_rc = P.op("dve", lambda v, rn=rn: v.reciprocal(rn, rn), after=[e_rt])
        st["bank"][7] = e_rc
        mmb_last = None
        for h in range(2):
            bcast = self.bank(6)
            mmb = P.op("tensor", lambda pe, h=h, rn=rn: pe.matmul(
                bcast, b.bc2[:, h * 128:(h + 1) * 128], rn, start=True, stop=True),
                after=_f([e_rc, st["bank"][6]]), inc=True)
            mmb_last = mmb
            last = None
            for m in (2 * h, 2 * h + 1):
                e_n = P.op("dve", lambda v, m=m, dstb=dstb, bcast=bcast: v.tensor_tensor(
                    dstb[:, m * 512:(m + 1) * 512], dstb[:, m * 512:(m + 1) * 512],
                    bcast, ALU.mult), after=[mmb])
                last = e_n
                if pj == 1:
                    e_fc = P.op("dve", lambda v, m=m, dstb=dstb: v.tensor_copy(
                        b.sf32[:, 1024:1536], dstb[:, m * 512:(m + 1) * 512]),
                        after=_f([e_n, st.get("sl2_rd")]))
                    self.kv_transpose(P, st, b.sf32[:, 1024:1536], False, h, m % 2, e_fc)
            st["bank"][6] = last
        st["sl3_rd"] = mmb_last
        st["qk_rd"] = None
        if pj == 0:
            st["qbf_ev"] = last
        else:
            st["kbf_ev"] = last

    def kv_transpose(self, P, st, src, is_v, h, vh, after):
        b = self
        bk = 7
        mm = None
        for ts_ in range(4):
            mm = P.op("tensor", lambda pe, ts_=ts_, bk=bk, src=src: pe.transpose(
                self.bank(bk)[:, ts_ * 128:(ts_ + 1) * 128],
                src[:, ts_ * 128:(ts_ + 1) * 128], b.idf[:, :]),
                after=_f([after, st["bank"][bk]]) if ts_ == 0 else (), inc=(ts_ == 3))
        st["sl2_rd"] = mm
        dstt = b.v_tm if is_v else b.k_tm
        dst = bass.AP(dstt, h * 256 + vh * 128, [[2048, 128], [512, 4], [1, 128]])
        e_ev = P.op("dve", lambda v, dst=dst, bk=bk: v.tensor_copy(
            dst, self.bank(bk)[:, 0:512].rearrange("p (c n) -> p c n", c=4)),
            after=[mm])
        st["bank"][bk] = e_ev
        st["tm_ev"] = e_ev

    def _ap32(self, tsr, freesz, base, g, col, n=32):
        return bass.AP(tsr, 32 * g * freesz + base + col, [[freesz, 32], [1, n]])

    def scan_phase(self, P, t, hp, st):
        b = self
        if t > 0:
            seed = None
            for r in range(4):
                aft = _f([st["bank"][0], st["bank"][1], st["bank"][2], st["bank"][3],
                          st["s32_ev"][hp]]) if r == 0 else ()
                seed = P.op("tensor", lambda pe, r=r: pe.matmul(
                    self.PSA[:, r * 512:(r + 1) * 512], b.idf[:, :],
                    b.S32[:, hp * 2048 + r * 512:hp * 2048 + (r + 1) * 512],
                    start=True, stop=False, skip_group_check=True),
                    after=aft, inc=(r == 3))
            e_supd = seed
        else:
            e_supd = None
        kq = _f([st.get("kbf_ev"), st.get("qbf_ev"), st.get("tm_ev")])
        e_z = P.op("dve", lambda v: v.memset(self.PSB[:, 0:1024], 0.0),
                   after=_f([st["bank"][4], st["bank"][5]]))
        st["bank"][4] = e_z
        st["bank"][5] = e_z
        mmA = None
        for h in range(2):
            for s in range(TC):
                for g in range(4):
                    c = s * 4 + g
                    for kc in range(2):
                        aft = _f([st["bank"][4], st["bank"][5]] + kq) \
                            if (h == 0 and s == 0 and g == 0 and kc == 0) else ()
                        col = (h * 4 + s) * 128 + g * 32
                        mmA = P.op("tensor", lambda pe, h=h, c=c, g=g, kc=kc, col=col: pe.matmul(
                            self.PSB[32 * g:32 * g + 32, col:col + 32],
                            b.kbf[:, (2 * h + kc) * 512 + c * 32:(2 * h + kc) * 512 + c * 32 + 32],
                            b.kbf[:, (2 * h + kc) * 512 + c * 32:(2 * h + kc) * 512 + c * 32 + 32],
                            start=(kc == 0), stop=(kc == 1), tile_position=(0, 32 * g)),
                            after=aft, inc=(kc == 1))
        mmS = None
        for h in range(2):
            for s in range(TC):
                for g in range(4):
                    c = s * 4 + g
                    for kc in range(2):
                        aft = _f([st["bank"][6]]) if (h == 0 and s == 0 and g == 0 and kc == 0) else ()
                        col = 2 * 512 + (h * 4 + s) * 32
                        mmS = P.op("tensor", lambda pe, h=h, c=c, g=g, kc=kc, col=col: pe.matmul(
                            self.PSB[32 * g:32 * g + 32, col:col + 32],
                            b.kbf[:, (2 * h + kc) * 512 + c * 32:(2 * h + kc) * 512 + c * 32 + 32],
                            b.qbf[:, (2 * h + kc) * 512 + c * 32:(2 * h + kc) * 512 + c * 32 + 32],
                            start=(kc == 0), stop=(kc == 1), tile_position=(0, 32 * g)),
                            after=aft, inc=(kc == 1))
        if SCANSUB < 2:
            st["scan_done"] = mmS
            st["nmT_rd"] = mmS
            st["nmn_rd"] = mmS
            st["s32_ev"][hp] = mmS
            return
        e_a = P.op("dve", lambda v: v.tensor_tensor(
            b.a_sb[:, :].rearrange("p (q j) -> p q j", q=8),
            bass.AP(self.PSB, 2 * 512, [[2048, 128], [32, 8], [1, 32]]),
            bass.AP(b.maskIncl, 0, [[32, 128], [0, 8], [1, 32]]), ALU.mult),
            after=[mmS])
        st["bank"][6] = e_a
        e_As = {}
        e_msk = None
        for scan in range(2):
            e_msk = P.op("dve", lambda v, scan=scan: v.tensor_tensor(
                bass.AP(b.nmn, scan * 1024, [[2048, 128], [128, 8], [32, 4], [1, 32]]),
                bass.AP(self.PSB, 0, [[2048, 128], [128, 8], [32, 4], [1, 32]]),
                bass.AP(b.maskSneg, 0, [[32, 128], [0, 8], [0, 4], [1, 32]]), ALU.mult),
                after=_f([mmA] + ([st.get("nmn_rd")] if scan == 0 else [])))
            bap = bass.AP(b.beta_tm, scan * 4 + 2 * hp,
                          [[32, 128], [1, 2], [8, 4], [0, 128]])
            e_As[scan] = P.op("dve", lambda v, scan=scan, bap=bap: v.tensor_tensor(
                bass.AP(b.nmn, scan * 1024, [[2048, 128], [512, 2], [128, 4], [1, 128]]),
                bass.AP(b.nmn, scan * 1024, [[2048, 128], [512, 2], [128, 4], [1, 128]]),
                bap, ALU.mult),
                after=_f([e_msk, st.get("beta_ev")]))
        st["bank"][4] = e_As[1]
        st["bank"][5] = e_As[1]
        if SCANSUB < 3:
            st["scan_done"] = e_As[1]
            st["nmT_rd"] = e_As[1]
            st["nmn_rd"] = e_As[1]
            st["s32_ev"][hp] = e_As[1]
            return
        for scan in range(2):
            e_abf = P.op("dve", lambda v, scan=scan: v.tensor_copy(
                b.nmB[:, 0:1024], b.nmn[:, scan * 1024:(scan + 1) * 1024]),
                after=_f([e_As[scan], st.get("nmB_rd")]))
            mm = None
            for blk in range(8):
                bk = 4 + blk // 4
                mm = P.op("tensor", lambda pe, blk=blk, bk=bk, scan=scan: pe.transpose(
                    self.bank(bk)[:, (blk % 4) * 128:(blk % 4) * 128 + 128],
                    b.nmn[:, scan * 1024 + blk * 128:scan * 1024 + (blk + 1) * 128],
                    b.idf[:, :]),
                    after=_f([e_As[scan], st["bank"][4], st["bank"][5]]) if blk == 0 else (),
                    inc=(blk == 7))
            e_Ap = P.op("dve", lambda v: v.tensor_copy(
                b.nmA[:, :].rearrange("p (u n) -> p u n", u=2),
                bass.AP(self.PSB, 0, [[2048, 128], [512, 2], [1, 512]])),
                after=_f([mm, st.get("nmA_rd")]))
            st["bank"][4] = e_Ap
            st["bank"][5] = e_Ap
            if SCANSUB < 4:
                st["scan_done"] = e_Ap
                st["nmT_rd"] = e_Ap
                st["nmn_rd"] = e_Ap
                st["s32_ev"][hp] = e_Ap
                st["ttil_%d" % scan] = e_Ap
                continue
            def bap32(slot, g, col, n=32):
                return self._ap32(b.nmB, 7168, slot * 1024, g, col, n)

            def aap32(g, col, n=32):
                return self._ap32(b.nmA, 1024, 0, g, col, n)

            chain = [(1, "A'", 0), (5, 0, "A'"), (2, 5, 1), (6, 1, 5),
                     (3, 6, 2), (5, 2, 6), (4, 5, 3)]
            prev_ev = [e_abf, e_Ap]
            for (dsts, lt, rh) in chain:
                mm = None
                for blk in range(8):
                    def sl_of(x, blk=0):
                        t_ = b.nmA if x == "A'" else None
                        return None
                    lt_ap = (b.nmA if lt == "A'" else
                             b.nmB[:, lt * 1024:(lt + 1) * 1024])
                    rh_ap = (b.nmA if rh == "A'" else
                             b.nmB[:, rh * 1024:(rh + 1) * 1024])
                    mm = P.op("tensor",
                              lambda pe, blk=blk, lt_ap=lt_ap, rh_ap=rh_ap: pe.matmul(
                                  self.PSB[:, blk * 128:(blk + 1) * 128],
                                  lt_ap[:, blk * 128:(blk + 1) * 128],
                                  rh_ap[:, blk * 128:(blk + 1) * 128],
                                  start=True, stop=True),
                              after=_f(prev_ev + [st["bank"][4], st["bank"][5]])
                              if blk == 0 else (),
                              inc=(blk == 7))
                e_ev = P.op("dve", lambda v, dsts=dsts: v.tensor_copy(
                    b.nmB[:, dsts * 1024:(dsts + 1) * 1024].rearrange("p (u n) -> p u n", u=2),
                    bass.AP(self.PSB, 0, [[2048, 128], [512, 2], [1, 512]])),
                    after=[mm])
                st["bank"][4] = e_ev
                st["bank"][5] = e_ev
                prev_ev = [e_ev]
            if SCANSUB < 5:
                st["scan_done"] = prev_ev[-1]
                st["nmT_rd"] = prev_ev[-1]
                st["nmn_rd"] = prev_ev[-1]
                st["s32_ev"][hp] = prev_ev[-1]
                st["ttil_%d" % scan] = prev_ev[-1]
                continue
            e_r = P.op("dve", lambda v: v.tensor_add(
                bass.AP(b.nmR, 0, [[1024, 128], [128, 8], [1, 128]]),
                bass.AP(b.nmA, 0, [[1024, 128], [128, 8], [1, 128]]),
                bass.AP(b.idf, 0, [[128, 128], [0, 8], [1, 128]])),
                after=_f([e_Ap, st.get("nmR_rd")]))
            first_r = True
            for bslot in (1, 2, 3, 4):
                mm = None
                for blk in range(8):
                    aft = ()
                    if blk == 0:
                        aft = _f([e_r] + ([st["bank"][6], st["bank"][7]]
                                          if first_r else []))
                    mm = P.op("tensor",
                              lambda pe, blk=blk, bslot=bslot: pe.matmul(
                                  self.PSB[:, 1024 + blk * 128:1024 + (blk + 1) * 128],
                                  b.nmB[:, bslot * 1024 + blk * 128:bslot * 1024 + (blk + 1) * 128],
                                  b.nmR[:, blk * 128:(blk + 1) * 128],
                                  start=True, stop=True),
                              after=aft, inc=(blk == 7))
                first_r = False
                e_r = P.op("dve", lambda v: v.tensor_add(
                    b.nmR[:, :], b.nmR[:, :], self.PSB[:, 1024:2048]), after=[mm])
            st["bank"][6] = e_r
            st["bank"][7] = e_r
            bap = bass.AP(b.beta_tm, scan * 4 + 2 * hp,
                          [[32, 128], [1, 2], [8, 4], [0, 128]])
            e_tt = P.op("dve", lambda v, scan=scan, bap=bap: v.tensor_tensor(
                bass.AP(b.nmT, scan * 1024, [[2048, 128], [512, 2], [128, 4], [1, 128]]),
                bass.AP(b.nmR, 0, [[1024, 128], [128, 8], [32, 4], [1, 128]])
                if False else
                bass.AP(b.nmR, 0, [[1024, 128], [512, 2], [128, 4], [1, 128]]),
                bap, ALU.mult),
                after=_f([e_r, st.get("nmT_rd")]))
            st["nmR_rd"] = e_tt
            st["nmA_rd"] = e_r
            st["nmB_rd"] = e_r
            st["ttil_%d" % scan] = e_tt
        if SCANSUB < 6:
            ev = st["ttil_1"]
            st["scan_done"] = ev
            st["nmT_rd"] = ev
            st["nmn_rd"] = ev
            st["s32_ev"][hp] = ev
            return
        e_prev_chunk = None
        for s in range(TC):
            ev_u0, ev_wm = {}, {}
            for scan in range(2):
                bk = 6 + scan
                mm = None
                for h in range(2):
                    mm = P.op("tensor", lambda pe, h=h, scan=scan, bk=bk, s=s: pe.matmul(
                        self.bank(bk)[:, h * 256:h * 256 + 256],
                        b.nmT[:, scan * 1024 + (h * 4 + s) * 128:
                              scan * 1024 + (h * 4 + s) * 128 + 128],
                        b.v_tm[:, (s * 2 + h) * 256:(s * 2 + h) * 256 + 256],
                        start=True, stop=True),
                        after=_f([st["ttil_%d" % scan], st["bank"][bk]])
                        if h == 0 else (), inc=(h == 1))
                wap = bass.AP(b.w3, scan * 16 + s * 4 + 2 * hp,
                              [[48, 128], [1, 2], [0, 256]])
                ev_u0[scan] = P.op("dve", lambda v, scan=scan, bk=bk, wap=wap: v.tensor_tensor(
                    b.u0s[:, scan * 512:(scan + 1) * 512].rearrange("p (h n) -> p h n", h=2),
                    self.bank(bk).rearrange("p (h n) -> p h n", h=2), wap, ALU.mult),
                    after=_f([mm] + st.get("w3_ev", [])))
                st["bank"][bk] = ev_u0[scan]
                mm2 = None
                for h in range(2):
                    for kc in range(2):
                        mm2 = P.op("tensor",
                                   lambda pe, scan=scan, bk=bk, s=s, h=h, kc=kc: pe.matmul(
                                       self.bank(bk)[:, (h * 2 + kc) * 128:(h * 2 + kc) * 128 + 128],
                                       b.k_tm[:, (s * 2 + h) * 256 + kc * 128:
                                              (s * 2 + h) * 256 + kc * 128 + 128],
                                       b.nmT[:, scan * 1024 + (h * 4 + s) * 128:
                                             scan * 1024 + (h * 4 + s) * 128 + 128],
                                       start=True, stop=True),
                                   after=_f([ev_u0[scan]]) if (h == 0 and kc == 0) else (),
                                   inc=(h == 1 and kc == 1))
                ev_wm[scan] = P.op("dve", lambda v, scan=scan, bk=bk: v.tensor_copy(
                    b.wm_sb[:, scan * 512:(scan + 1) * 512], self.bank(bk)), after=[mm2])
                st["bank"][bk] = ev_wm[scan]
            if SCANSUB < 7:
                ev = ev_wm[1]
                st["scan_done"] = ev
                st["nmT_rd"] = ev
                st["nmn_rd"] = ev
                st["s32_ev"][hp] = ev
                continue
            for g in range(4):
                c = s * 4 + g
                if not (t == 0 and c == 0):
                    e_snap = P.op("act", lambda a: a.activation(
                        b.S_bf[:, :], self.PSA[:, :], AF.Copy),
                        after=_f([e_supd, e_prev_chunk, st.get("o_mm_prev")]))
                else:
                    e_snap = None
                e_wmS = None
                if e_snap is not None:
                    mm = None
                    for h in range(2):
                        for scan in range(2):
                            for kc in range(2):
                                col4 = (h * 2 + scan) * 256
                                mm = P.op("tensor",
                                          lambda pe, h=h, scan=scan, kc=kc, g=g, col4=col4, s=s:
                                          pe.matmul(
                                              self.PSB[32 * g:32 * g + 32, col4:col4 + 256],
                                              b.wm_sb[:, scan * 512 + (h * 2 + kc) * 128 + g * 32:
                                                      scan * 512 + (h * 2 + kc) * 128 + g * 32 + 32],
                                              b.S_bf[:, (h * 2 + scan) * 512 + kc * 256:
                                                     (h * 2 + scan) * 512 + kc * 256 + 256],
                                              start=(kc == 0), stop=(kc == 1),
                                              tile_position=(0, 32 * g)),
                                          after=_f([e_snap, ev_wm[0], ev_wm[1],
                                                    st["bank"][4], st["bank"][5]])
                                          if (h == 0 and scan == 0 and kc == 0) else (),
                                          inc=(h == 1 and scan == 1 and kc == 1))
                    e_wmS = mm
                e_us = []
                for h in range(2):
                    for scan in range(2):
                        dst = self._ap32(b.u_sb, 1024, (h * 2 + scan) * 256, g, 0, 256)
                        src0 = self._ap32(b.u0s, 1024, scan * 512 + h * 256, g, 0, 256)
                        if e_wmS is None:
                            e_us.append(P.op("dve", lambda v, dst=dst, src0=src0: v.tensor_copy(
                                dst, src0), after=_f([ev_u0[scan]])))
                        else:
                            wnc = b.w3n[32 * g:32 * g + 32,
                                        scan * 16 + s * 4 + 2 * hp + h:
                                        scan * 16 + s * 4 + 2 * hp + h + 1]
                            psrc = self.PSB[32 * g:32 * g + 32,
                                            (h * 2 + scan) * 256:(h * 2 + scan) * 256 + 256]
                            e_us.append(P.op("dve",
                                             lambda v, dst=dst, src0=src0, wnc=wnc, psrc=psrc:
                                             v.scalar_tensor_tensor(dst, psrc, wnc, src0,
                                                                    ALU.mult, ALU.add),
                                             after=[e_wmS]))
                st["bank"][4] = e_us[-1]
                st["bank"][5] = e_us[-1]
                e_usum = P.op("dve", lambda v, g=g: v.tensor_add(
                    bass.AP(b.usum, 32 * g * 512, [[512, 32], [256, 2], [1, 256]]),
                    bass.AP(b.u_sb, 32 * g * 1024, [[1024, 32], [512, 2], [1, 256]]),
                    bass.AP(b.u_sb, 32 * g * 1024 + 256, [[1024, 32], [512, 2], [1, 256]])),
                    after=e_us)
                mm = None
                for h in range(2):
                    for scan in range(2):
                        for kc in range(2):
                            sstart = (t == 0 and c == 0 and kc == 0)
                            mm = P.op("tensor",
                                      lambda pe, h=h, scan=scan, kc=kc, g=g, s=s, sstart=sstart:
                                      pe.matmul(
                                          self.PSA[:, (h * 2 + scan) * 512 + kc * 256:
                                                   (h * 2 + scan) * 512 + kc * 256 + 256],
                                          self._ap32(b.k_tm, 2048, (s * 2 + h) * 256 + kc * 128, g, 0, 128),
                                          self._ap32(b.u_sb, 1024, (h * 2 + scan) * 256, g, 0, 256),
                                          start=sstart, stop=False,
                                          tile_position=(32 * g, 0),
                                          skip_group_check=True),
                                      after=_f([e_usum] + ([e_snap] if e_snap else []))
                                      if (h == 0 and scan == 0 and kc == 0) else (),
                                      inc=(h == 1 and scan == 1 and kc == 1))
                e_supd = mm
                e_prev_chunk = mm
                mmo = None
                for h in range(2):
                    if e_snap is not None:
                        for scan in range(2):
                            for kc in range(2):
                                aft = _f([e_snap] + ([st["bank"][6]] if (h == 0 and scan == 0 and kc == 0 and g == 0) else []))
                                mmo = P.op("tensor",
                                           lambda pe, h=h, scan=scan, kc=kc, g=g, c=c:
                                           pe.matmul(
                                               self.bank(6)[32 * g:32 * g + 32, h * 256:h * 256 + 256],
                                               b.qbf[:, (2 * h + kc) * 512 + c * 32:
                                                     (2 * h + kc) * 512 + c * 32 + 32],
                                               b.S_bf[:, (h * 2 + scan) * 512 + kc * 256:
                                                      (h * 2 + scan) * 512 + kc * 256 + 256],
                                               start=(scan == 0 and kc == 0), stop=False,
                                               tile_position=(0, 32 * g)),
                                           after=aft if (scan == 0 and kc == 0) else (),
                                           inc=False)
                        astart = False
                    else:
                        astart = True
                    aft = _f([e_usum] + ([st["bank"][6]] if (astart and h == 0 and g == 0) else []))
                    mmo = P.op("tensor", lambda pe, h=h, g=g, s=s, astart=astart: pe.matmul(
                        self.bank(6)[32 * g:32 * g + 32, h * 256:h * 256 + 256],
                        self._ap32(b.a_sb, 256, (h * 4 + s) * 32, g, 0),
                        bass.AP(b.usum, 32 * g * 512 + h * 256, [[512, 32], [1, 256]]),
                        start=astart, stop=True, tile_position=(32 * g, 32 * g)),
                        after=aft, inc=(h == 1))
                st["o_mm_%d" % g] = mmo
                st["o_mm_prev"] = mmo
            if SCANSUB < 8:
                ev = e_prev_chunk
                st["scan_done"] = ev
                st["nmT_rd"] = ev
                st["nmn_rd"] = ev
                st["s32_ev"][hp] = ev
                st["bank"][6] = st.get("o_mm_3")
                continue
            o_ev = _f([st.get("o_mm_%d" % g) for g in range(4)])
            wrap = bass.AP(b.w3, 32 + s * 4 + 2 * hp, [[48, 128], [1, 2], [0, 256]])
            e_t1 = P.op("dve", lambda v, s=s, wrap=wrap: v.tensor_tensor(
                b.sf32[:, 1024:1536].rearrange("p (h n) -> p h n", h=2),
                b.v_tm[:, s * 512:(s + 1) * 512].rearrange("p (h n) -> p h n", h=2),
                wrap, ALU.mult),
                after=_f([st.get("sl2_rd")] + st.get("w3_ev", [])))
            e_t2 = P.op("dve", lambda v: v.tensor_add(
                b.sf32[:, 1024:1536], b.sf32[:, 1024:1536], self.bank(6)),
                after=[e_t1] + o_ev)
            st["bank"][6] = e_t2
            st["sl2_rd"] = e_t2
            e_ms = []
            prev = e_t2
            for h in range(2):
                prev = P.op("act", lambda a, h=h, s=s, hp=hp: a.activation(
                    b.sq_scr[:, 0:256], b.sf32[:, 1024 + h * 256:1024 + h * 256 + 256],
                    AF.Square, accum_out=b.rmsv[:, hp * 16 + s * 2 + h:hp * 16 + s * 2 + h + 1]),
                    after=[prev])
                e_ms.append(prev)
            e_rt = P.op("act", lambda a, s=s, hp=hp: a.activation(
                b.rmsv[:, hp * 16 + 8 + s * 2:hp * 16 + 8 + s * 2 + 2],
                b.rmsv[:, hp * 16 + s * 2:hp * 16 + s * 2 + 2],
                AF.Sqrt, bias=b.ceps[:, 0:1], scale=1.0 / 256), after=e_ms)
            e_rc = P.op("dve", lambda v, s=s, hp=hp: v.reciprocal(
                b.rmsv[:, hp * 16 + 8 + s * 2:hp * 16 + 8 + s * 2 + 2],
                b.rmsv[:, hp * 16 + 8 + s * 2:hp * 16 + 8 + s * 2 + 2]), after=[e_rt])
            rstdap = bass.AP(b.rmsv, hp * 16 + 8 + s * 2, [[32, 128], [1, 2], [0, 256]])
            e_orm = P.op("dve", lambda v, rstdap=rstdap: v.tensor_tensor(
                b.sf32[:, 1536:2048].rearrange("p (h n) -> p h n", h=2),
                b.sf32[:, 1024:1536].rearrange("p (h n) -> p h n", h=2),
                rstdap, ALU.mult), after=_f([e_rc, st.get("sl3_rd")])),
            e_orm = e_orm[0]
            mm = None
            for i in range(4):
                mm = P.op("tensor", lambda pe, i=i: pe.transpose(
                    self.bank(7)[:, i * 128:(i + 1) * 128],
                    b.sf32[:, 1536 + i * 128:1536 + (i + 1) * 128], b.idf[:, :]),
                    after=_f([e_orm, st["bank"][7]]) if i == 0 else (), inc=(i == 3))
            st["sl3_rd"] = mm
            dst = bass.AP(b.o_fm, hp * 2048 + s * 128,
                          [[4096, 128], [1024, 2], [512, 2], [1, 128]])
            e_ofm = P.op("dve", lambda v, dst=dst: v.tensor_copy(
                dst, bass.AP(self.PSB, 3 * 512, [[2048, 128], [256, 2], [128, 2], [1, 128]])),
                after=[mm])
            st["bank"][7] = e_ofm
            st["ofm_ev"] = e_ofm
        e_s32 = P.op("act", lambda a, hp=hp: a.activation(
            b.S32[:, hp * 2048:(hp + 1) * 2048], self.PSA[:, :], AF.Copy),
            after=_f([e_supd]))
        st["s32_ev"][hp] = e_s32
        for r in range(4):
            st["bank"][r] = e_s32
        st["scan_done"] = e_s32
        st["nmT_rd"] = e_s32
        st["nmn_rd"] = e_s32

    def wo_phase(self, P, t, st):
        b = self
        wo_ev = {}
        for half in range(2):
            par = half
            apd = [[D, 128], [128 * D, 8], [1, 512]]
            if t == 0:
                srcap = bass.AP(b.Wo_h, half * 512, apd)
            else:
                srcap = bass.AP(b.wq_s, 3 * D * D + half * 512, apd)
            dstv = b.wqb[:, par * 4096:(par + 1) * 4096].rearrange("p (c n) -> p c n", c=8)
            e_d = P.dma("pool", "wq%d" % par, dstv, srcap,
                        after=_f([st["wq_free"][par],
                                  st["wqw"][par]]))
            if t == 0:
                st["wqw"][par] = P.dma(
                    "sync", "ww%d" % par, bass.AP(b.wq_s, 3 * D * D + half * 512, apd),
                    dstv, after=[e_d])
            last = e_d
            for m in range(8):
                last = P.op("dve", lambda v, m=m, par=par: v.tensor_scalar(
                    b.wqb[:, par * 4096 + m * 512:par * 4096 + (m + 1) * 512],
                    b.wqb[:, par * 4096 + m * 512:par * 4096 + (m + 1) * 512],
                    b.normw[:, m % 2:m % 2 + 1], None, ALU.mult),
                    after=_f([e_d, st["wqw"][par]]) if m == 0 else ())
            wo_ev[half] = last
        mm = None
        for m in range(8):
            for tc in range(TC):
                for half in range(2):
                    bki = tc * 2 + half
                    aft = []
                    if m == 0:
                        aft = _f([wo_ev[half], st["bank"][bki], st.get("ofm_ev")])
                    mm = P.op("tensor", lambda pe, m=m, tc=tc, half=half, bki=bki: pe.matmul(
                        self.bank(bki), b.o_fm[:, m * 512 + tc * 128:m * 512 + tc * 128 + 128],
                        b.wqb[:, half * 4096 + m * 512:half * 4096 + (m + 1) * 512],
                        start=(m == 0), stop=(m == 7)),
                        after=aft, inc=(m == 7 and tc == 3 and half == 1))
        st["wq_free"][0] = mm
        st["wq_free"][1] = mm
        st["s_mm"] = mm

    def phase_combine(self, P, t, ev_g, st):
        b = self
        last = None
        for tc in range(TC):
            stage = b.xn_scr if tc % 2 == 0 else b.sq_scr
            e_gs = P.op("act", lambda a, tc=tc: a.activation(
                b.sf32[:, 0:1024], b.g_acc[:, tc * 1024:(tc + 1) * 1024], AF.Silu),
                after=_f([ev_g, st.get("acc_rd"), st.get("gs_rd")]))
            e_o = None
            for half in range(2):
                bki = tc * 2 + half
                sl = stage[:, half * 512:(half + 1) * 512]
                e_m = P.op("dve", lambda v, sl=sl, bki=bki, half=half: v.tensor_tensor(
                    sl, self.bank(bki), b.sf32[:, half * 512:(half + 1) * 512], ALU.mult),
                    after=_f([e_gs, st["s_mm"], st["stage_rd"][tc % 2],
                              st.get("xn_scr_rd") if tc % 2 == 0 else None]))
                e_o = P.op("dve", lambda v, sl=sl, tc=tc, half=half: v.tensor_tensor(
                    sl, sl, b.x_t[:, tc * 1024 + half * 512:tc * 1024 + half * 512 + 512],
                    ALU.add), after=[e_m])
                st["bank"][bki] = e_o
            st["gs_rd"] = e_o
            dst = bass.AP(b.y_h, (t * T + tc * 128) * D, [[D, 128], [1, D]])
            e_dma = P.dma("sync", "outa" if tc % 2 == 0 else "outb",
                          dst, stage[:, 0:1024], after=[e_o])
            st["out"].append(e_dma)
            st["stage_rd"][tc % 2] = e_dma
            last = e_o
        return last


def build_nc():
    nc = bass.Bass("TRN2", target_bir_lowering=False)
    b = Builder(nc)
    with b.stack, nc.Block() as block:
        block.sync(lambda eng: b.walk(eng, "sync"))
        block.scalar(lambda eng: b.walk(eng, "act"))
        block.vector(lambda eng: b.walk(eng, "dve"))
        block.tensor(lambda eng: b.walk(eng, "tensor"))
        block.gpsimd(lambda eng: b.walk(eng, "pool"))
    return nc, b


def kernel(**inputs):
    x = np.ascontiguousarray(inputs["L0_attn_hidden_states"])
    nm = {"L0_attn_Wq": "Wq", "L0_attn_Wk": "Wk", "L0_attn_Wv": "Wv",
          "L0_attn_conv_q": "conv_q", "L0_attn_conv_k": "conv_k",
          "L0_attn_conv_v": "conv_v", "L0_attn_Wb": "Wb", "L0_attn_Wg": "Wg",
          "L0_attn_bg": "bg", "L0_attn_Wh": "Wh", "L0_attn_bh": "bh",
          "L0_attn_norm_w": "norm_w", "L0_attn_Wo": "Wo",
          "L0_ffn_router_w": "router_w", "L0_ffn_up_w": "up_w",
          "L0_ffn_down_w": "down_w"}
    weights = {v: np.ascontiguousarray(inputs[k]) for k, v in nm.items()}
    nc, b = build_nc()
    in_maps = []
    for core in range(8):
        m = {"x": x[core]}
        m.update(weights)
        in_maps.append(m)
    res = run_bass_kernel_spmd(nc, in_maps, core_ids=list(range(8)))
    return np.stack([r["y"] for r in res.results], axis=0)


# version 04 (final)
# speedup vs baseline: 1.4365x; 1.0032x over previous
import numpy as np
from contextlib import ExitStack

import concourse.bass as bass
import concourse.mybir as mybir
from concourse.bass_utils import run_bass_kernel_spmd

F32 = mybir.dt.float32
BF16 = mybir.dt.bfloat16
AF = mybir.ActivationFunctionType
ALU = mybir.AluOpType
AX = mybir.AxisListType

L, D, NH, DK, DV, CHK, NE, HF = 8192, 1024, 4, 256, 256, 32, 8, 4096
T = 512
NT = L // T
TC = T // 128
PHASE = 9
SCANSUB = 9


def _f(evts):
    return [e for e in evts if e is not None]


class Walk:
    def __init__(self, b, eng_obj, eng_name):
        self.b = b
        self.eng = eng_obj
        self.name = eng_name
        self.cnt = {k: 0 for k in b.semnames}
        self.last_wait = {k: -1 for k in b.semnames}

    def _wait(self, evt):
        if evt is None:
            return
        sem, val = evt
        if self.last_wait[sem] >= val:
            return
        self.last_wait[sem] = val
        self.eng.wait_ge(self.b.sems[sem], val)

    def op(self, engine, fn, after=(), inc=True):
        evt = None
        if inc:
            self.cnt[engine] += 1
            evt = (engine, self.cnt[engine])
        if engine == self.name:
            for e in after:
                self._wait(e)
            inst = fn(self.eng)
            if inc:
                inst.then_inc(self.b.sems[engine], 1)
        return evt

    def dma(self, engine, sem, out, in_, after=()):
        self.cnt[sem] += 16
        evt = (sem, self.cnt[sem])
        if engine == self.name:
            for e in after:
                self._wait(e)
            self.eng.dma_start(out=out, in_=in_).then_inc(self.b.sems[sem], 16)
        return evt

    def wait(self, engine, evts):
        if engine == self.name:
            for e in evts:
                self._wait(e)


class Builder:
    semnames = ("tensor", "dve", "act", "pool", "sync", "dinit", "xin",
                "outa", "outb", "up0", "up1", "dn0", "dn1", "wq0", "wq1", "dinig",
                "uw0", "uw1", "dw0", "dw1", "ww0", "ww1")

    def __init__(self, nc):
        self.nc = nc
        self.stack = ExitStack()
        s = self.stack
        sb = lambda name, shape, dt=F32: s.enter_context(nc.sbuf_tensor(name, shape, dt))
        dram = lambda name, shape: nc.dram_tensor(name, shape, F32, kind="ExternalInput")
        self.x_h = dram("x", [L, D])
        self.y_h = nc.dram_tensor("y", [L, D], F32, kind="ExternalOutput")
        self.Wq_h, self.Wk_h, self.Wv_h = dram("Wq", [D, D]), dram("Wk", [D, D]), dram("Wv", [D, D])
        self.cq_h, self.ck_h, self.cv_h = dram("conv_q", [D, 4]), dram("conv_k", [D, 4]), dram("conv_v", [D, 4])
        self.Wb_h, self.Wg_h, self.Wh_h = dram("Wb", [D, 8]), dram("Wg", [D, 4]), dram("Wh", [D, 4])
        self.bg_h, self.bh_h = dram("bg", [4]), dram("bh", [4])
        self.normw_h, self.Wo_h = dram("norm_w", [256]), dram("Wo", [D, D])
        self.rwt_h = dram("router_w", [D, 8])
        self.up_h, self.dn_h = dram("up_w", [NE, D, HF]), dram("down_w", [NE, HF, D])
        self.up_s = nc.dram_tensor("up_s", [NE, D, HF], BF16)
        self.dn_s = nc.dram_tensor("dn_s", [NE, HF, D], BF16)
        self.wq_s = nc.dram_tensor("wq_s", [4, D, D], BF16)
        self.idf = sb("idf", [128, 128])
        self.onesf = sb("onesf", [128, 128])
        self.maskSneg = sb("maskSneg", [128, 32])
        self.maskIncl = sb("maskIncl", [128, 32])
        self.i32 = sb("i32", [128, 32])
        self.c2 = sb("c2", [128, 4])
        self.bc2 = sb("bc2", [2, 256])
        self.convt = sb("convt", [128, 96])
        self.wcat = sb("wcat", [128, 192], BF16)
        self.bgt = sb("bgt", [128, 8])
        self.normw = sb("normw", [128, 2])
        self.ceps = sb("ceps", [128, 2])
        self.x_t = sb("x_t", [128, 4096])
        self.wqb = sb("wqb", [128, 2 * 4096], BF16)
        self.upb = sb("upb", [128, 2 * 2048], BF16)
        self.dnb = sb("dnb", [128, 2 * 2048], BF16)
        self.ln_s = sb("ln_s", [128, 20])
        self.xn_scr = sb("xn_scr", [128, 1024])
        self.sq_scr = sb("sq_scr", [128, 1024])
        self.xn_fm = sb("xn_fm", [128, 4096], BF16)
        self.gh = sb("gh", [128, 32 * 512], BF16)
        self.pre = self.gh
        self.g_acc = sb("g_acc", [128, 4096])
        self.gat = sb("gat", [128, 64])
        self.beta_tm = sb("beta_tm", [128, 32])
        self.w3 = sb("w3", [128, 48])
        self.w3n = sb("w3n", [128, 32])
        self.rsc = sb("rsc", [128, 80])
        self.srt = sb("srt", [128, 32])
        self.rw_tm = sb("rw_tm", [128, 32])
        self.tails = sb("tails", [128, 72], BF16)
        self.qbf = sb("qbf", [128, 2048], BF16)
        self.kbf = sb("kbf", [128, 2048], BF16)
        self.sf32 = sb("sf32", [128, 2048])
        self.k_tm = sb("k_tm", [128, 2048], BF16)
        self.v_tm = sb("v_tm", [128, 2048], BF16)
        self.nmn = sb("nmn", [128, 2 * 1024])
        self.nmB = sb("nmB", [128, 7 * 1024], BF16)
        self.nmA = sb("nmA", [128, 1024], BF16)
        self.nmR = sb("nmR", [128, 1024], BF16)
        self.nmT = sb("nmT", [128, 2 * 1024], BF16)
        self.wm_sb = sb("wm_sb", [128, 1024], BF16)
        self.u0s = sb("u0s", [128, 1024], BF16)
        self.u_sb = sb("u_sb", [128, 1024], BF16)
        self.usum = sb("usum", [128, 512], BF16)
        self.a_sb = sb("a_sb", [128, 256], BF16)
        self.S_bf = sb("S_bf", [128, 2048], BF16)
        self.S32 = sb("S32", [128, 4096])
        self.o_fm = sb("o_fm", [128, 4096], BF16)
        self.rmsv = sb("rmsv", [128, 32])
        self.PSA = s.enter_context(nc.psum_tensor("PSA", [128, 2048], F32))
        self.PSB = s.enter_context(nc.psum_tensor("PSB", [128, 2048], F32))
        self.sems = {k: s.enter_context(nc.semaphore("sem_" + k)) for k in self.semnames}

    def bank(self, i, lo=0, n=512):
        t = self.PSA if i < 4 else self.PSB
        return t[:, (i % 4) * 512 + lo:(i % 4) * 512 + lo + n]

    def walk(self, eng_obj, eng_name):
        P = Walk(self, eng_obj, eng_name)
        self.program(P)

    def program(self, P):
        st = {"bank": [None] * 8, "out": [], "gh_rd": None,
              "stage_rd": [None, None], "scan_done": None, "s32_ev": [None, None],
              "wq_free": [None, None], "up_free": [None, None],
              "dn_free": [None, None], "upw": [None, None], "dnw": [None, None],
              "wqw": [None, None]}
        st["init"] = self.init_consts(P)
        ev_x = self.load_x(P, 0, after=st["init"])
        for t in range(NT):
            ev_ln = self.phase_ln(P, t, ev_x, st)
            ev_g = ev_ln
            if PHASE >= 2:
                ev_g = self.phase_ffn(P, t, ev_ln, st)
            if PHASE >= 3:
                self.phase_attn(P, t, ev_ln, st)
            if PHASE >= 9:
                ev_cmb = self.phase_combine(P, t, ev_g, st)
            else:
                dst = bass.AP(self.y_h, t * T * D, [[D, 128], [1, D]])
                ev_cmb = P.dma("sync", "outa", dst, self.x_t[:, 0:1024],
                               after=_f([ev_g, st.get("s_mm"), st["stage_rd"][0]]))
                st["out"].append(ev_cmb)
                st["stage_rd"][0] = ev_cmb
            if t + 1 < NT:
                ev_x = self.load_x(P, t + 1, after=[ev_cmb])
        P.wait("sync", st["out"][-2:])

    def load_x(self, P, t, after=()):
        xv = bass.AP(self.x_h, t * T * D, [[D, 128], [128 * D, TC], [1, D]])
        return P.dma("sync", "xin", self.x_t[:, :].rearrange("p (tc d) -> p tc d", tc=TC),
                     xv, after=_f(after))

    def init_consts(self, P):
        b = self
        ev = []
        ch = None
        chg = None
        e1 = P.op("pool", lambda g: g.memset(b.onesf[:, :], 1.0))
        ev.append(P.op("pool", lambda g: g.affine_select(
            b.idf[:, :], b.onesf[:, :], [[-1, 128]], ALU.is_equal, 0.0,
            base=0, channel_multiplier=1), after=[e1]))
        for tsr, fill, cmp_, cm, stp in [(b.maskSneg, -1.0, ALU.is_gt, 1, -1),
                                         (b.maskIncl, 1.0, ALU.is_ge, -1, 1),
                                         (b.i32, 1.0, ALU.is_equal, 1, -1)]:
            e2 = P.op("pool", lambda g, t_=tsr, f=fill: g.memset(t_[:, :], f))
            e3 = P.op("pool", lambda g, t_=tsr, c=cmp_, cm_=cm, s_=stp: g.affine_select(
                t_[0:32, :], t_[0:32, :], [[s_, 32]], c, 0.0, base=0,
                channel_multiplier=cm_), after=[e2])
            chg = P.dma("pool", "dinig", tsr[32:64, :], tsr[0:32, :], after=_f([e3, chg]))
            chg = P.dma("pool", "dinig", tsr[64:128, :], tsr[0:64, :], after=[chg])
        e4 = P.op("pool", lambda g: g.memset(b.c2[:, :], 0.0))
        e5 = P.op("pool", lambda g: g.memset(b.c2[:, 0:1], 1.0), after=[e4])
        ev.append(P.op("pool", lambda g: g.memset(b.c2[:, 3:4], 1.0), after=[e5]))
        e6 = P.op("pool", lambda g: g.memset(b.bc2[:, :], 0.0))
        e7 = P.op("pool", lambda g: g.memset(b.bc2[0:1, 0:128], 1.0), after=[e6])
        chg = P.dma("pool", "dinig", b.bc2[1:2, 128:256], b.bc2[0:1, 0:128],
                    after=_f([e7, chg]))
        ev.append(P.op("pool", lambda g: g.memset(b.tails[:, :], 0.0)))
        ev.append(P.op("pool", lambda g: g.memset(b.ceps[:, 0:1], 1e-5)))
        ev.append(P.op("pool", lambda g: g.memset(b.ceps[:, 1:2], 1e-12)))
        for i, h in enumerate([b.cq_h, b.ck_h, b.cv_h]):
            dst = bass.AP(b.convt, i * 32, [[96, 128], [4, 8], [1, 4]])
            ch = P.dma("sync", "dinit", dst,
                       h.ap().rearrange("(c p) k -> p c k", p=128), after=[ch])
        for hsrc, o0, wdt in [(b.Wb_h, 0, 8), (b.Wg_h, 8, 4), (b.Wh_h, 12, 4), (b.rwt_h, 16, 8)]:
            dst = bass.AP(b.wcat, o0, [[192, 128], [24, 8], [1, wdt]])
            chg = P.dma("pool", "dinig", dst,
                        hsrc.ap().rearrange("(c p) k -> p c k", p=128), after=[chg])
        ch = P.dma("sync", "dinit", bass.AP(b.bgt, 0, [[8, 1], [1, 4]]),
                   b.bg_h.ap()[None, :], after=[ch])
        ch = P.dma("sync", "dinit", bass.AP(b.bgt, 4, [[8, 1], [1, 4]]),
                   b.bh_h.ap()[None, :], after=[ch])
        nrep = 1
        while nrep < 128:
            hi = min(2 * nrep, 128)
            ch = P.dma("sync", "dinit", b.bgt[nrep:hi, :], b.bgt[0:hi - nrep, :],
                       after=[ch])
            nrep = hi
        for j in range(2):
            ch = P.dma("sync", "dinit", b.normw[:, j:j + 1],
                       b.normw_h.ap()[j * 128:(j + 1) * 128][:, None], after=[ch])
        ev.append(ch)
        ev.append(chg)
        return _f(ev)

    def phase_ln(self, P, t, ev_x, st):
        b = self
        last_evac = None
        for tc in range(TC):
            xc = b.x_t[:, tc * 1024:(tc + 1) * 1024]
            cS, cQ, cM, cR, cT = tc, 4 + tc, 8 + tc, 12 + tc, 16 + tc
            first = _f([st["stage_rd"][0], st["stage_rd"][1]]) if tc == 0 else []
            e_sum = P.op("dve", lambda v, xc=xc, cS=cS: v.reduce_sum(
                b.ln_s[:, cS:cS + 1], xc, AX.X), after=[ev_x] + first)
            e_sq = P.op("act", lambda a, xc=xc, cQ=cQ: a.activation(
                b.sq_scr[:, :], xc, AF.Square, accum_out=b.ln_s[:, cQ:cQ + 1]),
                after=[ev_x] + first)
            e_mu = P.op("dve", lambda v, cS=cS, cM=cM: v.tensor_scalar(
                b.ln_s[:, cM:cM + 1], b.ln_s[:, cS:cS + 1], 1.0 / D, None, ALU.mult),
                after=[e_sum])
            e_m2 = P.op("dve", lambda v, cM=cM, cT=cT: v.tensor_tensor(
                b.ln_s[:, cT:cT + 1], b.ln_s[:, cM:cM + 1], b.ln_s[:, cM:cM + 1],
                ALU.mult), after=[e_mu])
            e_qd = P.op("dve", lambda v, cQ=cQ, cR=cR: v.tensor_scalar(
                b.ln_s[:, cR:cR + 1], b.ln_s[:, cQ:cQ + 1], 1.0 / D, None, ALU.mult),
                after=[e_sq])
            e_var = P.op("dve", lambda v, cR=cR, cT=cT: v.tensor_sub(
                b.ln_s[:, cR:cR + 1], b.ln_s[:, cR:cR + 1], b.ln_s[:, cT:cT + 1]),
                after=[e_qd, e_m2])
            e_sd = P.op("act", lambda a, cR=cR: a.activation(
                b.ln_s[:, cR:cR + 1], b.ln_s[:, cR:cR + 1], AF.Sqrt, bias=b.ceps[:, 0:1],
                scale=1.0), after=[e_var])
            e_rs = P.op("dve", lambda v, cR=cR: v.reciprocal(
                b.ln_s[:, cR:cR + 1], b.ln_s[:, cR:cR + 1]), after=[e_sd])
            e_xn = P.op("dve", lambda v, xc=xc, cM=cM, cR=cR: v.tensor_scalar(
                b.xn_scr[:, :], xc, b.ln_s[:, cM:cM + 1], b.ln_s[:, cR:cR + 1],
                ALU.subtract, ALU.mult),
                after=[e_rs] + (_f([st["stage_rd"][0], st.get("xn_scr_rd")]) if tc == 0 else []))
            for half in range(2):
                bk = 4 + half
                mm = None
                for j in range(4):
                    dd = half * 4 + j
                    mm = P.op("tensor", lambda pe, dd=dd, bk=bk, j=j: pe.transpose(
                        self.bank(bk)[:, j * 128:(j + 1) * 128],
                        b.xn_scr[:, dd * 128:(dd + 1) * 128], b.idf[:, :]),
                        after=_f([e_xn, st["bank"][bk]] + st["init"]) if j == 0 else (),
                        inc=(j == 3))
                dst = bass.AP(b.xn_fm, half * 2048 + tc * 128,
                              [[4096, 128], [512, 4], [1, 128]])
                e_ev = P.op("dve", lambda v, bk=bk, dst=dst: v.tensor_copy(
                    dst, self.bank(bk)[:, 0:512].rearrange("p (c n) -> p c n", c=4)),
                    after=[mm])
                st["bank"][bk] = e_ev
                last_evac = e_ev
            st["xn_scr_rd"] = mm
        return last_evac

    def phase_ffn(self, P, t, ev_ln, st):
        b = self
        mmg = None
        for s in range(TC):
            for k in range(8):
                aft = _f([ev_ln, st["bank"][6]]) if (s == 0 and k == 0) else ()
                mmg = P.op("tensor", lambda pe, s=s, k=k: pe.matmul(
                    self.bank(6, s * 24, 24),
                    b.xn_fm[:, k * 512 + s * 128:k * 512 + s * 128 + 128],
                    b.wcat[:, k * 24:(k + 1) * 24], start=(k == 0), stop=(k == 7)),
                    after=aft, inc=(k == 7 and s == 3))
        e_bet = P.op("act", lambda a: a.activation(
            b.beta_tm[:, :].rearrange("p (s c) -> p s c", s=4),
            bass.AP(self.PSB, 2 * 512, [[2048, 128], [24, 4], [1, 8]]), AF.Sigmoid),
            after=[mmg])
        e_gat = P.op("dve", lambda v: v.tensor_copy(
            b.gat[:, :].rearrange("p (s c) -> p s c", s=4),
            bass.AP(self.PSB, 2 * 512 + 8, [[2048, 128], [24, 4], [1, 16]])),
            after=[mmg, e_bet])
        st["bank"][6] = e_gat
        st["beta_ev"] = e_bet
        gv = bass.AP(b.gat, 0, [[64, 128], [16, 4], [1, 4]])
        hv = bass.AP(b.gat, 4, [[64, 128], [16, 4], [1, 4]])
        bgap = bass.AP(b.bgt, 0, [[8, 128], [0, 4], [1, 4]])
        bhap = bass.AP(b.bgt, 4, [[8, 128], [0, 4], [1, 4]])
        e1 = P.op("dve", lambda v: v.tensor_tensor(gv, gv, bgap, ALU.add), after=[e_gat])
        e2 = P.op("dve", lambda v: v.tensor_tensor(hv, hv, bhap, ALU.add), after=[e1])
        m_ = b.rsc[:, 0:16]
        m_v = m_.rearrange("p (s c) -> p s c", s=4)
        e3 = P.op("dve", lambda v: v.tensor_tensor(m_v, gv, hv, ALU.max), after=[e2])
        e4 = P.op("dve", lambda v: v.tensor_scalar_max(m_, m_, 0.0), after=[e3])
        eg_, ehg = b.rsc[:, 16:32], b.rsc[:, 32:48]
        e5 = P.op("dve", lambda v: v.tensor_sub(
            eg_.rearrange("p (s c) -> p s c", s=4), gv, m_v), after=[e4])
        e6 = P.op("dve", lambda v: v.tensor_sub(
            ehg.rearrange("p (s c) -> p s c", s=4), hv, m_v), after=[e5])
        wl, wg, wr = b.w3[:, 0:16], b.w3[:, 16:32], b.w3[:, 32:48]
        e7 = P.op("act", lambda a: a.activation(wl, eg_, AF.Exp), after=[e5])
        e8 = P.op("act", lambda a: a.activation(wg, ehg, AF.Exp), after=[e6])
        e9 = P.op("act", lambda a: a.activation(wr, m_, AF.Exp, scale=-1.0), after=[e6])
        den = b.rsc[:, 48:64]
        e10 = P.op("dve", lambda v: v.tensor_add(den, wl, wg), after=[e7, e8])
        e11 = P.op("dve", lambda v: v.tensor_add(den, den, wr), after=[e10, e9])
        e12 = P.op("dve", lambda v: v.reciprocal(den, den), after=[e11])
        e13 = P.op("dve", lambda v: v.tensor_mul(wl, wl, den), after=[e12])
        e14 = P.op("dve", lambda v: v.tensor_mul(wg, wg, den), after=[e13])
        e15 = P.op("dve", lambda v: v.tensor_mul(wr, wr, den), after=[e14])
        e16 = P.op("dve", lambda v: v.tensor_scalar(
            b.w3n[:, 0:16], wl, -1.0, None, ALU.mult), after=[e13])
        e17 = P.op("dve", lambda v: v.tensor_scalar(
            b.w3n[:, 16:32], wg, -1.0, None, ALU.mult), after=[e14])
        st["w3_ev"] = [e15, e16, e17]
        rv = bass.AP(b.gat, 8, [[64, 128], [16, 4], [1, 8]])
        m8, dn8, ts4, rd2 = (b.rsc[:, 64:68], b.rsc[:, 68:72],
                             b.rsc[:, 72:76], b.rsc[:, 76:80])
        r1 = P.op("dve", lambda v: v.reduce_max(m8, rv, AX.X), after=[e_gat])
        subs = []
        for s in range(TC):
            subs.append(P.op("dve", lambda v, s=s: v.tensor_scalar(
                b.sq_scr[:, s * 8:(s + 1) * 8], b.gat[:, s * 16 + 8:s * 16 + 16],
                m8[:, s:s + 1], None, ALU.subtract), after=[r1]))
        r2 = P.op("act", lambda a: a.activation(
            b.rw_tm[:, :], b.sq_scr[:, 0:32], AF.Exp), after=subs)
        r3 = P.op("dve", lambda v: v.reduce_sum(
            dn8, b.rw_tm[:, :].rearrange("p (s c) -> p s c", s=4), AX.X), after=[r2])
        r4 = P.op("dve", lambda v: v.reciprocal(dn8, dn8), after=[r3])
        dnap = bass.AP(b.rsc, 68, [[80, 128], [1, 4], [0, 8]])
        r5 = P.op("dve", lambda v: v.tensor_tensor(
            b.rw_tm[:, :].rearrange("p (s c) -> p s c", s=4),
            b.rw_tm[:, :].rearrange("p (s c) -> p s c", s=4), dnap, ALU.mult),
            after=[r4])
        srts = []
        for s in range(TC):
            srts.append(P.op("dve", lambda v, s=s: v.max(
                b.srt[:, s * 8:(s + 1) * 8], b.rw_tm[:, s * 8:(s + 1) * 8]),
                after=[r5]))
        r6 = P.op("dve", lambda v: v.reduce_sum(
            ts4, bass.AP(b.srt, 0, [[32, 128], [8, 4], [1, 4]]), AX.X), after=srts)
        r7 = P.op("dve", lambda v: v.reciprocal(rd2, ts4), after=[r6])
        keeps = []
        for s in range(TC):
            keeps.append(P.op("dve", lambda v, s=s: v.tensor_scalar(
                b.sq_scr[:, s * 8:(s + 1) * 8], b.rw_tm[:, s * 8:(s + 1) * 8],
                b.srt[:, s * 8 + 3:s * 8 + 4], None, ALU.is_ge), after=[r7]))
        r8 = P.op("dve", lambda v: v.tensor_mul(
            b.rw_tm[:, :], b.rw_tm[:, :], b.sq_scr[:, 0:32]), after=keeps)
        rd2ap = bass.AP(b.rsc, 76, [[80, 128], [1, 4], [0, 8]])
        e_rw = P.op("dve", lambda v: v.tensor_tensor(
            b.rw_tm[:, :].rearrange("p (s c) -> p s c", s=4),
            b.rw_tm[:, :].rearrange("p (s c) -> p s c", s=4), rd2ap, ALU.mult),
            after=[r8])
        st["rw_ev"] = e_rw
        return self.experts(P, t, ev_ln, e_rw, st)

    def experts(self, P, t, ev_ln, e_rw, st):
        b = self
        upv = lambda par: b.upb[:, par * 2048:(par + 1) * 2048]
        dnv = lambda par: b.dnb[:, par * 2048:(par + 1) * 2048]
        e_up, e_dn = {}, {}

        def fetch_up(e, he):
            par = (e * 16 + he) % 2
            apd = [[HF, 128], [128 * HF, 8], [1, 256]]
            off = e * D * HF + he * 256
            srcap = bass.AP(b.up_h if t == 0 else b.up_s, off, apd)
            e_up[(e, he)] = P.dma("pool", "up%d" % par,
                                  upv(par).rearrange("p (c n) -> p c n", c=8),
                                  srcap,
                                  after=_f([st["up_free"][par],
                                            st["upw"][par]]))
            if t == 0:
                st["upw"][par] = P.dma(
                    "sync", "uw%d" % par, bass.AP(b.up_s, off, apd),
                    upv(par).rearrange("p (c n) -> p c n", c=8),
                    after=[e_up[(e, he)]])

        def fetch_dn(e, ke):
            par = (e * 16 + ke) % 2
            apd = [[D, 128], [128 * D, 2], [1, 1024]]
            off = e * HF * D + ke * 256 * D
            srcap = bass.AP(b.dn_h if t == 0 else b.dn_s, off, apd)
            e_dn[(e, ke)] = P.dma("pool", "dn%d" % par,
                                  dnv(par).rearrange("p (c n) -> p c n", c=2),
                                  srcap,
                                  after=_f([st["dn_free"][par],
                                            st["dnw"][par]]))
            if t == 0:
                st["dnw"][par] = P.dma(
                    "sync", "dw%d" % par, bass.AP(b.dn_s, off, apd),
                    dnv(par).rearrange("p (c n) -> p c n", c=2),
                    after=[e_dn[(e, ke)]])

        fetch_up(0, 0)
        fetch_up(0, 1)
        e_gacc = None
        for e in range(NE):
            gelu_ev = {}
            for he in range(16):
                par = (e * 16 + he) % 2
                for kk in range(2):
                    k = he * 2 + kk
                    bk = 4 + (k % 4)
                    mm = None
                    for dd in range(8):
                        aft = _f([e_up[(e, he)], st["bank"][bk], ev_ln]) if dd == 0 else ()
                        mm = P.op("tensor", lambda pe, dd=dd, bk=bk, par=par, kk=kk: pe.matmul(
                            self.bank(bk),
                            upv(par)[:, dd * 256 + kk * 128:dd * 256 + kk * 128 + 128],
                            b.xn_fm[:, dd * 512:(dd + 1) * 512],
                            start=(dd == 0), stop=(dd == 7)),
                            after=aft, inc=(dd == 7))
                    if kk == 1:
                        st["up_free"][par] = mm
                    gaft = [mm] + ([st["gh_rd"], st.get("tail_ev")] if k == 0 else [])
                    gelu_ev[k] = P.op("act", lambda a, k=k, bk=bk: a.activation(
                        b.gh[:, k * 512:(k + 1) * 512], self.bank(bk),
                        AF.Gelu_apprx_tanh), after=_f(gaft))
                    st["bank"][bk] = gelu_ev[k]
                if he + 2 < 16:
                    fetch_up(e, he + 2)
                elif e + 1 < NE:
                    fetch_up(e + 1, he - 14)
                if he == 15:
                    fetch_dn(e, 0)
                    fetch_dn(e, 1)
            mm = None
            for ke in range(16):
                par = (e * 16 + ke) % 2
                for kk in range(2):
                    k = ke * 2 + kk
                    for tc in range(TC):
                        for half in range(2):
                            bki = tc * 2 + half
                            aft = []
                            if tc == 0 and half == 0:
                                aft = [e_dn[(e, ke)], gelu_ev[k]]
                            if k == 0:
                                aft.append(st["bank"][bki])
                            mm = P.op("tensor",
                                      lambda pe, k=k, tc=tc, half=half, par=par, kk=kk, bki=bki:
                                      pe.matmul(
                                          self.bank(bki),
                                          b.gh[:, k * 512 + tc * 128:k * 512 + tc * 128 + 128],
                                          dnv(par)[:, kk * 1024 + half * 512:kk * 1024 + half * 512 + 512],
                                          start=(k == 0), stop=(k == 31)),
                                      after=_f(aft), inc=(kk == 1 and tc == 3 and half == 1))
                    if kk == 1:
                        st["dn_free"][par] = mm
                if ke + 2 < 16:
                    fetch_dn(e, ke + 2)
            st["gh_rd"] = mm
            for tc in range(TC):
                for half in range(2):
                    bki = tc * 2 + half
                    dst = b.g_acc[:, tc * 1024 + half * 512:tc * 1024 + half * 512 + 512]
                    rwc = b.rw_tm[:, tc * 8 + e:tc * 8 + e + 1]
                    if e == 0:
                        ev2 = P.op("dve", lambda v, dst=dst, rwc=rwc, bki=bki: v.tensor_scalar(
                            dst, self.bank(bki), rwc, None, ALU.mult),
                            after=_f([mm, e_rw]))
                    else:
                        ev2 = P.op("dve", lambda v, dst=dst, rwc=rwc, bki=bki:
                                   v.scalar_tensor_tensor(
                                       dst, self.bank(bki), rwc, dst, ALU.mult, ALU.add),
                                   after=[mm])
                    st["bank"][bki] = ev2
                    e_gacc = ev2
        return e_gacc

    def phase_attn(self, P, t, ev_ln, st):
        for hp in range(2):
            self.proj_phase(P, t, hp, ev_ln, st)
            if PHASE >= 4:
                self.scan_phase(P, t, hp, st)
        if PHASE >= 5:
            self.wo_phase(P, t, st)

    def proj_phase(self, P, t, hp, ev_ln, st):
        b = self
        srcs = [b.Wq_h, b.Wk_h, b.Wv_h]
        wq_ev = {}

        def fetch_w(slot, h_src, col0, mat):
            par = slot % 2
            dst = b.wqb[:, par * 4096:(par + 1) * 4096].rearrange("p (c n) -> p c n", c=8)
            apd = [[D, 128], [128 * D, 8], [1, 512]]
            if t == 0:
                srcap = bass.AP(h_src, col0, apd)
            else:
                srcap = bass.AP(b.wq_s, mat * D * D + col0, apd)
            wq_ev[slot] = P.dma("pool", "wq%d" % par, dst, srcap,
                                after=_f([st["wq_free"][par],
                                          st["wqw"][par]]))
            if t == 0:
                st["wqw"][par] = P.dma(
                    "sync", "ww%d" % par, bass.AP(b.wq_s, mat * D * D + col0, apd),
                    dst, after=[wq_ev[slot]])

        base = hp * 3
        if hp == 0:
            fetch_w(0, srcs[0], 0, 0)
        else:
            wq_ev[base] = st.pop("wq_next")
        for pj in range(3):
            slot = base + pj
            par = slot % 2
            if pj + 1 < 3:
                fetch_w(slot + 1, srcs[pj + 1], hp * 512, pj + 1)
            elif hp == 0:
                fetch_w(slot + 1, srcs[0], 512, 0)
                st["wq_next"] = wq_ev[slot + 1]
            e_pre = {}
            for m in range(4):
                bk = 4 + (m % 2)
                mm = None
                for dd in range(8):
                    aft = _f([wq_ev[slot], st["bank"][bk], ev_ln]) if dd == 0 else ()
                    mm = P.op("tensor", lambda pe, m=m, dd=dd, bk=bk, par=par: pe.matmul(
                        self.bank(bk),
                        b.wqb[:, par * 4096 + dd * 512 + m * 128:par * 4096 + dd * 512 + m * 128 + 128],
                        b.xn_fm[:, dd * 512:(dd + 1) * 512],
                        start=(dd == 0), stop=(dd == 7)), after=aft, inc=(dd == 7))
                if m == 3:
                    st["wq_free"][par] = mm
                pcol = (pj * 4 + m) * 515
                aft2 = [mm]
                if pj == 0 and m == 0:
                    aft2 += _f([st["scan_done"], st.get("rw_ev"), st.get("gh_rd")])
                e_pre[m] = P.op("act", lambda a, pcol=pcol, bk=bk: a.activation(
                    b.pre[:, pcol + 3:pcol + 515], self.bank(bk), AF.Copy),
                    after=_f(aft2))
                st["bank"][bk] = e_pre[m]
            e_sil = {}
            for m in range(4):
                mg = hp * 4 + m
                pcol = (pj * 4 + m) * 515
                tcol = (pj * 8 + mg) * 3
                acc = b.sf32[:, 0:512]
                tmp = b.sf32[:, 512:1024]
                tapc = lambda j, pj=pj, mg=mg: b.convt[:, pj * 32 + mg * 4 + j:pj * 32 + mg * 4 + j + 1]
                e_ti = P.op("pool", lambda g, pcol=pcol, tcol=tcol: g.tensor_copy(
                    b.pre[:, pcol:pcol + 3], b.tails[:, tcol:tcol + 3]),
                    after=_f([e_pre[m], st.get("tail_ev")] + st["init"]))
                e_c = P.op("dve", lambda v, pcol=pcol, tapc=tapc, acc=acc: v.tensor_scalar(
                    acc, b.pre[:, pcol + 3:pcol + 515], tapc(3), None, ALU.mult),
                    after=_f([e_ti, st.get("acc_rd")]))
                for j in (2, 1, 0):
                    e_c = P.op("dve", lambda v, pcol=pcol, j=j, tapc=tapc, acc=acc:
                               v.scalar_tensor_tensor(
                                   acc, b.pre[:, pcol + j:pcol + j + 512], tapc(j),
                                   acc, ALU.mult, ALU.add), after=[e_c])
                st["tail_ev"] = P.op("pool", lambda g, pcol=pcol, tcol=tcol: g.tensor_copy(
                    b.tails[:, tcol:tcol + 3], b.pre[:, pcol + 512:pcol + 515]),
                    after=[e_c])
                if pj < 2:
                    dstb = [b.qbf, b.kbf][pj]
                    e_sil[m] = P.op("act", lambda a, m=m, dstb=dstb, acc=acc: a.activation(
                        dstb[:, m * 512:(m + 1) * 512], acc, AF.Silu),
                        after=_f([e_c, st.get("qk_rd")]))
                    st["acc_rd"] = e_sil[m]
                else:
                    e_sv = P.op("act", lambda a, acc=acc: a.activation(
                        b.sf32[:, 1024:1536], acc, AF.Silu),
                        after=_f([e_c, st.get("sl2_rd")]))
                    st["acc_rd"] = e_sv
                    self.kv_transpose(P, st, b.sf32[:, 1024:1536], True, m // 2, m % 2, e_sv)
            if pj < 2:
                self.l2_phase(P, hp, pj, e_sil, st)

    def l2_phase(self, P, hp, pj, e_sil, st):
        b = self
        dstb = [b.qbf, b.kbf][pj]
        rs = self.PSB[0:2, 3 * 512:4 * 512]
        mm = None
        for m in range(4):
            e_q = P.op("dve", lambda v, m=m, dstb=dstb: v.tensor_mul(
                b.sq_scr[:, 0:512], dstb[:, m * 512:(m + 1) * 512],
                dstb[:, m * 512:(m + 1) * 512]),
                after=_f([e_sil[m]] + ([mm] if mm else []) +
                         ([st.get("rw_ev")] if m == 0 else [])))
            aft = _f([e_q] + ([st["bank"][7]] if m == 0 else []))
            mm = P.op("tensor", lambda pe, m=m: pe.matmul(
                rs, b.c2[:, (m // 2) * 2:(m // 2) * 2 + 2], b.sq_scr[:, 0:512],
                start=(m == 0), stop=(m == 3)), after=aft, inc=True)
        rn = b.sf32[0:2, 1536:2048]
        e_rt = P.op("act", lambda a, rn=rn: a.activation(
            rn, rs, AF.Sqrt, bias=b.ceps[0:2, 1:2], scale=1.0),
            after=_f([mm, st.get("sl3_rd")]))
        e_rc = P.op("dve", lambda v, rn=rn: v.reciprocal(rn, rn), after=[e_rt])
        st["bank"][7] = e_rc
        mmb_last = None
        for h in range(2):
            bcast = self.bank(6)
            mmb = P.op("tensor", lambda pe, h=h, rn=rn: pe.matmul(
                bcast, b.bc2[:, h * 128:(h + 1) * 128], rn, start=True, stop=True),
                after=_f([e_rc, st["bank"][6]]), inc=True)
            mmb_last = mmb
            last = None
            for m in (2 * h, 2 * h + 1):
                e_n = P.op("dve", lambda v, m=m, dstb=dstb, bcast=bcast: v.tensor_tensor(
                    dstb[:, m * 512:(m + 1) * 512], dstb[:, m * 512:(m + 1) * 512],
                    bcast, ALU.mult), after=[mmb])
                last = e_n
                if pj == 1:
                    e_fc = P.op("dve", lambda v, m=m, dstb=dstb: v.tensor_copy(
                        b.sf32[:, 1024:1536], dstb[:, m * 512:(m + 1) * 512]),
                        after=_f([e_n, st.get("sl2_rd")]))
                    self.kv_transpose(P, st, b.sf32[:, 1024:1536], False, h, m % 2, e_fc)
            st["bank"][6] = last
        st["sl3_rd"] = mmb_last
        st["qk_rd"] = None
        if pj == 0:
            st["qbf_ev"] = last
        else:
            st["kbf_ev"] = last

    def kv_transpose(self, P, st, src, is_v, h, vh, after):
        b = self
        bk = 7
        mm = None
        for ts_ in range(4):
            mm = P.op("tensor", lambda pe, ts_=ts_, bk=bk, src=src: pe.transpose(
                self.bank(bk)[:, ts_ * 128:(ts_ + 1) * 128],
                src[:, ts_ * 128:(ts_ + 1) * 128], b.idf[:, :]),
                after=_f([after, st["bank"][bk]]) if ts_ == 0 else (), inc=(ts_ == 3))
        st["sl2_rd"] = mm
        dstt = b.v_tm if is_v else b.k_tm
        dst = bass.AP(dstt, h * 256 + vh * 128, [[2048, 128], [512, 4], [1, 128]])
        e_ev = P.op("dve", lambda v, dst=dst, bk=bk: v.tensor_copy(
            dst, self.bank(bk)[:, 0:512].rearrange("p (c n) -> p c n", c=4)),
            after=[mm])
        st["bank"][bk] = e_ev
        st["tm_ev"] = e_ev

    def _ap32(self, tsr, freesz, base, g, col, n=32):
        return bass.AP(tsr, 32 * g * freesz + base + col, [[freesz, 32], [1, n]])

    def scan_phase(self, P, t, hp, st):
        b = self
        if t > 0:
            seed = None
            for r in range(4):
                aft = _f([st["bank"][0], st["bank"][1], st["bank"][2], st["bank"][3],
                          st["s32_ev"][hp]]) if r == 0 else ()
                seed = P.op("tensor", lambda pe, r=r: pe.matmul(
                    self.PSA[:, r * 512:(r + 1) * 512], b.idf[:, :],
                    b.S32[:, hp * 2048 + r * 512:hp * 2048 + (r + 1) * 512],
                    start=True, stop=False, skip_group_check=True),
                    after=aft, inc=(r == 3))
            e_supd = seed
        else:
            e_supd = None
        kq = _f([st.get("kbf_ev"), st.get("qbf_ev"), st.get("tm_ev")])
        e_z = P.op("dve", lambda v: v.memset(self.PSB[:, 0:1024], 0.0),
                   after=_f([st["bank"][4], st["bank"][5]]))
        st["bank"][4] = e_z
        st["bank"][5] = e_z
        mmA = None
        for h in range(2):
            for s in range(TC):
                for g in range(4):
                    c = s * 4 + g
                    for kc in range(2):
                        aft = _f([st["bank"][4], st["bank"][5]] + kq) \
                            if (h == 0 and s == 0 and g == 0 and kc == 0) else ()
                        col = (h * 4 + s) * 128 + g * 32
                        mmA = P.op("tensor", lambda pe, h=h, c=c, g=g, kc=kc, col=col: pe.matmul(
                            self.PSB[32 * g:32 * g + 32, col:col + 32],
                            b.kbf[:, (2 * h + kc) * 512 + c * 32:(2 * h + kc) * 512 + c * 32 + 32],
                            b.kbf[:, (2 * h + kc) * 512 + c * 32:(2 * h + kc) * 512 + c * 32 + 32],
                            start=(kc == 0), stop=(kc == 1), tile_position=(0, 32 * g)),
                            after=aft, inc=(kc == 1))
        mmS = None
        for h in range(2):
            for s in range(TC):
                for g in range(4):
                    c = s * 4 + g
                    for kc in range(2):
                        aft = _f([st["bank"][6]]) if (h == 0 and s == 0 and g == 0 and kc == 0) else ()
                        col = 2 * 512 + (h * 4 + s) * 32
                        mmS = P.op("tensor", lambda pe, h=h, c=c, g=g, kc=kc, col=col: pe.matmul(
                            self.PSB[32 * g:32 * g + 32, col:col + 32],
                            b.kbf[:, (2 * h + kc) * 512 + c * 32:(2 * h + kc) * 512 + c * 32 + 32],
                            b.qbf[:, (2 * h + kc) * 512 + c * 32:(2 * h + kc) * 512 + c * 32 + 32],
                            start=(kc == 0), stop=(kc == 1), tile_position=(0, 32 * g)),
                            after=aft, inc=(kc == 1))
        if SCANSUB < 2:
            st["scan_done"] = mmS
            st["nmT_rd"] = mmS
            st["nmn_rd"] = mmS
            st["s32_ev"][hp] = mmS
            return
        e_a = P.op("dve", lambda v: v.tensor_tensor(
            b.a_sb[:, :].rearrange("p (q j) -> p q j", q=8),
            bass.AP(self.PSB, 2 * 512, [[2048, 128], [32, 8], [1, 32]]),
            bass.AP(b.maskIncl, 0, [[32, 128], [0, 8], [1, 32]]), ALU.mult),
            after=[mmS])
        st["bank"][6] = e_a
        e_As = {}
        e_msk = None
        for scan in range(2):
            e_msk = P.op("dve", lambda v, scan=scan: v.tensor_tensor(
                bass.AP(b.nmn, scan * 1024, [[2048, 128], [128, 8], [32, 4], [1, 32]]),
                bass.AP(self.PSB, 0, [[2048, 128], [128, 8], [32, 4], [1, 32]]),
                bass.AP(b.maskSneg, 0, [[32, 128], [0, 8], [0, 4], [1, 32]]), ALU.mult),
                after=_f([mmA] + ([st.get("nmn_rd")] if scan == 0 else [])))
            bap = bass.AP(b.beta_tm, scan * 4 + 2 * hp,
                          [[32, 128], [1, 2], [8, 4], [0, 128]])
            e_As[scan] = P.op("dve", lambda v, scan=scan, bap=bap: v.tensor_tensor(
                bass.AP(b.nmn, scan * 1024, [[2048, 128], [512, 2], [128, 4], [1, 128]]),
                bass.AP(b.nmn, scan * 1024, [[2048, 128], [512, 2], [128, 4], [1, 128]]),
                bap, ALU.mult),
                after=_f([e_msk, st.get("beta_ev")]))
        st["bank"][4] = e_As[1]
        st["bank"][5] = e_As[1]
        if SCANSUB < 3:
            st["scan_done"] = e_As[1]
            st["nmT_rd"] = e_As[1]
            st["nmn_rd"] = e_As[1]
            st["s32_ev"][hp] = e_As[1]
            return
        for scan in range(2):
            e_abf = P.op("dve", lambda v, scan=scan: v.tensor_copy(
                b.nmB[:, 0:1024], b.nmn[:, scan * 1024:(scan + 1) * 1024]),
                after=_f([e_As[scan], st.get("nmB_rd")]))
            mm = None
            for blk in range(8):
                bk = 4 + blk // 4
                mm = P.op("tensor", lambda pe, blk=blk, bk=bk, scan=scan: pe.transpose(
                    self.bank(bk)[:, (blk % 4) * 128:(blk % 4) * 128 + 128],
                    b.nmn[:, scan * 1024 + blk * 128:scan * 1024 + (blk + 1) * 128],
                    b.idf[:, :]),
                    after=_f([e_As[scan], st["bank"][4], st["bank"][5]]) if blk == 0 else (),
                    inc=(blk == 7))
            e_Ap = P.op("dve", lambda v: v.tensor_copy(
                b.nmA[:, :].rearrange("p (u n) -> p u n", u=2),
                bass.AP(self.PSB, 0, [[2048, 128], [512, 2], [1, 512]])),
                after=_f([mm, st.get("nmA_rd")]))
            st["bank"][4] = e_Ap
            st["bank"][5] = e_Ap
            if SCANSUB < 4:
                st["scan_done"] = e_Ap
                st["nmT_rd"] = e_Ap
                st["nmn_rd"] = e_Ap
                st["s32_ev"][hp] = e_Ap
                st["ttil_%d" % scan] = e_Ap
                continue
            def bap32(slot, g, col, n=32):
                return self._ap32(b.nmB, 7168, slot * 1024, g, col, n)

            def aap32(g, col, n=32):
                return self._ap32(b.nmA, 1024, 0, g, col, n)

            chain = [(1, "A'", 0), (5, 0, "A'"), (2, 5, 1), (6, 1, 5),
                     (3, 6, 2), (5, 2, 6), (4, 5, 3)]
            prev_ev = [e_abf, e_Ap]
            for (dsts, lt, rh) in chain:
                mm = None
                for blk in range(8):
                    def sl_of(x, blk=0):
                        t_ = b.nmA if x == "A'" else None
                        return None
                    lt_ap = (b.nmA if lt == "A'" else
                             b.nmB[:, lt * 1024:(lt + 1) * 1024])
                    rh_ap = (b.nmA if rh == "A'" else
                             b.nmB[:, rh * 1024:(rh + 1) * 1024])
                    mm = P.op("tensor",
                              lambda pe, blk=blk, lt_ap=lt_ap, rh_ap=rh_ap: pe.matmul(
                                  self.PSB[:, blk * 128:(blk + 1) * 128],
                                  lt_ap[:, blk * 128:(blk + 1) * 128],
                                  rh_ap[:, blk * 128:(blk + 1) * 128],
                                  start=True, stop=True),
                              after=_f(prev_ev + [st["bank"][4], st["bank"][5]])
                              if blk == 0 else (),
                              inc=(blk == 7))
                e_ev = P.op("dve", lambda v, dsts=dsts: v.tensor_copy(
                    b.nmB[:, dsts * 1024:(dsts + 1) * 1024].rearrange("p (u n) -> p u n", u=2),
                    bass.AP(self.PSB, 0, [[2048, 128], [512, 2], [1, 512]])),
                    after=[mm])
                st["bank"][4] = e_ev
                st["bank"][5] = e_ev
                prev_ev = [e_ev]
            if SCANSUB < 5:
                st["scan_done"] = prev_ev[-1]
                st["nmT_rd"] = prev_ev[-1]
                st["nmn_rd"] = prev_ev[-1]
                st["s32_ev"][hp] = prev_ev[-1]
                st["ttil_%d" % scan] = prev_ev[-1]
                continue
            e_r = P.op("dve", lambda v: v.tensor_add(
                bass.AP(b.nmR, 0, [[1024, 128], [128, 8], [1, 128]]),
                bass.AP(b.nmA, 0, [[1024, 128], [128, 8], [1, 128]]),
                bass.AP(b.idf, 0, [[128, 128], [0, 8], [1, 128]])),
                after=_f([e_Ap, st.get("nmR_rd")]))
            first_r = True
            for bslot in (1, 2, 3, 4):
                mm = None
                for blk in range(8):
                    aft = ()
                    if blk == 0:
                        aft = _f([e_r] + ([st["bank"][6], st["bank"][7]]
                                          if first_r else []))
                    mm = P.op("tensor",
                              lambda pe, blk=blk, bslot=bslot: pe.matmul(
                                  self.PSB[:, 1024 + blk * 128:1024 + (blk + 1) * 128],
                                  b.nmB[:, bslot * 1024 + blk * 128:bslot * 1024 + (blk + 1) * 128],
                                  b.nmR[:, blk * 128:(blk + 1) * 128],
                                  start=True, stop=True),
                              after=aft, inc=(blk == 7))
                first_r = False
                e_r = P.op("dve", lambda v: v.tensor_add(
                    b.nmR[:, :], b.nmR[:, :], self.PSB[:, 1024:2048]), after=[mm])
            st["bank"][6] = e_r
            st["bank"][7] = e_r
            bap = bass.AP(b.beta_tm, scan * 4 + 2 * hp,
                          [[32, 128], [1, 2], [8, 4], [0, 128]])
            e_tt = P.op("dve", lambda v, scan=scan, bap=bap: v.tensor_tensor(
                bass.AP(b.nmT, scan * 1024, [[2048, 128], [512, 2], [128, 4], [1, 128]]),
                bass.AP(b.nmR, 0, [[1024, 128], [128, 8], [32, 4], [1, 128]])
                if False else
                bass.AP(b.nmR, 0, [[1024, 128], [512, 2], [128, 4], [1, 128]]),
                bap, ALU.mult),
                after=_f([e_r, st.get("nmT_rd")]))
            st["nmR_rd"] = e_tt
            st["nmA_rd"] = e_r
            st["nmB_rd"] = e_r
            st["ttil_%d" % scan] = e_tt
        if SCANSUB < 6:
            ev = st["ttil_1"]
            st["scan_done"] = ev
            st["nmT_rd"] = ev
            st["nmn_rd"] = ev
            st["s32_ev"][hp] = ev
            return
        e_prev_chunk = None
        for s in range(TC):
            ev_u0, ev_wm = {}, {}
            for scan in range(2):
                bk = 6 + scan
                mm = None
                for h in range(2):
                    mm = P.op("tensor", lambda pe, h=h, scan=scan, bk=bk, s=s: pe.matmul(
                        self.bank(bk)[:, h * 256:h * 256 + 256],
                        b.nmT[:, scan * 1024 + (h * 4 + s) * 128:
                              scan * 1024 + (h * 4 + s) * 128 + 128],
                        b.v_tm[:, (s * 2 + h) * 256:(s * 2 + h) * 256 + 256],
                        start=True, stop=True),
                        after=_f([st["ttil_%d" % scan], st["bank"][bk]])
                        if h == 0 else (), inc=(h == 1))
                wap = bass.AP(b.w3, scan * 16 + s * 4 + 2 * hp,
                              [[48, 128], [1, 2], [0, 256]])
                ev_u0[scan] = P.op("dve", lambda v, scan=scan, bk=bk, wap=wap: v.tensor_tensor(
                    b.u0s[:, scan * 512:(scan + 1) * 512].rearrange("p (h n) -> p h n", h=2),
                    self.bank(bk).rearrange("p (h n) -> p h n", h=2), wap, ALU.mult),
                    after=_f([mm] + st.get("w3_ev", [])))
                st["bank"][bk] = ev_u0[scan]
                mm2 = None
                for h in range(2):
                    for kc in range(2):
                        mm2 = P.op("tensor",
                                   lambda pe, scan=scan, bk=bk, s=s, h=h, kc=kc: pe.matmul(
                                       self.bank(bk)[:, (h * 2 + kc) * 128:(h * 2 + kc) * 128 + 128],
                                       b.k_tm[:, (s * 2 + h) * 256 + kc * 128:
                                              (s * 2 + h) * 256 + kc * 128 + 128],
                                       b.nmT[:, scan * 1024 + (h * 4 + s) * 128:
                                             scan * 1024 + (h * 4 + s) * 128 + 128],
                                       start=True, stop=True),
                                   after=_f([ev_u0[scan]]) if (h == 0 and kc == 0) else (),
                                   inc=(h == 1 and kc == 1))
                ev_wm[scan] = P.op("dve", lambda v, scan=scan, bk=bk: v.tensor_copy(
                    b.wm_sb[:, scan * 512:(scan + 1) * 512], self.bank(bk)), after=[mm2])
                st["bank"][bk] = ev_wm[scan]
            if SCANSUB < 7:
                ev = ev_wm[1]
                st["scan_done"] = ev
                st["nmT_rd"] = ev
                st["nmn_rd"] = ev
                st["s32_ev"][hp] = ev
                continue
            for g in range(4):
                c = s * 4 + g
                if not (t == 0 and c == 0):
                    e_snap = P.op("act", lambda a: a.activation(
                        b.S_bf[:, :], self.PSA[:, :], AF.Copy),
                        after=_f([e_supd, e_prev_chunk, st.get("o_mm_prev")]))
                else:
                    e_snap = None
                e_wmS = None
                if e_snap is not None:
                    mm = None
                    for h in range(2):
                        for scan in range(2):
                            for kc in range(2):
                                col4 = (h * 2 + scan) * 256
                                mm = P.op("tensor",
                                          lambda pe, h=h, scan=scan, kc=kc, g=g, col4=col4, s=s:
                                          pe.matmul(
                                              self.PSB[32 * g:32 * g + 32, col4:col4 + 256],
                                              b.wm_sb[:, scan * 512 + (h * 2 + kc) * 128 + g * 32:
                                                      scan * 512 + (h * 2 + kc) * 128 + g * 32 + 32],
                                              b.S_bf[:, (h * 2 + scan) * 512 + kc * 256:
                                                     (h * 2 + scan) * 512 + kc * 256 + 256],
                                              start=(kc == 0), stop=(kc == 1),
                                              tile_position=(0, 32 * g)),
                                          after=_f([e_snap, ev_wm[0], ev_wm[1],
                                                    st["bank"][4], st["bank"][5]])
                                          if (h == 0 and scan == 0 and kc == 0) else (),
                                          inc=(h == 1 and scan == 1 and kc == 1))
                    e_wmS = mm
                e_us = []
                for h in range(2):
                    for scan in range(2):
                        dst = self._ap32(b.u_sb, 1024, (h * 2 + scan) * 256, g, 0, 256)
                        src0 = self._ap32(b.u0s, 1024, scan * 512 + h * 256, g, 0, 256)
                        if e_wmS is None:
                            e_us.append(P.op("dve", lambda v, dst=dst, src0=src0: v.tensor_copy(
                                dst, src0), after=_f([ev_u0[scan]])))
                        else:
                            wnc = b.w3n[32 * g:32 * g + 32,
                                        scan * 16 + s * 4 + 2 * hp + h:
                                        scan * 16 + s * 4 + 2 * hp + h + 1]
                            psrc = self.PSB[32 * g:32 * g + 32,
                                            (h * 2 + scan) * 256:(h * 2 + scan) * 256 + 256]
                            e_us.append(P.op("dve",
                                             lambda v, dst=dst, src0=src0, wnc=wnc, psrc=psrc:
                                             v.scalar_tensor_tensor(dst, psrc, wnc, src0,
                                                                    ALU.mult, ALU.add),
                                             after=[e_wmS]))
                st["bank"][4] = e_us[-1]
                st["bank"][5] = e_us[-1]
                e_usum = P.op("dve", lambda v, g=g: v.tensor_add(
                    bass.AP(b.usum, 32 * g * 512, [[512, 32], [256, 2], [1, 256]]),
                    bass.AP(b.u_sb, 32 * g * 1024, [[1024, 32], [512, 2], [1, 256]]),
                    bass.AP(b.u_sb, 32 * g * 1024 + 256, [[1024, 32], [512, 2], [1, 256]])),
                    after=e_us)
                mm = None
                for h in range(2):
                    for scan in range(2):
                        for kc in range(2):
                            sstart = (t == 0 and c == 0 and kc == 0)
                            mm = P.op("tensor",
                                      lambda pe, h=h, scan=scan, kc=kc, g=g, s=s, sstart=sstart:
                                      pe.matmul(
                                          self.PSA[:, (h * 2 + scan) * 512 + kc * 256:
                                                   (h * 2 + scan) * 512 + kc * 256 + 256],
                                          self._ap32(b.k_tm, 2048, (s * 2 + h) * 256 + kc * 128, g, 0, 128),
                                          self._ap32(b.u_sb, 1024, (h * 2 + scan) * 256, g, 0, 256),
                                          start=sstart, stop=False,
                                          tile_position=(32 * g, 0),
                                          skip_group_check=True),
                                      after=_f([e_usum] + ([e_snap] if e_snap else []))
                                      if (h == 0 and scan == 0 and kc == 0) else (),
                                      inc=(h == 1 and scan == 1 and kc == 1))
                e_supd = mm
                e_prev_chunk = mm
                mmo = None
                for h in range(2):
                    if e_snap is not None:
                        for scan in range(2):
                            for kc in range(2):
                                aft = _f([e_snap] + ([st["bank"][6]] if (h == 0 and scan == 0 and kc == 0 and g == 0) else []))
                                mmo = P.op("tensor",
                                           lambda pe, h=h, scan=scan, kc=kc, g=g, c=c:
                                           pe.matmul(
                                               self.bank(6)[32 * g:32 * g + 32, h * 256:h * 256 + 256],
                                               b.qbf[:, (2 * h + kc) * 512 + c * 32:
                                                     (2 * h + kc) * 512 + c * 32 + 32],
                                               b.S_bf[:, (h * 2 + scan) * 512 + kc * 256:
                                                      (h * 2 + scan) * 512 + kc * 256 + 256],
                                               start=(scan == 0 and kc == 0), stop=False,
                                               tile_position=(0, 32 * g)),
                                           after=aft if (scan == 0 and kc == 0) else (),
                                           inc=False)
                        astart = False
                    else:
                        astart = True
                    aft = _f([e_usum] + ([st["bank"][6]] if (astart and h == 0 and g == 0) else []))
                    mmo = P.op("tensor", lambda pe, h=h, g=g, s=s, astart=astart: pe.matmul(
                        self.bank(6)[32 * g:32 * g + 32, h * 256:h * 256 + 256],
                        self._ap32(b.a_sb, 256, (h * 4 + s) * 32, g, 0),
                        bass.AP(b.usum, 32 * g * 512 + h * 256, [[512, 32], [1, 256]]),
                        start=astart, stop=True, tile_position=(32 * g, 32 * g)),
                        after=aft, inc=(h == 1))
                st["o_mm_%d" % g] = mmo
                st["o_mm_prev"] = mmo
            if SCANSUB < 8:
                ev = e_prev_chunk
                st["scan_done"] = ev
                st["nmT_rd"] = ev
                st["nmn_rd"] = ev
                st["s32_ev"][hp] = ev
                st["bank"][6] = st.get("o_mm_3")
                continue
            o_ev = _f([st.get("o_mm_%d" % g) for g in range(4)])
            wrap = bass.AP(b.w3, 32 + s * 4 + 2 * hp, [[48, 128], [1, 2], [0, 256]])
            e_t1 = P.op("dve", lambda v, s=s, wrap=wrap: v.tensor_tensor(
                b.sf32[:, 1024:1536].rearrange("p (h n) -> p h n", h=2),
                b.v_tm[:, s * 512:(s + 1) * 512].rearrange("p (h n) -> p h n", h=2),
                wrap, ALU.mult),
                after=_f([st.get("sl2_rd")] + st.get("w3_ev", [])))
            e_t2 = P.op("dve", lambda v: v.tensor_add(
                b.sf32[:, 1024:1536], b.sf32[:, 1024:1536], self.bank(6)),
                after=[e_t1] + o_ev)
            st["bank"][6] = e_t2
            st["sl2_rd"] = e_t2
            e_ms = []
            prev = e_t2
            for h in range(2):
                prev = P.op("act", lambda a, h=h, s=s, hp=hp: a.activation(
                    b.sq_scr[:, 0:256], b.sf32[:, 1024 + h * 256:1024 + h * 256 + 256],
                    AF.Square, accum_out=b.rmsv[:, hp * 16 + s * 2 + h:hp * 16 + s * 2 + h + 1]),
                    after=[prev])
                e_ms.append(prev)
            e_rt = P.op("act", lambda a, s=s, hp=hp: a.activation(
                b.rmsv[:, hp * 16 + 8 + s * 2:hp * 16 + 8 + s * 2 + 2],
                b.rmsv[:, hp * 16 + s * 2:hp * 16 + s * 2 + 2],
                AF.Sqrt, bias=b.ceps[:, 0:1], scale=1.0 / 256), after=e_ms)
            e_rc = P.op("dve", lambda v, s=s, hp=hp: v.reciprocal(
                b.rmsv[:, hp * 16 + 8 + s * 2:hp * 16 + 8 + s * 2 + 2],
                b.rmsv[:, hp * 16 + 8 + s * 2:hp * 16 + 8 + s * 2 + 2]), after=[e_rt])
            rstdap = bass.AP(b.rmsv, hp * 16 + 8 + s * 2, [[32, 128], [1, 2], [0, 256]])
            e_orm = P.op("dve", lambda v, rstdap=rstdap: v.tensor_tensor(
                b.sf32[:, 1536:2048].rearrange("p (h n) -> p h n", h=2),
                b.sf32[:, 1024:1536].rearrange("p (h n) -> p h n", h=2),
                rstdap, ALU.mult), after=_f([e_rc, st.get("sl3_rd")])),
            e_orm = e_orm[0]
            mm = None
            for i in range(4):
                mm = P.op("tensor", lambda pe, i=i: pe.transpose(
                    self.bank(7)[:, i * 128:(i + 1) * 128],
                    b.sf32[:, 1536 + i * 128:1536 + (i + 1) * 128], b.idf[:, :]),
                    after=_f([e_orm, st["bank"][7]]) if i == 0 else (), inc=(i == 3))
            st["sl3_rd"] = mm
            dst = bass.AP(b.o_fm, hp * 2048 + s * 128,
                          [[4096, 128], [1024, 2], [512, 2], [1, 128]])
            e_ofm = P.op("dve", lambda v, dst=dst: v.tensor_copy(
                dst, bass.AP(self.PSB, 3 * 512, [[2048, 128], [256, 2], [128, 2], [1, 128]])),
                after=[mm])
            st["bank"][7] = e_ofm
            st["ofm_ev"] = e_ofm
        e_s32 = P.op("act", lambda a, hp=hp: a.activation(
            b.S32[:, hp * 2048:(hp + 1) * 2048], self.PSA[:, :], AF.Copy),
            after=_f([e_supd]))
        st["s32_ev"][hp] = e_s32
        for r in range(4):
            st["bank"][r] = e_s32
        st["scan_done"] = e_s32
        st["nmT_rd"] = e_s32
        st["nmn_rd"] = e_s32

    def wo_phase(self, P, t, st):
        b = self
        wo_ev = {}
        for half in range(2):
            par = half
            apd = [[D, 128], [128 * D, 8], [1, 512]]
            if t == 0:
                srcap = bass.AP(b.Wo_h, half * 512, apd)
            else:
                srcap = bass.AP(b.wq_s, 3 * D * D + half * 512, apd)
            dstv = b.wqb[:, par * 4096:(par + 1) * 4096].rearrange("p (c n) -> p c n", c=8)
            e_d = P.dma("pool", "wq%d" % par, dstv, srcap,
                        after=_f([st["wq_free"][par],
                                  st["wqw"][par]]))
            if t == 0:
                st["wqw"][par] = P.dma(
                    "sync", "ww%d" % par, bass.AP(b.wq_s, 3 * D * D + half * 512, apd),
                    dstv, after=[e_d])
            last = e_d
            for m in range(8):
                last = P.op("dve", lambda v, m=m, par=par: v.tensor_scalar(
                    b.wqb[:, par * 4096 + m * 512:par * 4096 + (m + 1) * 512],
                    b.wqb[:, par * 4096 + m * 512:par * 4096 + (m + 1) * 512],
                    b.normw[:, m % 2:m % 2 + 1], None, ALU.mult),
                    after=_f([e_d, st["wqw"][par]]) if m == 0 else ())
            wo_ev[half] = last
        mm = None
        for m in range(8):
            for tc in range(TC):
                for half in range(2):
                    bki = tc * 2 + half
                    aft = []
                    if m == 0:
                        aft = _f([wo_ev[half], st["bank"][bki], st.get("ofm_ev")])
                    mm = P.op("tensor", lambda pe, m=m, tc=tc, half=half, bki=bki: pe.matmul(
                        self.bank(bki), b.o_fm[:, m * 512 + tc * 128:m * 512 + tc * 128 + 128],
                        b.wqb[:, half * 4096 + m * 512:half * 4096 + (m + 1) * 512],
                        start=(m == 0), stop=(m == 7)),
                        after=aft, inc=(m == 7 and tc == 3 and half == 1))
        st["wq_free"][0] = mm
        st["wq_free"][1] = mm
        st["s_mm"] = mm

    def phase_combine(self, P, t, ev_g, st):
        b = self
        last = None
        for tc in range(TC):
            stage = b.xn_scr if tc % 2 == 0 else b.sq_scr
            e_gs = P.op("act", lambda a, tc=tc: a.activation(
                b.sf32[:, 0:1024], b.g_acc[:, tc * 1024:(tc + 1) * 1024], AF.Silu),
                after=_f([ev_g, st.get("acc_rd"), st.get("gs_rd")]))
            e_o = None
            for half in range(2):
                bki = tc * 2 + half
                sl = stage[:, half * 512:(half + 1) * 512]
                e_m = P.op("dve", lambda v, sl=sl, bki=bki, half=half: v.tensor_tensor(
                    sl, self.bank(bki), b.sf32[:, half * 512:(half + 1) * 512], ALU.mult),
                    after=_f([e_gs, st["s_mm"], st["stage_rd"][tc % 2],
                              st.get("xn_scr_rd") if tc % 2 == 0 else None]))
                e_o = P.op("dve", lambda v, sl=sl, tc=tc, half=half: v.tensor_tensor(
                    sl, sl, b.x_t[:, tc * 1024 + half * 512:tc * 1024 + half * 512 + 512],
                    ALU.add), after=[e_m])
                st["bank"][bki] = e_o
            st["gs_rd"] = e_o
            dst = bass.AP(b.y_h, (t * T + tc * 128) * D, [[D, 128], [1, D]])
            e_dma = P.dma("sync", "outa" if tc % 2 == 0 else "outb",
                          dst, stage[:, 0:1024], after=[e_o])
            st["out"].append(e_dma)
            st["stage_rd"][tc % 2] = e_dma
            last = e_o
        return last


def build_nc():
    nc = bass.Bass("TRN2", target_bir_lowering=False)
    b = Builder(nc)
    with b.stack, nc.Block() as block:
        block.sync(lambda eng: b.walk(eng, "sync"))
        block.scalar(lambda eng: b.walk(eng, "act"))
        block.vector(lambda eng: b.walk(eng, "dve"))
        block.tensor(lambda eng: b.walk(eng, "tensor"))
        block.gpsimd(lambda eng: b.walk(eng, "pool"))
    return nc, b


def kernel(**inputs):
    x = np.ascontiguousarray(inputs["L0_attn_hidden_states"])
    nm = {"L0_attn_Wq": "Wq", "L0_attn_Wk": "Wk", "L0_attn_Wv": "Wv",
          "L0_attn_conv_q": "conv_q", "L0_attn_conv_k": "conv_k",
          "L0_attn_conv_v": "conv_v", "L0_attn_Wb": "Wb", "L0_attn_Wg": "Wg",
          "L0_attn_bg": "bg", "L0_attn_Wh": "Wh", "L0_attn_bh": "bh",
          "L0_attn_norm_w": "norm_w", "L0_attn_Wo": "Wo",
          "L0_ffn_router_w": "router_w", "L0_ffn_up_w": "up_w",
          "L0_ffn_down_w": "down_w"}
    weights = {v: np.ascontiguousarray(inputs[k]) for k, v in nm.items()}
    nc, b = build_nc()
    in_maps = []
    for core in range(8):
        m = {"x": x[core]}
        m.update(weights)
        in_maps.append(m)
    res = run_bass_kernel_spmd(nc, in_maps, core_ids=list(range(8)))
    return np.stack([r["y"] for r in res.results], axis=0)


# version 05
# speedup vs baseline: 1.4367x; 1.0002x over previous
import numpy as np
from contextlib import ExitStack

import concourse.bass as bass
import concourse.mybir as mybir
from concourse.bass_utils import run_bass_kernel_spmd

F32 = mybir.dt.float32
BF16 = mybir.dt.bfloat16
AF = mybir.ActivationFunctionType
ALU = mybir.AluOpType
AX = mybir.AxisListType

L, D, NH, DK, DV, CHK, NE, HF = 8192, 1024, 4, 256, 256, 32, 8, 4096
T = 512
NT = L // T
TC = T // 128
PHASE = 9
SCANSUB = 9


def _f(evts):
    return [e for e in evts if e is not None]


class Walk:
    def __init__(self, b, eng_obj, eng_name):
        self.b = b
        self.eng = eng_obj
        self.name = eng_name
        self.cnt = {k: 0 for k in b.semnames}
        self.last_wait = {k: -1 for k in b.semnames}

    def _wait(self, evt):
        if evt is None:
            return
        sem, val = evt
        if self.last_wait[sem] >= val:
            return
        self.last_wait[sem] = val
        self.eng.wait_ge(self.b.sems[sem], val)

    def op(self, engine, fn, after=(), inc=True):
        evt = None
        if inc:
            self.cnt[engine] += 1
            evt = (engine, self.cnt[engine])
        if engine == self.name:
            for e in after:
                self._wait(e)
            inst = fn(self.eng)
            if inc:
                inst.then_inc(self.b.sems[engine], 1)
        return evt

    def dma(self, engine, sem, out, in_, after=()):
        self.cnt[sem] += 16
        evt = (sem, self.cnt[sem])
        if engine == self.name:
            for e in after:
                self._wait(e)
            self.eng.dma_start(out=out, in_=in_).then_inc(self.b.sems[sem], 16)
        return evt

    def wait(self, engine, evts):
        if engine == self.name:
            for e in evts:
                self._wait(e)


class Builder:
    semnames = ("tensor", "dve", "act", "pool", "sync", "dinit", "xin",
                "outa", "outb", "up0", "up1", "dn0", "dn1", "wq0", "wq1", "dinig",
                "uw0", "uw1", "dw0", "dw1", "ww0", "ww1")

    def __init__(self, nc):
        self.nc = nc
        self.stack = ExitStack()
        s = self.stack
        sb = lambda name, shape, dt=F32: s.enter_context(nc.sbuf_tensor(name, shape, dt))
        dram = lambda name, shape: nc.dram_tensor(name, shape, F32, kind="ExternalInput")
        self.x_h = dram("x", [L, D])
        self.y_h = nc.dram_tensor("y", [L, D], F32, kind="ExternalOutput")
        self.Wq_h, self.Wk_h, self.Wv_h = dram("Wq", [D, D]), dram("Wk", [D, D]), dram("Wv", [D, D])
        self.cq_h, self.ck_h, self.cv_h = dram("conv_q", [D, 4]), dram("conv_k", [D, 4]), dram("conv_v", [D, 4])
        self.Wb_h, self.Wg_h, self.Wh_h = dram("Wb", [D, 8]), dram("Wg", [D, 4]), dram("Wh", [D, 4])
        self.bg_h, self.bh_h = dram("bg", [4]), dram("bh", [4])
        self.normw_h, self.Wo_h = dram("norm_w", [256]), dram("Wo", [D, D])
        self.rwt_h = dram("router_w", [D, 8])
        self.up_h, self.dn_h = dram("up_w", [NE, D, HF]), dram("down_w", [NE, HF, D])
        self.up_s = nc.dram_tensor("up_s", [NE, D, HF], BF16)
        self.dn_s = nc.dram_tensor("dn_s", [NE, HF, D], BF16)
        self.wq_s = nc.dram_tensor("wq_s", [4, D, D], BF16)
        self.idf = sb("idf", [128, 128])
        self.onesf = sb("onesf", [128, 128])
        self.maskSneg = sb("maskSneg", [128, 32])
        self.maskIncl = sb("maskIncl", [128, 32])
        self.i32 = sb("i32", [128, 32])
        self.c2 = sb("c2", [128, 4])
        self.bc2 = sb("bc2", [2, 256])
        self.convt = sb("convt", [128, 96])
        self.wcat = sb("wcat", [128, 192], BF16)
        self.bgt = sb("bgt", [128, 8])
        self.normw = sb("normw", [128, 2])
        self.ceps = sb("ceps", [128, 2])
        self.x_t = sb("x_t", [128, 4096])
        self.wqb = sb("wqb", [128, 2 * 4096], BF16)
        self.upb = sb("upb", [128, 2 * 2048], BF16)
        self.dnb = sb("dnb", [128, 2 * 2048], BF16)
        self.ln_s = sb("ln_s", [128, 20])
        self.xn_scr = sb("xn_scr", [128, 1024])
        self.sq_scr = sb("sq_scr", [128, 1024])
        self.xn_fm = sb("xn_fm", [128, 4096], BF16)
        self.gh = sb("gh", [128, 32 * 512], BF16)
        self.pre = self.gh
        self.g_acc = sb("g_acc", [128, 4096])
        self.gat = sb("gat", [128, 64])
        self.beta_tm = sb("beta_tm", [128, 32])
        self.w3 = sb("w3", [128, 48])
        self.w3n = sb("w3n", [128, 32])
        self.rsc = sb("rsc", [128, 80])
        self.srt = sb("srt", [128, 32])
        self.rw_tm = sb("rw_tm", [128, 32])
        self.tails = sb("tails", [128, 72], BF16)
        self.qbf = sb("qbf", [128, 2048], BF16)
        self.kbf = sb("kbf", [128, 2048], BF16)
        self.sf32 = sb("sf32", [128, 2048])
        self.k_tm = sb("k_tm", [128, 2048], BF16)
        self.v_tm = sb("v_tm", [128, 2048], BF16)
        self.nmn = sb("nmn", [128, 2 * 1024])
        self.nmB = sb("nmB", [128, 7 * 1024], BF16)
        self.nmA = sb("nmA", [128, 1024], BF16)
        self.nmR = sb("nmR", [128, 1024], BF16)
        self.nmT = sb("nmT", [128, 2 * 1024], BF16)
        self.wm_sb = sb("wm_sb", [128, 1024], BF16)
        self.u0s = sb("u0s", [128, 1024], BF16)
        self.u_sb = sb("u_sb", [128, 1024], BF16)
        self.usum = sb("usum", [128, 512], BF16)
        self.a_sb = sb("a_sb", [128, 256], BF16)
        self.S_bf = sb("S_bf", [128, 2048], BF16)
        self.S32 = sb("S32", [128, 4096])
        self.o_fm = sb("o_fm", [128, 4096], BF16)
        self.rmsv = sb("rmsv", [128, 32])
        self.PSA = s.enter_context(nc.psum_tensor("PSA", [128, 2048], F32))
        self.PSB = s.enter_context(nc.psum_tensor("PSB", [128, 2048], F32))
        self.sems = {k: s.enter_context(nc.semaphore("sem_" + k)) for k in self.semnames}

    def bank(self, i, lo=0, n=512):
        t = self.PSA if i < 4 else self.PSB
        return t[:, (i % 4) * 512 + lo:(i % 4) * 512 + lo + n]

    def walk(self, eng_obj, eng_name):
        P = Walk(self, eng_obj, eng_name)
        self.program(P)

    def program(self, P):
        st = {"bank": [None] * 8, "out": [], "gh_rd": None,
              "stage_rd": [None, None], "scan_done": None, "s32_ev": [None, None],
              "wq_free": [None, None], "up_free": [None, None],
              "dn_free": [None, None], "upw": [None, None], "dnw": [None, None],
              "wqw": [None, None]}
        st["init"] = self.init_consts(P)
        ev_x = self.load_x(P, 0, after=st["init"])
        for t in range(NT):
            ev_ln = self.phase_ln(P, t, ev_x, st)
            ev_g = ev_ln
            if PHASE >= 2:
                ev_g = self.phase_ffn(P, t, ev_ln, st)
            if PHASE >= 3:
                self.phase_attn(P, t, ev_ln, st)
            if PHASE >= 9:
                ev_cmb = self.phase_combine(P, t, ev_g, st)
            else:
                dst = bass.AP(self.y_h, t * T * D, [[D, 128], [1, D]])
                ev_cmb = P.dma("sync", "outa", dst, self.x_t[:, 0:1024],
                               after=_f([ev_g, st.get("s_mm"), st["stage_rd"][0]]))
                st["out"].append(ev_cmb)
                st["stage_rd"][0] = ev_cmb
            if t + 1 < NT:
                ev_x = self.load_x(P, t + 1, after=[ev_cmb])
        P.wait("sync", st["out"][-2:])

    def load_x(self, P, t, after=()):
        xv = bass.AP(self.x_h, t * T * D, [[D, 128], [128 * D, TC], [1, D]])
        return P.dma("sync", "xin", self.x_t[:, :].rearrange("p (tc d) -> p tc d", tc=TC),
                     xv, after=_f(after))

    def init_consts(self, P):
        b = self
        ev = []
        ch = None
        chg = None
        e1 = P.op("pool", lambda g: g.memset(b.onesf[:, :], 1.0))
        ev.append(P.op("pool", lambda g: g.affine_select(
            b.idf[:, :], b.onesf[:, :], [[-1, 128]], ALU.is_equal, 0.0,
            base=0, channel_multiplier=1), after=[e1]))
        for tsr, fill, cmp_, cm, stp in [(b.maskSneg, -1.0, ALU.is_gt, 1, -1),
                                         (b.maskIncl, 1.0, ALU.is_ge, -1, 1),
                                         (b.i32, 1.0, ALU.is_equal, 1, -1)]:
            e2 = P.op("pool", lambda g, t_=tsr, f=fill: g.memset(t_[:, :], f))
            e3 = P.op("pool", lambda g, t_=tsr, c=cmp_, cm_=cm, s_=stp: g.affine_select(
                t_[0:32, :], t_[0:32, :], [[s_, 32]], c, 0.0, base=0,
                channel_multiplier=cm_), after=[e2])
            chg = P.dma("pool", "dinig", tsr[32:64, :], tsr[0:32, :], after=_f([e3, chg]))
            chg = P.dma("pool", "dinig", tsr[64:128, :], tsr[0:64, :], after=[chg])
        e4 = P.op("pool", lambda g: g.memset(b.c2[:, :], 0.0))
        e5 = P.op("pool", lambda g: g.memset(b.c2[:, 0:1], 1.0), after=[e4])
        ev.append(P.op("pool", lambda g: g.memset(b.c2[:, 3:4], 1.0), after=[e5]))
        e6 = P.op("pool", lambda g: g.memset(b.bc2[:, :], 0.0))
        e7 = P.op("pool", lambda g: g.memset(b.bc2[0:1, 0:128], 1.0), after=[e6])
        chg = P.dma("pool", "dinig", b.bc2[1:2, 128:256], b.bc2[0:1, 0:128],
                    after=_f([e7, chg]))
        ev.append(P.op("pool", lambda g: g.memset(b.tails[:, :], 0.0)))
        ev.append(P.op("pool", lambda g: g.memset(b.ceps[:, 0:1], 1e-5)))
        ev.append(P.op("pool", lambda g: g.memset(b.ceps[:, 1:2], 1e-12)))
        for i, h in enumerate([b.cq_h, b.ck_h, b.cv_h]):
            dst = bass.AP(b.convt, i * 32, [[96, 128], [4, 8], [1, 4]])
            ch = P.dma("sync", "dinit", dst,
                       h.ap().rearrange("(c p) k -> p c k", p=128), after=[ch])
        for hsrc, o0, wdt in [(b.Wb_h, 0, 8), (b.Wg_h, 8, 4), (b.Wh_h, 12, 4), (b.rwt_h, 16, 8)]:
            dst = bass.AP(b.wcat, o0, [[192, 128], [24, 8], [1, wdt]])
            chg = P.dma("pool", "dinig", dst,
                        hsrc.ap().rearrange("(c p) k -> p c k", p=128), after=[chg])
        ch = P.dma("sync", "dinit", bass.AP(b.bgt, 0, [[8, 1], [1, 4]]),
                   b.bg_h.ap()[None, :], after=[ch])
        ch = P.dma("sync", "dinit", bass.AP(b.bgt, 4, [[8, 1], [1, 4]]),
                   b.bh_h.ap()[None, :], after=[ch])
        nrep = 1
        while nrep < 128:
            hi = min(2 * nrep, 128)
            ch = P.dma("sync", "dinit", b.bgt[nrep:hi, :], b.bgt[0:hi - nrep, :],
                       after=[ch])
            nrep = hi
        for j in range(2):
            ch = P.dma("sync", "dinit", b.normw[:, j:j + 1],
                       b.normw_h.ap()[j * 128:(j + 1) * 128][:, None], after=[ch])
        ev.append(ch)
        ev.append(chg)
        return _f(ev)

    def phase_ln(self, P, t, ev_x, st):
        b = self
        last_evac = None
        for tc in range(TC):
            xc = b.x_t[:, tc * 1024:(tc + 1) * 1024]
            cS, cQ, cM, cR, cT = tc, 4 + tc, 8 + tc, 12 + tc, 16 + tc
            first = _f([st["stage_rd"][0], st["stage_rd"][1]]) if tc == 0 else []
            e_sum = P.op("dve", lambda v, xc=xc, cS=cS: v.reduce_sum(
                b.ln_s[:, cS:cS + 1], xc, AX.X), after=[ev_x] + first)
            e_sq = P.op("act", lambda a, xc=xc, cQ=cQ: a.activation(
                b.sq_scr[:, :], xc, AF.Square, accum_out=b.ln_s[:, cQ:cQ + 1]),
                after=[ev_x] + first)
            e_mu = P.op("dve", lambda v, cS=cS, cM=cM: v.tensor_scalar(
                b.ln_s[:, cM:cM + 1], b.ln_s[:, cS:cS + 1], 1.0 / D, None, ALU.mult),
                after=[e_sum])
            e_m2 = P.op("dve", lambda v, cM=cM, cT=cT: v.tensor_tensor(
                b.ln_s[:, cT:cT + 1], b.ln_s[:, cM:cM + 1], b.ln_s[:, cM:cM + 1],
                ALU.mult), after=[e_mu])
            e_qd = P.op("dve", lambda v, cQ=cQ, cR=cR: v.tensor_scalar(
                b.ln_s[:, cR:cR + 1], b.ln_s[:, cQ:cQ + 1], 1.0 / D, None, ALU.mult),
                after=[e_sq])
            e_var = P.op("dve", lambda v, cR=cR, cT=cT: v.tensor_sub(
                b.ln_s[:, cR:cR + 1], b.ln_s[:, cR:cR + 1], b.ln_s[:, cT:cT + 1]),
                after=[e_qd, e_m2])
            e_sd = P.op("act", lambda a, cR=cR: a.activation(
                b.ln_s[:, cR:cR + 1], b.ln_s[:, cR:cR + 1], AF.Sqrt, bias=b.ceps[:, 0:1],
                scale=1.0), after=[e_var])
            e_rs = P.op("dve", lambda v, cR=cR: v.reciprocal(
                b.ln_s[:, cR:cR + 1], b.ln_s[:, cR:cR + 1]), after=[e_sd])
            e_xn = P.op("dve", lambda v, xc=xc, cM=cM, cR=cR: v.tensor_scalar(
                b.xn_scr[:, :], xc, b.ln_s[:, cM:cM + 1], b.ln_s[:, cR:cR + 1],
                ALU.subtract, ALU.mult),
                after=[e_rs] + (_f([st["stage_rd"][0], st.get("xn_scr_rd")]) if tc == 0 else []))
            for half in range(2):
                bk = 4 + half
                mm = None
                for j in range(4):
                    dd = half * 4 + j
                    mm = P.op("tensor", lambda pe, dd=dd, bk=bk, j=j: pe.transpose(
                        self.bank(bk)[:, j * 128:(j + 1) * 128],
                        b.xn_scr[:, dd * 128:(dd + 1) * 128], b.idf[:, :]),
                        after=_f([e_xn, st["bank"][bk]] + st["init"]) if j == 0 else (),
                        inc=(j == 3))
                dst = bass.AP(b.xn_fm, half * 2048 + tc * 128,
                              [[4096, 128], [512, 4], [1, 128]])
                e_ev = P.op("dve", lambda v, bk=bk, dst=dst: v.tensor_copy(
                    dst, self.bank(bk)[:, 0:512].rearrange("p (c n) -> p c n", c=4)),
                    after=[mm])
                st["bank"][bk] = e_ev
                last_evac = e_ev
            st["xn_scr_rd"] = mm
        return last_evac

    def phase_ffn(self, P, t, ev_ln, st):
        b = self
        mmg = None
        for s in range(TC):
            for k in range(8):
                aft = _f([ev_ln, st["bank"][6]]) if (s == 0 and k == 0) else ()
                mmg = P.op("tensor", lambda pe, s=s, k=k: pe.matmul(
                    self.bank(6, s * 24, 24),
                    b.xn_fm[:, k * 512 + s * 128:k * 512 + s * 128 + 128],
                    b.wcat[:, k * 24:(k + 1) * 24], start=(k == 0), stop=(k == 7)),
                    after=aft, inc=(k == 7 and s == 3))
        e_bet = P.op("act", lambda a: a.activation(
            b.beta_tm[:, :].rearrange("p (s c) -> p s c", s=4),
            bass.AP(self.PSB, 2 * 512, [[2048, 128], [24, 4], [1, 8]]), AF.Sigmoid),
            after=[mmg])
        e_gat = P.op("dve", lambda v: v.tensor_copy(
            b.gat[:, :].rearrange("p (s c) -> p s c", s=4),
            bass.AP(self.PSB, 2 * 512 + 8, [[2048, 128], [24, 4], [1, 16]])),
            after=[mmg, e_bet])
        st["bank"][6] = e_gat
        st["beta_ev"] = e_bet
        gv = bass.AP(b.gat, 0, [[64, 128], [16, 4], [1, 4]])
        hv = bass.AP(b.gat, 4, [[64, 128], [16, 4], [1, 4]])
        bgap = bass.AP(b.bgt, 0, [[8, 128], [0, 4], [1, 4]])
        bhap = bass.AP(b.bgt, 4, [[8, 128], [0, 4], [1, 4]])
        e1 = P.op("dve", lambda v: v.tensor_tensor(gv, gv, bgap, ALU.add), after=[e_gat])
        e2 = P.op("dve", lambda v: v.tensor_tensor(hv, hv, bhap, ALU.add), after=[e1])
        m_ = b.rsc[:, 0:16]
        m_v = m_.rearrange("p (s c) -> p s c", s=4)
        e3 = P.op("dve", lambda v: v.tensor_tensor(m_v, gv, hv, ALU.max), after=[e2])
        e4 = P.op("dve", lambda v: v.tensor_scalar_max(m_, m_, 0.0), after=[e3])
        eg_, ehg = b.rsc[:, 16:32], b.rsc[:, 32:48]
        e5 = P.op("dve", lambda v: v.tensor_sub(
            eg_.rearrange("p (s c) -> p s c", s=4), gv, m_v), after=[e4])
        e6 = P.op("dve", lambda v: v.tensor_sub(
            ehg.rearrange("p (s c) -> p s c", s=4), hv, m_v), after=[e5])
        wl, wg, wr = b.w3[:, 0:16], b.w3[:, 16:32], b.w3[:, 32:48]
        e7 = P.op("act", lambda a: a.activation(wl, eg_, AF.Exp), after=[e5])
        e8 = P.op("act", lambda a: a.activation(wg, ehg, AF.Exp), after=[e6])
        e9 = P.op("act", lambda a: a.activation(wr, m_, AF.Exp, scale=-1.0), after=[e6])
        den = b.rsc[:, 48:64]
        e10 = P.op("dve", lambda v: v.tensor_add(den, wl, wg), after=[e7, e8])
        e11 = P.op("dve", lambda v: v.tensor_add(den, den, wr), after=[e10, e9])
        e12 = P.op("dve", lambda v: v.reciprocal(den, den), after=[e11])
        e13 = P.op("dve", lambda v: v.tensor_mul(wl, wl, den), after=[e12])
        e14 = P.op("dve", lambda v: v.tensor_mul(wg, wg, den), after=[e13])
        e15 = P.op("dve", lambda v: v.tensor_mul(wr, wr, den), after=[e14])
        e16 = P.op("dve", lambda v: v.tensor_scalar(
            b.w3n[:, 0:16], wl, -1.0, None, ALU.mult), after=[e13])
        e17 = P.op("dve", lambda v: v.tensor_scalar(
            b.w3n[:, 16:32], wg, -1.0, None, ALU.mult), after=[e14])
        st["w3_ev"] = [e15, e16, e17]
        rv = bass.AP(b.gat, 8, [[64, 128], [16, 4], [1, 8]])
        m8, dn8, ts4, rd2 = (b.rsc[:, 64:68], b.rsc[:, 68:72],
                             b.rsc[:, 72:76], b.rsc[:, 76:80])
        r1 = P.op("dve", lambda v: v.reduce_max(m8, rv, AX.X), after=[e_gat])
        subs = []
        for s in range(TC):
            subs.append(P.op("dve", lambda v, s=s: v.tensor_scalar(
                b.sq_scr[:, s * 8:(s + 1) * 8], b.gat[:, s * 16 + 8:s * 16 + 16],
                m8[:, s:s + 1], None, ALU.subtract), after=[r1]))
        r2 = P.op("act", lambda a: a.activation(
            b.rw_tm[:, :], b.sq_scr[:, 0:32], AF.Exp), after=subs)
        r3 = P.op("dve", lambda v: v.reduce_sum(
            dn8, b.rw_tm[:, :].rearrange("p (s c) -> p s c", s=4), AX.X), after=[r2])
        r4 = P.op("dve", lambda v: v.reciprocal(dn8, dn8), after=[r3])
        dnap = bass.AP(b.rsc, 68, [[80, 128], [1, 4], [0, 8]])
        r5 = P.op("dve", lambda v: v.tensor_tensor(
            b.rw_tm[:, :].rearrange("p (s c) -> p s c", s=4),
            b.rw_tm[:, :].rearrange("p (s c) -> p s c", s=4), dnap, ALU.mult),
            after=[r4])
        srts = []
        for s in range(TC):
            srts.append(P.op("dve", lambda v, s=s: v.max(
                b.srt[:, s * 8:(s + 1) * 8], b.rw_tm[:, s * 8:(s + 1) * 8]),
                after=[r5]))
        r6 = P.op("dve", lambda v: v.reduce_sum(
            ts4, bass.AP(b.srt, 0, [[32, 128], [8, 4], [1, 4]]), AX.X), after=srts)
        r7 = P.op("dve", lambda v: v.reciprocal(rd2, ts4), after=[r6])
        keeps = []
        for s in range(TC):
            keeps.append(P.op("dve", lambda v, s=s: v.tensor_scalar(
                b.sq_scr[:, s * 8:(s + 1) * 8], b.rw_tm[:, s * 8:(s + 1) * 8],
                b.srt[:, s * 8 + 3:s * 8 + 4], None, ALU.is_ge), after=[r7]))
        r8 = P.op("dve", lambda v: v.tensor_mul(
            b.rw_tm[:, :], b.rw_tm[:, :], b.sq_scr[:, 0:32]), after=keeps)
        rd2ap = bass.AP(b.rsc, 76, [[80, 128], [1, 4], [0, 8]])
        e_rw = P.op("dve", lambda v: v.tensor_tensor(
            b.rw_tm[:, :].rearrange("p (s c) -> p s c", s=4),
            b.rw_tm[:, :].rearrange("p (s c) -> p s c", s=4), rd2ap, ALU.mult),
            after=[r8])
        st["rw_ev"] = e_rw
        return self.experts(P, t, ev_ln, e_rw, st)

    def experts(self, P, t, ev_ln, e_rw, st):
        b = self
        upv = lambda par: b.upb[:, par * 2048:(par + 1) * 2048]
        dnv = lambda par: b.dnb[:, par * 2048:(par + 1) * 2048]
        e_up, e_dn = {}, {}

        def fetch_up(e, he):
            par = (e * 16 + he) % 2
            apd = [[HF, 128], [128 * HF, 8], [1, 256]]
            off = e * D * HF + he * 256
            srcap = bass.AP(b.up_h if t == 0 else b.up_s, off, apd)
            e_up[(e, he)] = P.dma("pool", "up%d" % par,
                                  upv(par).rearrange("p (c n) -> p c n", c=8),
                                  srcap,
                                  after=_f([st["up_free"][par],
                                            st["upw"][par]]))
            if t == 0:
                st["upw"][par] = P.dma(
                    "sync", "uw%d" % par, bass.AP(b.up_s, off, apd),
                    upv(par).rearrange("p (c n) -> p c n", c=8),
                    after=[e_up[(e, he)]])

        def fetch_dn(e, ke):
            par = (e * 16 + ke) % 2
            apd = [[D, 128], [128 * D, 2], [1, 1024]]
            off = e * HF * D + ke * 256 * D
            srcap = bass.AP(b.dn_h if t == 0 else b.dn_s, off, apd)
            e_dn[(e, ke)] = P.dma("pool", "dn%d" % par,
                                  dnv(par).rearrange("p (c n) -> p c n", c=2),
                                  srcap,
                                  after=_f([st["dn_free"][par],
                                            st["dnw"][par]]))
            if t == 0:
                st["dnw"][par] = P.dma(
                    "sync", "dw%d" % par, bass.AP(b.dn_s, off, apd),
                    dnv(par).rearrange("p (c n) -> p c n", c=2),
                    after=[e_dn[(e, ke)]])

        fetch_up(0, 0)
        fetch_up(0, 1)
        e_gacc = None
        for e in range(NE):
            gelu_ev = {}
            for he in range(16):
                par = (e * 16 + he) % 2
                for kk in range(2):
                    k = he * 2 + kk
                    bk = 4 + (k % 4)
                    mm = None
                    for dd in range(8):
                        aft = _f([e_up[(e, he)], st["bank"][bk], ev_ln]) if dd == 0 else ()
                        mm = P.op("tensor", lambda pe, dd=dd, bk=bk, par=par, kk=kk: pe.matmul(
                            self.bank(bk),
                            upv(par)[:, dd * 256 + kk * 128:dd * 256 + kk * 128 + 128],
                            b.xn_fm[:, dd * 512:(dd + 1) * 512],
                            start=(dd == 0), stop=(dd == 7)),
                            after=aft, inc=(dd == 7))
                    if kk == 1:
                        st["up_free"][par] = mm
                    gaft = [mm] + ([st["gh_rd"], st.get("tail_ev")] if k == 0 else [])
                    gelu_ev[k] = P.op("act", lambda a, k=k, bk=bk: a.activation(
                        b.gh[:, k * 512:(k + 1) * 512], self.bank(bk),
                        AF.Gelu_apprx_tanh), after=_f(gaft))
                    st["bank"][bk] = gelu_ev[k]
                if he + 2 < 16:
                    fetch_up(e, he + 2)
                elif e + 1 < NE:
                    fetch_up(e + 1, he - 14)
                if he == 15:
                    fetch_dn(e, 0)
                    fetch_dn(e, 1)
            mm = None
            for ke in range(16):
                par = (e * 16 + ke) % 2
                for kk in range(2):
                    k = ke * 2 + kk
                    for tc in range(TC):
                        for half in range(2):
                            bki = tc * 2 + half
                            aft = []
                            if tc == 0 and half == 0:
                                aft = [e_dn[(e, ke)], gelu_ev[k]]
                            if k == 0:
                                aft.append(st["bank"][bki])
                            mm = P.op("tensor",
                                      lambda pe, k=k, tc=tc, half=half, par=par, kk=kk, bki=bki:
                                      pe.matmul(
                                          self.bank(bki),
                                          b.gh[:, k * 512 + tc * 128:k * 512 + tc * 128 + 128],
                                          dnv(par)[:, kk * 1024 + half * 512:kk * 1024 + half * 512 + 512],
                                          start=(k == 0), stop=(k == 31)),
                                      after=_f(aft), inc=(kk == 1 and tc == 3 and half == 1))
                    if kk == 1:
                        st["dn_free"][par] = mm
                if ke + 2 < 16:
                    fetch_dn(e, ke + 2)
            st["gh_rd"] = mm
            for tc in range(TC):
                for half in range(2):
                    bki = tc * 2 + half
                    dst = b.g_acc[:, tc * 1024 + half * 512:tc * 1024 + half * 512 + 512]
                    rwc = b.rw_tm[:, tc * 8 + e:tc * 8 + e + 1]
                    if e == 0:
                        ev2 = P.op("dve", lambda v, dst=dst, rwc=rwc, bki=bki: v.tensor_scalar(
                            dst, self.bank(bki), rwc, None, ALU.mult),
                            after=_f([mm, e_rw]))
                    else:
                        ev2 = P.op("dve", lambda v, dst=dst, rwc=rwc, bki=bki:
                                   v.scalar_tensor_tensor(
                                       dst, self.bank(bki), rwc, dst, ALU.mult, ALU.add),
                                   after=[mm])
                    st["bank"][bki] = ev2
                    e_gacc = ev2
        return e_gacc

    def phase_attn(self, P, t, ev_ln, st):
        for hp in range(2):
            self.proj_phase(P, t, hp, ev_ln, st)
            if PHASE >= 4:
                self.scan_phase(P, t, hp, st)
        if PHASE >= 5:
            self.wo_phase(P, t, st)

    def proj_phase(self, P, t, hp, ev_ln, st):
        b = self
        srcs = [b.Wq_h, b.Wk_h, b.Wv_h]
        wq_ev = {}

        def fetch_w(slot, h_src, col0, mat):
            par = slot % 2
            dst = b.wqb[:, par * 4096:(par + 1) * 4096].rearrange("p (c n) -> p c n", c=8)
            apd = [[D, 128], [128 * D, 8], [1, 512]]
            if t == 0:
                srcap = bass.AP(h_src, col0, apd)
            else:
                srcap = bass.AP(b.wq_s, mat * D * D + col0, apd)
            wq_ev[slot] = P.dma("pool", "wq%d" % par, dst, srcap,
                                after=_f([st["wq_free"][par],
                                          st["wqw"][par]]))
            if t == 0:
                st["wqw"][par] = P.dma(
                    "sync", "ww%d" % par, bass.AP(b.wq_s, mat * D * D + col0, apd),
                    dst, after=[wq_ev[slot]])

        base = hp * 3
        if hp == 0:
            fetch_w(0, srcs[0], 0, 0)
        else:
            wq_ev[base] = st.pop("wq_next")
        for pj in range(3):
            slot = base + pj
            par = slot % 2
            if pj + 1 < 3:
                fetch_w(slot + 1, srcs[pj + 1], hp * 512, pj + 1)
            elif hp == 0:
                fetch_w(slot + 1, srcs[0], 512, 0)
                st["wq_next"] = wq_ev[slot + 1]
            e_pre = {}
            for m in range(4):
                bk = 4 + (m % 2)
                mm = None
                for dd in range(8):
                    aft = _f([wq_ev[slot], st["bank"][bk], ev_ln]) if dd == 0 else ()
                    mm = P.op("tensor", lambda pe, m=m, dd=dd, bk=bk, par=par: pe.matmul(
                        self.bank(bk),
                        b.wqb[:, par * 4096 + dd * 512 + m * 128:par * 4096 + dd * 512 + m * 128 + 128],
                        b.xn_fm[:, dd * 512:(dd + 1) * 512],
                        start=(dd == 0), stop=(dd == 7)), after=aft, inc=(dd == 7))
                if m == 3:
                    st["wq_free"][par] = mm
                pcol = (pj * 4 + m) * 515
                aft2 = [mm]
                if pj == 0 and m == 0:
                    aft2 += _f([st["scan_done"], st.get("rw_ev"), st.get("gh_rd")])
                e_pre[m] = P.op("act", lambda a, pcol=pcol, bk=bk: a.activation(
                    b.pre[:, pcol + 3:pcol + 515], self.bank(bk), AF.Copy),
                    after=_f(aft2))
                st["bank"][bk] = e_pre[m]
            e_sil = {}
            for m in range(4):
                mg = hp * 4 + m
                pcol = (pj * 4 + m) * 515
                tcol = (pj * 8 + mg) * 3
                acc = b.sf32[:, 0:512]
                tmp = b.sf32[:, 512:1024]
                tapc = lambda j, pj=pj, mg=mg: b.convt[:, pj * 32 + mg * 4 + j:pj * 32 + mg * 4 + j + 1]
                e_ti = P.op("pool", lambda g, pcol=pcol, tcol=tcol: g.tensor_copy(
                    b.pre[:, pcol:pcol + 3], b.tails[:, tcol:tcol + 3]),
                    after=_f([e_pre[m], st.get("tail_ev")] + st["init"]))
                e_c = P.op("dve", lambda v, pcol=pcol, tapc=tapc, acc=acc: v.tensor_scalar(
                    acc, b.pre[:, pcol + 3:pcol + 515], tapc(3), None, ALU.mult),
                    after=_f([e_ti, st.get("acc_rd")]))
                for j in (2, 1, 0):
                    e_c = P.op("dve", lambda v, pcol=pcol, j=j, tapc=tapc, acc=acc:
                               v.scalar_tensor_tensor(
                                   acc, b.pre[:, pcol + j:pcol + j + 512], tapc(j),
                                   acc, ALU.mult, ALU.add), after=[e_c])
                st["tail_ev"] = P.op("pool", lambda g, pcol=pcol, tcol=tcol: g.tensor_copy(
                    b.tails[:, tcol:tcol + 3], b.pre[:, pcol + 512:pcol + 515]),
                    after=[e_c])
                if pj < 2:
                    dstb = [b.qbf, b.kbf][pj]
                    e_sil[m] = P.op("act", lambda a, m=m, dstb=dstb, acc=acc: a.activation(
                        dstb[:, m * 512:(m + 1) * 512], acc, AF.Silu),
                        after=_f([e_c, st.get("qk_rd")]))
                    st["acc_rd"] = e_sil[m]
                else:
                    e_sv = P.op("act", lambda a, acc=acc: a.activation(
                        b.sf32[:, 1024:1536], acc, AF.Silu),
                        after=_f([e_c, st.get("sl2_rd")]))
                    st["acc_rd"] = e_sv
                    self.kv_transpose(P, st, b.sf32[:, 1024:1536], True, m // 2, m % 2, e_sv)
            if pj < 2:
                self.l2_phase(P, hp, pj, e_sil, st)

    def l2_phase(self, P, hp, pj, e_sil, st):
        b = self
        dstb = [b.qbf, b.kbf][pj]
        rs = self.PSB[0:2, 3 * 512:4 * 512]
        mm = None
        for m in range(4):
            e_q = P.op("dve", lambda v, m=m, dstb=dstb: v.tensor_mul(
                b.sq_scr[:, 0:512], dstb[:, m * 512:(m + 1) * 512],
                dstb[:, m * 512:(m + 1) * 512]),
                after=_f([e_sil[m]] + ([mm] if mm else []) +
                         ([st.get("rw_ev")] if m == 0 else [])))
            aft = _f([e_q] + ([st["bank"][7]] if m == 0 else []))
            mm = P.op("tensor", lambda pe, m=m: pe.matmul(
                rs, b.c2[:, (m // 2) * 2:(m // 2) * 2 + 2], b.sq_scr[:, 0:512],
                start=(m == 0), stop=(m == 3)), after=aft, inc=True)
        rn = b.sf32[0:2, 1536:2048]
        e_rt = P.op("act", lambda a, rn=rn: a.activation(
            rn, rs, AF.Sqrt, bias=b.ceps[0:2, 1:2], scale=1.0),
            after=_f([mm, st.get("sl3_rd")]))
        e_rc = P.op("dve", lambda v, rn=rn: v.reciprocal(rn, rn), after=[e_rt])
        st["bank"][7] = e_rc
        mmb_last = None
        for h in range(2):
            bcast = self.bank(6)
            mmb = P.op("tensor", lambda pe, h=h, rn=rn: pe.matmul(
                bcast, b.bc2[:, h * 128:(h + 1) * 128], rn, start=True, stop=True),
                after=_f([e_rc, st["bank"][6]]), inc=True)
            mmb_last = mmb
            last = None
            for m in (2 * h, 2 * h + 1):
                e_n = P.op("dve", lambda v, m=m, dstb=dstb, bcast=bcast: v.tensor_tensor(
                    dstb[:, m * 512:(m + 1) * 512], dstb[:, m * 512:(m + 1) * 512],
                    bcast, ALU.mult), after=[mmb])
                last = e_n
                if pj == 1:
                    e_fc = P.op("dve", lambda v, m=m, dstb=dstb: v.tensor_copy(
                        b.sf32[:, 1024:1536], dstb[:, m * 512:(m + 1) * 512]),
                        after=_f([e_n, st.get("sl2_rd")]))
                    self.kv_transpose(P, st, b.sf32[:, 1024:1536], False, h, m % 2, e_fc)
            st["bank"][6] = last
        st["sl3_rd"] = mmb_last
        st["qk_rd"] = None
        if pj == 0:
            st["qbf_ev"] = last
        else:
            st["kbf_ev"] = last

    def kv_transpose(self, P, st, src, is_v, h, vh, after):
        b = self
        bk = 7
        mm = None
        for ts_ in range(4):
            mm = P.op("tensor", lambda pe, ts_=ts_, bk=bk, src=src: pe.transpose(
                self.bank(bk)[:, ts_ * 128:(ts_ + 1) * 128],
                src[:, ts_ * 128:(ts_ + 1) * 128], b.idf[:, :]),
                after=_f([after, st["bank"][bk]]) if ts_ == 0 else (), inc=(ts_ == 3))
        st["sl2_rd"] = mm
        dstt = b.v_tm if is_v else b.k_tm
        dst = bass.AP(dstt, h * 256 + vh * 128, [[2048, 128], [512, 4], [1, 128]])
        e_ev = P.op("dve", lambda v, dst=dst, bk=bk: v.tensor_copy(
            dst, self.bank(bk)[:, 0:512].rearrange("p (c n) -> p c n", c=4)),
            after=[mm])
        st["bank"][bk] = e_ev
        st["tm_ev"] = e_ev

    def _ap32(self, tsr, freesz, base, g, col, n=32):
        return bass.AP(tsr, 32 * g * freesz + base + col, [[freesz, 32], [1, n]])

    def scan_phase(self, P, t, hp, st):
        b = self
        if t > 0:
            seed = None
            for r in range(4):
                aft = _f([st["bank"][0], st["bank"][1], st["bank"][2], st["bank"][3],
                          st["s32_ev"][hp]]) if r == 0 else ()
                seed = P.op("tensor", lambda pe, r=r: pe.matmul(
                    self.PSA[:, r * 512:(r + 1) * 512], b.idf[:, :],
                    b.S32[:, hp * 2048 + r * 512:hp * 2048 + (r + 1) * 512],
                    start=True, stop=False, skip_group_check=True),
                    after=aft, inc=(r == 3))
            e_supd = seed
        else:
            e_supd = None
        kq = _f([st.get("kbf_ev"), st.get("qbf_ev"), st.get("tm_ev")])
        e_z = P.op("dve", lambda v: v.memset(self.PSB[:, 0:1024], 0.0),
                   after=_f([st["bank"][4], st["bank"][5]]))
        st["bank"][4] = e_z
        st["bank"][5] = e_z
        mmA = None
        for h in range(2):
            for s in range(TC):
                for g in range(4):
                    c = s * 4 + g
                    for kc in range(2):
                        aft = _f([st["bank"][4], st["bank"][5]] + kq) \
                            if (h == 0 and s == 0 and g == 0 and kc == 0) else ()
                        col = (h * 4 + s) * 128 + g * 32
                        mmA = P.op("tensor", lambda pe, h=h, c=c, g=g, kc=kc, col=col: pe.matmul(
                            self.PSB[32 * g:32 * g + 32, col:col + 32],
                            b.kbf[:, (2 * h + kc) * 512 + c * 32:(2 * h + kc) * 512 + c * 32 + 32],
                            b.kbf[:, (2 * h + kc) * 512 + c * 32:(2 * h + kc) * 512 + c * 32 + 32],
                            start=(kc == 0), stop=(kc == 1), tile_position=(0, 32 * g)),
                            after=aft, inc=(kc == 1))
        mmS = None
        for h in range(2):
            for s in range(TC):
                for g in range(4):
                    c = s * 4 + g
                    for kc in range(2):
                        aft = _f([st["bank"][6]]) if (h == 0 and s == 0 and g == 0 and kc == 0) else ()
                        col = 2 * 512 + (h * 4 + s) * 32
                        mmS = P.op("tensor", lambda pe, h=h, c=c, g=g, kc=kc, col=col: pe.matmul(
                            self.PSB[32 * g:32 * g + 32, col:col + 32],
                            b.kbf[:, (2 * h + kc) * 512 + c * 32:(2 * h + kc) * 512 + c * 32 + 32],
                            b.qbf[:, (2 * h + kc) * 512 + c * 32:(2 * h + kc) * 512 + c * 32 + 32],
                            start=(kc == 0), stop=(kc == 1), tile_position=(0, 32 * g)),
                            after=aft, inc=(kc == 1))
        if SCANSUB < 2:
            st["scan_done"] = mmS
            st["nmT_rd"] = mmS
            st["nmn_rd"] = mmS
            st["s32_ev"][hp] = mmS
            return
        e_a = P.op("dve", lambda v: v.tensor_tensor(
            b.a_sb[:, :].rearrange("p (q j) -> p q j", q=8),
            bass.AP(self.PSB, 2 * 512, [[2048, 128], [32, 8], [1, 32]]),
            bass.AP(b.maskIncl, 0, [[32, 128], [0, 8], [1, 32]]), ALU.mult),
            after=[mmS])
        st["bank"][6] = e_a
        e_As = {}
        e_msk = None
        for scan in range(2):
            e_msk = P.op("dve", lambda v, scan=scan: v.tensor_tensor(
                bass.AP(b.nmn, scan * 1024, [[2048, 128], [128, 8], [32, 4], [1, 32]]),
                bass.AP(self.PSB, 0, [[2048, 128], [128, 8], [32, 4], [1, 32]]),
                bass.AP(b.maskSneg, 0, [[32, 128], [0, 8], [0, 4], [1, 32]]), ALU.mult),
                after=_f([mmA] + ([st.get("nmn_rd")] if scan == 0 else [])))
            bap = bass.AP(b.beta_tm, scan * 4 + 2 * hp,
                          [[32, 128], [1, 2], [8, 4], [0, 128]])
            e_As[scan] = P.op("dve", lambda v, scan=scan, bap=bap: v.tensor_tensor(
                bass.AP(b.nmn, scan * 1024, [[2048, 128], [512, 2], [128, 4], [1, 128]]),
                bass.AP(b.nmn, scan * 1024, [[2048, 128], [512, 2], [128, 4], [1, 128]]),
                bap, ALU.mult),
                after=_f([e_msk, st.get("beta_ev")]))
        st["bank"][4] = e_As[1]
        st["bank"][5] = e_As[1]
        if SCANSUB < 3:
            st["scan_done"] = e_As[1]
            st["nmT_rd"] = e_As[1]
            st["nmn_rd"] = e_As[1]
            st["s32_ev"][hp] = e_As[1]
            return
        for scan in range(2):
            e_abf = P.op("dve", lambda v, scan=scan: v.tensor_copy(
                b.nmB[:, 0:1024], b.nmn[:, scan * 1024:(scan + 1) * 1024]),
                after=_f([e_As[scan], st.get("nmB_rd")]))
            mm = None
            for blk in range(8):
                bk = 4 + blk // 4
                mm = P.op("tensor", lambda pe, blk=blk, bk=bk, scan=scan: pe.transpose(
                    self.bank(bk)[:, (blk % 4) * 128:(blk % 4) * 128 + 128],
                    b.nmn[:, scan * 1024 + blk * 128:scan * 1024 + (blk + 1) * 128],
                    b.idf[:, :]),
                    after=_f([e_As[scan], st["bank"][4], st["bank"][5]]) if blk == 0 else (),
                    inc=(blk == 7))
            e_Ap = P.op("dve", lambda v: v.tensor_copy(
                b.nmA[:, :].rearrange("p (u n) -> p u n", u=2),
                bass.AP(self.PSB, 0, [[2048, 128], [512, 2], [1, 512]])),
                after=_f([mm, st.get("nmA_rd")]))
            st["bank"][4] = e_Ap
            st["bank"][5] = e_Ap
            if SCANSUB < 4:
                st["scan_done"] = e_Ap
                st["nmT_rd"] = e_Ap
                st["nmn_rd"] = e_Ap
                st["s32_ev"][hp] = e_Ap
                st["ttil_%d" % scan] = e_Ap
                continue
            def bap32(slot, g, col, n=32):
                return self._ap32(b.nmB, 7168, slot * 1024, g, col, n)

            def aap32(g, col, n=32):
                return self._ap32(b.nmA, 1024, 0, g, col, n)

            chain = [(1, "A'", 0), (5, 0, "A'"), (2, 5, 1), (6, 1, 5),
                     (3, 6, 2), (5, 2, 6), (4, 5, 3)]
            prev_ev = [e_abf, e_Ap]
            for (dsts, lt, rh) in chain:
                mm = None
                for blk in range(8):
                    def sl_of(x, blk=0):
                        t_ = b.nmA if x == "A'" else None
                        return None
                    lt_ap = (b.nmA if lt == "A'" else
                             b.nmB[:, lt * 1024:(lt + 1) * 1024])
                    rh_ap = (b.nmA if rh == "A'" else
                             b.nmB[:, rh * 1024:(rh + 1) * 1024])
                    mm = P.op("tensor",
                              lambda pe, blk=blk, lt_ap=lt_ap, rh_ap=rh_ap: pe.matmul(
                                  self.PSB[:, blk * 128:(blk + 1) * 128],
                                  lt_ap[:, blk * 128:(blk + 1) * 128],
                                  rh_ap[:, blk * 128:(blk + 1) * 128],
                                  start=True, stop=True),
                              after=_f(prev_ev + [st["bank"][4], st["bank"][5]])
                              if blk == 0 else (),
                              inc=(blk == 7))
                e_ev = P.op("dve", lambda v, dsts=dsts: v.tensor_copy(
                    b.nmB[:, dsts * 1024:(dsts + 1) * 1024].rearrange("p (u n) -> p u n", u=2),
                    bass.AP(self.PSB, 0, [[2048, 128], [512, 2], [1, 512]])),
                    after=[mm])
                st["bank"][4] = e_ev
                st["bank"][5] = e_ev
                prev_ev = [e_ev]
            if SCANSUB < 5:
                st["scan_done"] = prev_ev[-1]
                st["nmT_rd"] = prev_ev[-1]
                st["nmn_rd"] = prev_ev[-1]
                st["s32_ev"][hp] = prev_ev[-1]
                st["ttil_%d" % scan] = prev_ev[-1]
                continue
            e_r = P.op("dve", lambda v: v.tensor_add(
                bass.AP(b.nmR, 0, [[1024, 128], [128, 8], [1, 128]]),
                bass.AP(b.nmA, 0, [[1024, 128], [128, 8], [1, 128]]),
                bass.AP(b.idf, 0, [[128, 128], [0, 8], [1, 128]])),
                after=_f([e_Ap, st.get("nmR_rd")]))
            first_r = True
            for bslot in (1, 2, 3, 4):
                mm = None
                for blk in range(8):
                    aft = ()
                    if blk == 0:
                        aft = _f([e_r] + ([st["bank"][6], st["bank"][7]]
                                          if first_r else []))
                    mm = P.op("tensor",
                              lambda pe, blk=blk, bslot=bslot: pe.matmul(
                                  self.PSB[:, 1024 + blk * 128:1024 + (blk + 1) * 128],
                                  b.nmB[:, bslot * 1024 + blk * 128:bslot * 1024 + (blk + 1) * 128],
                                  b.nmR[:, blk * 128:(blk + 1) * 128],
                                  start=True, stop=True),
                              after=aft, inc=(blk == 7))
                first_r = False
                e_r = P.op("dve", lambda v: v.tensor_add(
                    b.nmR[:, :], b.nmR[:, :], self.PSB[:, 1024:2048]), after=[mm])
            st["bank"][6] = e_r
            st["bank"][7] = e_r
            bap = bass.AP(b.beta_tm, scan * 4 + 2 * hp,
                          [[32, 128], [1, 2], [8, 4], [0, 128]])
            e_tt = P.op("dve", lambda v, scan=scan, bap=bap: v.tensor_tensor(
                bass.AP(b.nmT, scan * 1024, [[2048, 128], [512, 2], [128, 4], [1, 128]]),
                bass.AP(b.nmR, 0, [[1024, 128], [128, 8], [32, 4], [1, 128]])
                if False else
                bass.AP(b.nmR, 0, [[1024, 128], [512, 2], [128, 4], [1, 128]]),
                bap, ALU.mult),
                after=_f([e_r, st.get("nmT_rd")]))
            st["nmR_rd"] = e_tt
            st["nmA_rd"] = e_r
            st["nmB_rd"] = e_r
            st["ttil_%d" % scan] = e_tt
        if SCANSUB < 6:
            ev = st["ttil_1"]
            st["scan_done"] = ev
            st["nmT_rd"] = ev
            st["nmn_rd"] = ev
            st["s32_ev"][hp] = ev
            return
        e_prev_chunk = None
        for s in range(TC):
            ev_u0, ev_wm = {}, {}
            for scan in range(2):
                bk = 6 + scan
                mm = None
                for h in range(2):
                    mm = P.op("tensor", lambda pe, h=h, scan=scan, bk=bk, s=s: pe.matmul(
                        self.bank(bk)[:, h * 256:h * 256 + 256],
                        b.nmT[:, scan * 1024 + (h * 4 + s) * 128:
                              scan * 1024 + (h * 4 + s) * 128 + 128],
                        b.v_tm[:, (s * 2 + h) * 256:(s * 2 + h) * 256 + 256],
                        start=True, stop=True),
                        after=_f([st["ttil_%d" % scan], st["bank"][bk]])
                        if h == 0 else (), inc=(h == 1))
                wap = bass.AP(b.w3, scan * 16 + s * 4 + 2 * hp,
                              [[48, 128], [1, 2], [0, 256]])
                ev_u0[scan] = P.op("dve", lambda v, scan=scan, bk=bk, wap=wap: v.tensor_tensor(
                    b.u0s[:, scan * 512:(scan + 1) * 512].rearrange("p (h n) -> p h n", h=2),
                    self.bank(bk).rearrange("p (h n) -> p h n", h=2), wap, ALU.mult),
                    after=_f([mm] + st.get("w3_ev", [])))
                st["bank"][bk] = ev_u0[scan]
                mm2 = None
                for h in range(2):
                    for kc in range(2):
                        mm2 = P.op("tensor",
                                   lambda pe, scan=scan, bk=bk, s=s, h=h, kc=kc: pe.matmul(
                                       self.bank(bk)[:, (h * 2 + kc) * 128:(h * 2 + kc) * 128 + 128],
                                       b.k_tm[:, (s * 2 + h) * 256 + kc * 128:
                                              (s * 2 + h) * 256 + kc * 128 + 128],
                                       b.nmT[:, scan * 1024 + (h * 4 + s) * 128:
                                             scan * 1024 + (h * 4 + s) * 128 + 128],
                                       start=True, stop=True),
                                   after=_f([ev_u0[scan]]) if (h == 0 and kc == 0) else (),
                                   inc=(h == 1 and kc == 1))
                ev_wm[scan] = P.op("dve", lambda v, scan=scan, bk=bk: v.tensor_copy(
                    b.wm_sb[:, scan * 512:(scan + 1) * 512], self.bank(bk)), after=[mm2])
                st["bank"][bk] = ev_wm[scan]
            if SCANSUB < 7:
                ev = ev_wm[1]
                st["scan_done"] = ev
                st["nmT_rd"] = ev
                st["nmn_rd"] = ev
                st["s32_ev"][hp] = ev
                continue
            for g in range(4):
                c = s * 4 + g
                if not (t == 0 and c == 0):
                    afts = _f([e_supd, e_prev_chunk, st.get("o_mm_prev")])
                    e_sn1 = P.op("act", lambda a: a.activation(
                        b.S_bf[:, 0:1024], self.PSA[:, 0:1024], AF.Copy), after=afts)
                    e_sn2 = P.op("dve", lambda v: v.tensor_copy(
                        b.S_bf[:, 1024:2048], self.PSA[:, 1024:2048]), after=afts)
                    e_snap = e_sn1
                    e_snap2 = e_sn2
                else:
                    e_snap = None
                    e_snap2 = None
                e_wmS = None
                if e_snap is not None:
                    mm = None
                    for h in range(2):
                        for scan in range(2):
                            for kc in range(2):
                                col4 = (h * 2 + scan) * 256
                                mm = P.op("tensor",
                                          lambda pe, h=h, scan=scan, kc=kc, g=g, col4=col4, s=s:
                                          pe.matmul(
                                              self.PSB[32 * g:32 * g + 32, col4:col4 + 256],
                                              b.wm_sb[:, scan * 512 + (h * 2 + kc) * 128 + g * 32:
                                                      scan * 512 + (h * 2 + kc) * 128 + g * 32 + 32],
                                              b.S_bf[:, (h * 2 + scan) * 512 + kc * 256:
                                                     (h * 2 + scan) * 512 + kc * 256 + 256],
                                              start=(kc == 0), stop=(kc == 1),
                                              tile_position=(0, 32 * g)),
                                          after=_f([e_snap, e_snap2, ev_wm[0], ev_wm[1],
                                                    st["bank"][4], st["bank"][5]])
                                          if (h == 0 and scan == 0 and kc == 0) else (),
                                          inc=(h == 1 and scan == 1 and kc == 1))
                    e_wmS = mm
                e_us = []
                for h in range(2):
                    for scan in range(2):
                        dst = self._ap32(b.u_sb, 1024, (h * 2 + scan) * 256, g, 0, 256)
                        src0 = self._ap32(b.u0s, 1024, scan * 512 + h * 256, g, 0, 256)
                        if e_wmS is None:
                            e_us.append(P.op("dve", lambda v, dst=dst, src0=src0: v.tensor_copy(
                                dst, src0), after=_f([ev_u0[scan]])))
                        else:
                            wnc = b.w3n[32 * g:32 * g + 32,
                                        scan * 16 + s * 4 + 2 * hp + h:
                                        scan * 16 + s * 4 + 2 * hp + h + 1]
                            psrc = self.PSB[32 * g:32 * g + 32,
                                            (h * 2 + scan) * 256:(h * 2 + scan) * 256 + 256]
                            e_us.append(P.op("dve",
                                             lambda v, dst=dst, src0=src0, wnc=wnc, psrc=psrc:
                                             v.scalar_tensor_tensor(dst, psrc, wnc, src0,
                                                                    ALU.mult, ALU.add),
                                             after=[e_wmS]))
                st["bank"][4] = e_us[-1]
                st["bank"][5] = e_us[-1]
                e_usum = P.op("dve", lambda v, g=g: v.tensor_add(
                    bass.AP(b.usum, 32 * g * 512, [[512, 32], [256, 2], [1, 256]]),
                    bass.AP(b.u_sb, 32 * g * 1024, [[1024, 32], [512, 2], [1, 256]]),
                    bass.AP(b.u_sb, 32 * g * 1024 + 256, [[1024, 32], [512, 2], [1, 256]])),
                    after=e_us)
                mm = None
                for h in range(2):
                    for scan in range(2):
                        for kc in range(2):
                            sstart = (t == 0 and c == 0 and kc == 0)
                            mm = P.op("tensor",
                                      lambda pe, h=h, scan=scan, kc=kc, g=g, s=s, sstart=sstart:
                                      pe.matmul(
                                          self.PSA[:, (h * 2 + scan) * 512 + kc * 256:
                                                   (h * 2 + scan) * 512 + kc * 256 + 256],
                                          self._ap32(b.k_tm, 2048, (s * 2 + h) * 256 + kc * 128, g, 0, 128),
                                          self._ap32(b.u_sb, 1024, (h * 2 + scan) * 256, g, 0, 256),
                                          start=sstart, stop=False,
                                          tile_position=(32 * g, 0),
                                          skip_group_check=True),
                                      after=_f([e_usum] + ([e_snap, e_snap2] if e_snap else []))
                                      if (h == 0 and scan == 0 and kc == 0) else (),
                                      inc=(h == 1 and scan == 1 and kc == 1))
                e_supd = mm
                e_prev_chunk = mm
                mmo = None
                for h in range(2):
                    if e_snap is not None:
                        for scan in range(2):
                            for kc in range(2):
                                aft = _f([e_snap] + ([st["bank"][6]] if (h == 0 and scan == 0 and kc == 0 and g == 0) else []))
                                mmo = P.op("tensor",
                                           lambda pe, h=h, scan=scan, kc=kc, g=g, c=c:
                                           pe.matmul(
                                               self.bank(6)[32 * g:32 * g + 32, h * 256:h * 256 + 256],
                                               b.qbf[:, (2 * h + kc) * 512 + c * 32:
                                                     (2 * h + kc) * 512 + c * 32 + 32],
                                               b.S_bf[:, (h * 2 + scan) * 512 + kc * 256:
                                                      (h * 2 + scan) * 512 + kc * 256 + 256],
                                               start=(scan == 0 and kc == 0), stop=False,
                                               tile_position=(0, 32 * g)),
                                           after=(aft + _f([e_snap2])) if (scan == 0 and kc == 0) else (),
                                           inc=False)
                        astart = False
                    else:
                        astart = True
                    aft = _f([e_usum] + ([st["bank"][6]] if (astart and h == 0 and g == 0) else []))
                    mmo = P.op("tensor", lambda pe, h=h, g=g, s=s, astart=astart: pe.matmul(
                        self.bank(6)[32 * g:32 * g + 32, h * 256:h * 256 + 256],
                        self._ap32(b.a_sb, 256, (h * 4 + s) * 32, g, 0),
                        bass.AP(b.usum, 32 * g * 512 + h * 256, [[512, 32], [1, 256]]),
                        start=astart, stop=True, tile_position=(32 * g, 32 * g)),
                        after=aft, inc=(h == 1))
                st["o_mm_%d" % g] = mmo
                st["o_mm_prev"] = mmo
            if SCANSUB < 8:
                ev = e_prev_chunk
                st["scan_done"] = ev
                st["nmT_rd"] = ev
                st["nmn_rd"] = ev
                st["s32_ev"][hp] = ev
                st["bank"][6] = st.get("o_mm_3")
                continue
            o_ev = _f([st.get("o_mm_%d" % g) for g in range(4)])
            wrap = bass.AP(b.w3, 32 + s * 4 + 2 * hp, [[48, 128], [1, 2], [0, 256]])
            e_t1 = P.op("dve", lambda v, s=s, wrap=wrap: v.tensor_tensor(
                b.sf32[:, 1024:1536].rearrange("p (h n) -> p h n", h=2),
                b.v_tm[:, s * 512:(s + 1) * 512].rearrange("p (h n) -> p h n", h=2),
                wrap, ALU.mult),
                after=_f([st.get("sl2_rd")] + st.get("w3_ev", [])))
            e_t2 = P.op("dve", lambda v: v.tensor_add(
                b.sf32[:, 1024:1536], b.sf32[:, 1024:1536], self.bank(6)),
                after=[e_t1] + o_ev)
            st["bank"][6] = e_t2
            st["sl2_rd"] = e_t2
            e_ms = []
            prev = e_t2
            for h in range(2):
                prev = P.op("act", lambda a, h=h, s=s, hp=hp: a.activation(
                    b.sq_scr[:, 0:256], b.sf32[:, 1024 + h * 256:1024 + h * 256 + 256],
                    AF.Square, accum_out=b.rmsv[:, hp * 16 + s * 2 + h:hp * 16 + s * 2 + h + 1]),
                    after=[prev])
                e_ms.append(prev)
            e_rt = P.op("act", lambda a, s=s, hp=hp: a.activation(
                b.rmsv[:, hp * 16 + 8 + s * 2:hp * 16 + 8 + s * 2 + 2],
                b.rmsv[:, hp * 16 + s * 2:hp * 16 + s * 2 + 2],
                AF.Sqrt, bias=b.ceps[:, 0:1], scale=1.0 / 256), after=e_ms)
            e_rc = P.op("dve", lambda v, s=s, hp=hp: v.reciprocal(
                b.rmsv[:, hp * 16 + 8 + s * 2:hp * 16 + 8 + s * 2 + 2],
                b.rmsv[:, hp * 16 + 8 + s * 2:hp * 16 + 8 + s * 2 + 2]), after=[e_rt])
            rstdap = bass.AP(b.rmsv, hp * 16 + 8 + s * 2, [[32, 128], [1, 2], [0, 256]])
            e_orm = P.op("dve", lambda v, rstdap=rstdap: v.tensor_tensor(
                b.sf32[:, 1536:2048].rearrange("p (h n) -> p h n", h=2),
                b.sf32[:, 1024:1536].rearrange("p (h n) -> p h n", h=2),
                rstdap, ALU.mult), after=_f([e_rc, st.get("sl3_rd")])),
            e_orm = e_orm[0]
            mm = None
            for i in range(4):
                mm = P.op("tensor", lambda pe, i=i: pe.transpose(
                    self.bank(7)[:, i * 128:(i + 1) * 128],
                    b.sf32[:, 1536 + i * 128:1536 + (i + 1) * 128], b.idf[:, :]),
                    after=_f([e_orm, st["bank"][7]]) if i == 0 else (), inc=(i == 3))
            st["sl3_rd"] = mm
            dst = bass.AP(b.o_fm, hp * 2048 + s * 128,
                          [[4096, 128], [1024, 2], [512, 2], [1, 128]])
            e_ofm = P.op("dve", lambda v, dst=dst: v.tensor_copy(
                dst, bass.AP(self.PSB, 3 * 512, [[2048, 128], [256, 2], [128, 2], [1, 128]])),
                after=[mm])
            st["bank"][7] = e_ofm
            st["ofm_ev"] = e_ofm
        e_s32 = P.op("act", lambda a, hp=hp: a.activation(
            b.S32[:, hp * 2048:(hp + 1) * 2048], self.PSA[:, :], AF.Copy),
            after=_f([e_supd]))
        st["s32_ev"][hp] = e_s32
        for r in range(4):
            st["bank"][r] = e_s32
        st["scan_done"] = e_s32
        st["nmT_rd"] = e_s32
        st["nmn_rd"] = e_s32

    def wo_phase(self, P, t, st):
        b = self
        wo_ev = {}
        for half in range(2):
            par = half
            apd = [[D, 128], [128 * D, 8], [1, 512]]
            if t == 0:
                srcap = bass.AP(b.Wo_h, half * 512, apd)
            else:
                srcap = bass.AP(b.wq_s, 3 * D * D + half * 512, apd)
            dstv = b.wqb[:, par * 4096:(par + 1) * 4096].rearrange("p (c n) -> p c n", c=8)
            e_d = P.dma("pool", "wq%d" % par, dstv, srcap,
                        after=_f([st["wq_free"][par],
                                  st["wqw"][par]]))
            if t == 0:
                st["wqw"][par] = P.dma(
                    "sync", "ww%d" % par, bass.AP(b.wq_s, 3 * D * D + half * 512, apd),
                    dstv, after=[e_d])
            last = e_d
            for m in range(8):
                last = P.op("dve", lambda v, m=m, par=par: v.tensor_scalar(
                    b.wqb[:, par * 4096 + m * 512:par * 4096 + (m + 1) * 512],
                    b.wqb[:, par * 4096 + m * 512:par * 4096 + (m + 1) * 512],
                    b.normw[:, m % 2:m % 2 + 1], None, ALU.mult),
                    after=_f([e_d, st["wqw"][par]]) if m == 0 else ())
            wo_ev[half] = last
        mm = None
        st["s_mm_bank"] = [None] * 8
        for tc in range(TC):
            for half in range(2):
                bki = tc * 2 + half
                for m in range(8):
                    aft = []
                    if m == 0:
                        aft = _f([wo_ev[half], st["bank"][bki], st.get("ofm_ev")])
                    mm = P.op("tensor", lambda pe, m=m, tc=tc, half=half, bki=bki: pe.matmul(
                        self.bank(bki), b.o_fm[:, m * 512 + tc * 128:m * 512 + tc * 128 + 128],
                        b.wqb[:, half * 4096 + m * 512:half * 4096 + (m + 1) * 512],
                        start=(m == 0), stop=(m == 7)),
                        after=aft, inc=(m == 7))
                st["s_mm_bank"][bki] = mm
        st["wq_free"][0] = mm
        st["wq_free"][1] = mm
        st["s_mm"] = mm

    def phase_combine(self, P, t, ev_g, st):
        b = self
        last = None
        for tc in range(TC):
            stage = b.xn_scr if tc % 2 == 0 else b.sq_scr
            e_gs = P.op("act", lambda a, tc=tc: a.activation(
                b.sf32[:, 0:1024], b.g_acc[:, tc * 1024:(tc + 1) * 1024], AF.Silu),
                after=_f([ev_g, st.get("acc_rd"), st.get("gs_rd")]))
            e_o = None
            for half in range(2):
                bki = tc * 2 + half
                sl = stage[:, half * 512:(half + 1) * 512]
                e_m = P.op("dve", lambda v, sl=sl, bki=bki, half=half: v.tensor_tensor(
                    sl, self.bank(bki), b.sf32[:, half * 512:(half + 1) * 512], ALU.mult),
                    after=_f([e_gs, st.get("s_mm_bank", [None] * 8)[bki] or st["s_mm"],
                              st["stage_rd"][tc % 2],
                              st.get("xn_scr_rd") if tc % 2 == 0 else None]))
                e_o = P.op("dve", lambda v, sl=sl, tc=tc, half=half: v.tensor_tensor(
                    sl, sl, b.x_t[:, tc * 1024 + half * 512:tc * 1024 + half * 512 + 512],
                    ALU.add), after=[e_m])
                st["bank"][bki] = e_o
            st["gs_rd"] = e_o
            dst = bass.AP(b.y_h, (t * T + tc * 128) * D, [[D, 128], [1, D]])
            e_dma = P.dma("sync", "outa" if tc % 2 == 0 else "outb",
                          dst, stage[:, 0:1024], after=[e_o])
            st["out"].append(e_dma)
            st["stage_rd"][tc % 2] = e_dma
            last = e_o
        return last


def build_nc():
    nc = bass.Bass("TRN2", target_bir_lowering=False)
    b = Builder(nc)
    with b.stack, nc.Block() as block:
        block.sync(lambda eng: b.walk(eng, "sync"))
        block.scalar(lambda eng: b.walk(eng, "act"))
        block.vector(lambda eng: b.walk(eng, "dve"))
        block.tensor(lambda eng: b.walk(eng, "tensor"))
        block.gpsimd(lambda eng: b.walk(eng, "pool"))
    return nc, b


def kernel(**inputs):
    x = np.ascontiguousarray(inputs["L0_attn_hidden_states"])
    nm = {"L0_attn_Wq": "Wq", "L0_attn_Wk": "Wk", "L0_attn_Wv": "Wv",
          "L0_attn_conv_q": "conv_q", "L0_attn_conv_k": "conv_k",
          "L0_attn_conv_v": "conv_v", "L0_attn_Wb": "Wb", "L0_attn_Wg": "Wg",
          "L0_attn_bg": "bg", "L0_attn_Wh": "Wh", "L0_attn_bh": "bh",
          "L0_attn_norm_w": "norm_w", "L0_attn_Wo": "Wo",
          "L0_ffn_router_w": "router_w", "L0_ffn_up_w": "up_w",
          "L0_ffn_down_w": "down_w"}
    weights = {v: np.ascontiguousarray(inputs[k]) for k, v in nm.items()}
    nc, b = build_nc()
    in_maps = []
    for core in range(8):
        m = {"x": x[core]}
        m.update(weights)
        in_maps.append(m)
    res = run_bass_kernel_spmd(nc, in_maps, core_ids=list(range(8)))
    return np.stack([r["y"] for r in res.results], axis=0)
